# version 18
# baseline (speedup 1.0000x reference)
import numpy as np
import concourse.bass as bass
import concourse.mybir as mybir

F32 = mybir.dt.float32
BF16 = mybir.dt.bfloat16
AF = mybir.ActivationFunctionType
ALU = mybir.AluOpType
AX = mybir.AxisListType

TLOC = 1024
NT = TLOC // 128
D = 2048
ND = D // 128
PAIRS = [[0, 1], [2, 3], [4, 5], [6, 7]]

PEND = object()
BARRIER = object()


class Op:
    __slots__ = ("eng", "fn", "waits", "inc", "tag")

    def __init__(self, eng, fn, waits, inc, tag):
        self.eng, self.fn, self.waits, self.inc, self.tag = eng, fn, waits, inc, tag


class Prog:
    def __init__(self, nc):
        self.nc = nc
        self.ops = []
        self.sem_counts = {}
        self.sem_step = {}
        self.sem_mode = {}
        self.observed = {e: {} for e in ("T", "V", "S", "G", "Y")}
        self.W = {}
        self.R = {}
        self.sem_names = []
        self._engine_sem = {"T": "sT", "V": "sV", "S": "sS", "G": "sG", "Y": "sY"}
        self._snap = {}
        self._pendT = []
        for s in ("sT", "sV", "sS", "sG", "sY"):
            self._declare_sem(s, 1)

    def _declare_sem(self, name, step, mode="barrier"):
        if name not in self.sem_counts:
            self.sem_counts[name] = 0
            self.sem_step[name] = step
            self.sem_names.append(name)
            self._snap[name] = {}
            self.sem_mode[name] = mode

    def dma_stream(self, name, mode="barrier"):
        self._declare_sem(name, 16, mode)

    def _need(self, eng, dep, waits, tag=""):
        sem, val = dep
        if val is PEND:
            if eng == "T":
                return
            raise RuntimeError(
                f"op {tag}: engine {eng} depends on un-inc'd T chain; "
                f"give the chain's last matmul inc=True")
        if val is BARRIER:
            val = self.sem_counts[sem]
        if eng == "T" and sem == "sT":
            return
        if self.observed[eng].get(sem, 0) >= val:
            return
        snap = self._snap[sem].get(val)
        if snap:
            for s2, v2 in snap.items():
                if self.observed[eng].get(s2, 0) < v2:
                    self.observed[eng][s2] = v2
        self.observed[eng][sem] = val
        waits.append((sem, val))

    def _buf_names(self, bufs):
        out = []
        for b in bufs:
            if b is None:
                continue
            if isinstance(b, str):
                out.append(b)
            else:
                t = getattr(b, "tensor", b)
                out.append(t.name if hasattr(t, "name") else str(t))
        return out

    def op(self, eng, fn, reads=(), writes=(), inc=True, sem=None, tag="",
           war=True):
        waits = []
        rnames = self._buf_names(reads)
        wnames = self._buf_names(writes)
        for n in rnames:
            if n in self.W:
                self._need(eng, self.W[n], waits, tag=tag + "/r:" + n)
        for n in wnames:
            if n in self.W:
                self._need(eng, self.W[n], waits, tag=tag + "/w:" + n)
            if war:
                for r in self.R.get(n, ()):
                    self._need(eng, r, waits, tag=tag + "/war:" + n)
        sem_name = sem if sem is not None else self._engine_sem[eng]
        assert sem_name in self.sem_counts, f"undeclared stream {sem_name}"
        is_dma_stream = sem is not None
        if inc:
            self.sem_counts[sem_name] += self.sem_step[sem_name]
            val = self.sem_counts[sem_name]
            inc_rec = (sem_name, self.sem_step[sem_name])
            snap = dict(self.observed[eng])
            snap[sem_name] = val
            self._snap[sem_name][val] = snap
            if is_dma_stream and self.sem_mode[sem_name] == "barrier":
                done = (sem_name, BARRIER)
            else:
                done = (sem_name, val)
            if eng == "T" and sem_name == "sT":
                for cell, key in self._pendT:
                    cur = cell[key]
                    if isinstance(cur, tuple) and cur[1] is PEND:
                        cell[key] = done
                self._pendT = []
        else:
            assert eng in ("T", "Y") and sem is None, "inc=False only for T chains / final nop"
            inc_rec = None
            done = (sem_name, PEND)
        for n in wnames:
            self.W[n] = done
            self.R[n] = []
            if done[1] is PEND:
                self._pendT.append((self.W, n))
        for n in rnames:
            lst = self.R.setdefault(n, [])
            lst.append(done)
            if done[1] is PEND:
                self._pendT.append((lst, len(lst) - 1))
        self.ops.append(Op(eng, fn, waits, inc_rec, tag))
        return done

    def emit(self, block, semobjs):
        streams = {e: [] for e in ("T", "V", "S", "G", "Y")}
        for o in self.ops:
            streams[o.eng].append(o)

        def run(engine_obj, ops):
            for o in ops:
                for sem, val in o.waits:
                    engine_obj.wait_ge(semobjs[sem], val)
                ins = o.fn(engine_obj)
                if o.inc is not None:
                    assert ins is not None, f"op {o.tag} returned no instruction"
                    ins.then_inc(semobjs[o.inc[0]], o.inc[1])

        if streams["T"]:
            @block.tensor
            def _(t):
                run(t, streams["T"])
        if streams["V"]:
            @block.vector
            def _(v):
                run(v, streams["V"])
        if streams["S"]:
            @block.scalar
            def _(s):
                run(s, streams["S"])
        if streams["G"]:
            @block.gpsimd
            def _(g):
                run(g, streams["G"])
        if streams["Y"]:
            @block.sync
            def _(y):
                run(y, streams["Y"])


class Builder:
    def __init__(self, nc):
        self.nc = nc
        self.p = Prog(nc)
        self.sb = {}
        self.psd = {}
        self.dram = {}
        self.input_names = []

    def sbuf(self, name, shape, dtype):
        h = self.nc.alloc_sbuf_tensor(name, list(shape), dtype)
        self.sb[name] = h
        return h

    def psum(self, name, shape, dtype=F32):
        h = self.nc.alloc_psum_tensor(name, list(shape), dtype)
        self.psd[name] = h
        return h

    def din(self, name, shape, dtype=F32):
        h = self.nc.dram_tensor(name, list(shape), dtype, kind="ExternalInput")
        self.dram[name] = h
        self.input_names.append(name)
        return h

    def dout(self, name, shape, dtype=F32):
        h = self.nc.dram_tensor(name, list(shape), dtype, kind="ExternalOutput")
        self.dram[name] = h
        return h

    def dint(self, name, shape, dtype=F32):
        h = self.nc.dram_tensor(name, list(shape), dtype)
        self.dram[name] = h
        return h

    def dma(self, eng, out, in_, stream, reads=(), writes=(), tag="dma", **kw):
        return self.p.op(
            eng, lambda e: e.dma_start(out=out, in_=in_, **kw),
            reads=reads, writes=writes, sem=stream, tag=tag)

    def mm(self, out, lhsT, rhs, start, stop, reads, writes, inc=None,
           tag="mm", tile_position=None, war=True):
        if inc is None:
            inc = stop
        return self.p.op(
            "T",
            lambda e: e.matmul(out, lhsT, rhs, start=start, stop=stop,
                               tile_position=tile_position),
            reads=reads, writes=writes, inc=inc, tag=tag, war=war)

    def tr(self, out, in_, ident, reads, writes, inc=True, tag="tr"):
        return self.p.op(
            "T", lambda e: e.transpose(out, in_, ident),
            reads=reads, writes=writes, inc=inc, tag=tag)

    def v(self, fn, reads, writes, inc=True, tag="v"):
        return self.p.op("V", fn, reads=reads, writes=writes, inc=inc, tag=tag)

    def s(self, fn, reads, writes, inc=True, tag="s"):
        return self.p.op("S", fn, reads=reads, writes=writes, inc=inc, tag=tag)

    def g(self, fn, reads, writes, inc=True, tag="g", sem=None):
        return self.p.op("G", fn, reads=reads, writes=writes, inc=inc, tag=tag,
                         sem=sem)



def tile_cols(W, cw):
    Din, O = W.shape
    a = W.reshape(Din // 128, 128, O // cw, cw)
    return np.ascontiguousarray(a.transpose(2, 1, 0, 3))


def tile_cols_halfk(W, cw):
    Din, O = W.shape
    a = W.reshape(2, Din // 256, 128, O // cw, cw)
    return np.ascontiguousarray(a.transpose(3, 0, 2, 1, 4))


def col_layout(w):
    return np.ascontiguousarray(w.reshape(-1, 128).T)


def host_prep_l0(ins):
    import ml_dtypes
    P = {}
    P["WqT"] = tile_cols(np.ascontiguousarray(ins["L0_attn_Wq"].T), 128)
    P["WkT4"] = tile_cols_halfk(np.ascontiguousarray(ins["L0_attn_Wk"].T), 512)
    P["WvT4"] = tile_cols_halfk(np.ascontiguousarray(ins["L0_attn_Wv"].T), 512)
    P["WoT4"] = tile_cols_halfk(np.ascontiguousarray(ins["L0_attn_Wo"].T), 512)
    P["inprojT"] = np.ascontiguousarray(ins["L0_ffn_in_proj_w"].T)
    P["inprojQ"] = tile_cols(P["inprojT"][:, 0:2048], 128)
    P["outpT4"] = tile_cols_halfk(np.ascontiguousarray(ins["L0_ffn_out_proj_w"].T), 512)
    P["writeT"] = np.ascontiguousarray(ins["L0_ffn_write_w"].T)
    P["outpT"] = np.ascontiguousarray(ins["L0_ffn_out_proj_w"].T)
    P["competeT"] = np.ascontiguousarray(ins["L0_ffn_compete_w"].T)
    P["prew_col"] = col_layout(ins["L0_ffn_norm_pre_w"])
    P["postw_col"] = col_layout(ins["L0_ffn_norm_post_w"])
    P["postw_row"] = np.ascontiguousarray(ins["L0_ffn_norm_post_w"][None, :])
    P["inprojb_col"] = col_layout(ins["L0_ffn_in_proj_b"])
    P["competeb_col"] = np.ascontiguousarray(ins["L0_ffn_compete_b"][:, None])
    P["writeb_row"] = np.ascontiguousarray(ins["L0_ffn_write_b"][None, :])
    P["outpb_row"] = np.ascontiguousarray(ins["L0_ffn_out_proj_b"][None, :])
    P["workspace"] = np.ascontiguousarray(ins["L0_ffn_workspace"])
    P["identb"] = np.eye(128, dtype=ml_dtypes.bfloat16)
    P["ones_row"] = np.ones((1, 2048), dtype=ml_dtypes.bfloat16)
    return P


def shard_inputs(x_full):
    shards = []
    for c in range(8):
        b, h = c // 2, c % 2
        shards.append(np.ascontiguousarray(x_full[b, h * 1024:(h + 1) * 1024, :]))
    return shards



def declare_l0(b: Builder):
    d = {}
    d["x"] = b.din("x", (TLOC, D))
    for n, sh in [("WqT", (16, 128, 16, 128)), ("WkT4", (4, 2, 128, 8, 512)),
                  ("WvT4", (4, 2, 128, 8, 512)), ("WoT4", (4, 2, 128, 8, 512)),
                  ("inprojT", (D, 3 * D)), ("inprojQ", (16, 128, 16, 128)),
                  ("writeT", (D, D)),
                  ("outpT4", (4, 2, 128, 8, 512)), ("competeT", (D, 16))]:
        d[n] = b.din(n, sh)
    for n, sh in [("prew_col", (128, 16)), ("postw_col", (128, 16)),
                  ("postw_row", (1, D)), ("inprojb_col", (128, 48)),
                  ("competeb_col", (16, 1)), ("writeb_row", (1, D)),
                  ("outpb_row", (1, D)), ("workspace", (16, D))]:
        d[n] = b.din(n, sh)
    d["identb"] = b.din("identb", (128, 128), BF16)
    d["ones_row"] = b.din("ones_row", (1, D), BF16)
    d["ccA_i"] = b.dint("ccA_i", (128, 2064))
    d["ccA_o"] = b.dint("ccA_o", (128, 2064))
    d["ccT_i"] = b.dint("ccT_i", (16, 4))
    d["ccT_o"] = b.dint("ccT_o", (32, 4))
    d["ccU_i"] = b.dint("ccU_i", (16, D), BF16)
    d["ccU_o"] = b.dint("ccU_o", (16, D), BF16)
    d["ccZ_i"] = b.dint("ccZ_i", (16, 1))
    d["ccZ_o"] = b.dint("ccZ_o", (16, 1))
    d["zscr"] = b.dint("zscr", (16, TLOC))
    return d


def alloc_common(b: Builder):
    t = {}
    t["ytm"] = b.sbuf("ytm", (128, NT * D), F32)
    t["yT"] = b.sbuf("yT", (128, ND * TLOC), BF16)
    t["big1"] = b.sbuf("big1", (128, ND * TLOC), BF16)
    t["wA0"] = b.sbuf("wA0", (128, 2048), BF16)
    t["wA1"] = b.sbuf("wA1", (128, 2048), BF16)
    t["wB0"] = b.sbuf("wB0", (128, 2048), BF16)
    t["wB1"] = b.sbuf("wB1", (128, 2048), BF16)
    t["wC0"] = b.sbuf("wC0", (128, 4096), BF16)
    t["wC1"] = b.sbuf("wC1", (128, 4096), BF16)
    t["phik0"] = b.sbuf("phik0", (128, NT * 128), BF16)
    t["phik1"] = b.sbuf("phik1", (128, NT * 128), BF16)
    t["vext0"] = b.sbuf("vext0", (128, NT * 129), BF16)
    t["vext1"] = b.sbuf("vext1", (128, NT * 129), BF16)
    t["phiq0"] = t["phik0"]
    t["phiq1"] = t["phik1"]
    t["zb"] = t["vext0"]
    t["zmask"] = b.sbuf("zmask", (128, 256), BF16)
    t["kv_gb"] = b.sbuf("kv_gb", (128, 2064), BF16)
    t["kv_loc"] = b.sbuf("kv_loc", (128, 2064), F32)
    t["z_sb"] = b.sbuf("z_sb", (16, TLOC), F32)
    t["z_row"] = b.sbuf("z_row", (1, TLOC), BF16)
    t["xrot"] = b.sbuf("xrot", (128, D), F32)
    t["scrA"] = b.sbuf("scrA", (128, D), F32)
    for n in ("c_ssq", "c_nm", "c_var", "c_rstd", "c_tmp"):
        t[n] = b.sbuf(n, (128, 16), F32)
    t["c_eps5"] = b.sbuf("c_eps5", (128, 1), F32)
    t["c_eps6"] = b.sbuf("c_eps6", (128, 1), F32)
    t["identb"] = b.sbuf("identb_sb", (128, 128), BF16)
    t["ones"] = b.sbuf("ones_sb", (1, 128), BF16)
    return t


def load_consts(b, d, t):
    b.dma("Y", t["identb"][:, :], d["identb"][:, :], "dmisc", writes=["identb_sb"])
    b.dma("G", t["ones"][:, :], d["ones_row"][0:1, 0:128], "dmisc2", writes=["ones_sb"])
    b.v(lambda e: e.memset(t["c_eps5"][:, :], 1e-5), reads=[], writes=["c_eps5"])
    b.v(lambda e: e.memset(t["c_eps6"][:, :], 1e-6), reads=[], writes=["c_eps6"])


def emit_ln(b, d, t, src_dram, dst_ytm, eps=1e-5, tag="ln", src_dep=()):
    for ti in range(NT):
        xr = t["xrot"][:, 0:D]
        xname = "xrot"
        b.dma("Y", xr, src_dram[ti * 128:(ti + 1) * 128, :], "dx",
              writes=[xname], reads=list(src_dep), tag=f"{tag}_ld{ti}")
        sq = t["scrA"]
        b.s(lambda e, xr=xr, sq=sq, ti=ti: e.activation(
            sq[:, :], xr, AF.Square, accum_out=t["c_ssq"][:, ti:ti + 1]),
            reads=[xname], writes=["scrA", "c_ssq"], tag=f"{tag}_sq{ti}")
        b.v(lambda e, xr=xr, ti=ti: e.reduce_sum(
            t["c_nm"][:, ti:ti + 1], xr, AX.X),
            reads=[xname], writes=["c_nm"], tag=f"{tag}_sum{ti}")
        b.v(lambda e, ti=ti: e.tensor_scalar(
            t["c_nm"][:, ti:ti + 1], t["c_nm"][:, ti:ti + 1], -1.0 / D, None,
            ALU.mult), reads=["c_nm"], writes=["c_nm"], tag=f"{tag}_nm{ti}")
        b.v(lambda e, ti=ti: e.tensor_mul(
            t["c_tmp"][:, ti:ti + 1], t["c_nm"][:, ti:ti + 1],
            t["c_nm"][:, ti:ti + 1]),
            reads=["c_nm"], writes=["c_tmp"], tag=f"{tag}_msq{ti}")
        b.v(lambda e, ti=ti: e.tensor_scalar(
            t["c_var"][:, ti:ti + 1], t["c_ssq"][:, ti:ti + 1], 1.0 / D,
            t["c_tmp"][:, ti:ti + 1], ALU.mult, ALU.subtract),
            reads=["c_ssq", "c_tmp"], writes=["c_var"], tag=f"{tag}_var{ti}")
        b.s(lambda e, ti=ti: e.activation(
            t["c_var"][:, ti:ti + 1], t["c_var"][:, ti:ti + 1], AF.Sqrt,
            bias=t["c_eps5"][:, 0:1]), reads=["c_var", "c_eps5"],
            writes=["c_var"], tag=f"{tag}_sd{ti}")
        b.v(lambda e, ti=ti: e.reciprocal(
            t["c_rstd"][:, ti:ti + 1], t["c_var"][:, ti:ti + 1]),
            reads=["c_var"], writes=["c_rstd"], tag=f"{tag}_rs{ti}")
        dst = dst_ytm[:, ti * D:(ti + 1) * D]
        b.v(lambda e, xr=xr, dst=dst, ti=ti: e.tensor_scalar(
            dst, xr, t["c_nm"][:, ti:ti + 1], t["c_rstd"][:, ti:ti + 1],
            ALU.add, ALU.mult),
            reads=[xname, "c_nm", "c_rstd"], writes=["ytm"],
            tag=f"{tag}_ap{ti}")


def emit_transpose_tm_fm(b, t, src, dst, psW, tag="tp"):
    for dj in range(ND):
        for ti in range(NT):
            b.tr(psW[:, ti * 128:(ti + 1) * 128],
                 src[:, ti * D + dj * 128: ti * D + dj * 128 + 128],
                 t["identb"][:, :],
                 reads=[src, "identb_sb"], writes=["psW"], inc=(ti == NT - 1),
                 tag=f"{tag}_t{dj}_{ti}")
        b.v(lambda e, dj=dj: e.tensor_copy(
            dst[:, dj * TLOC:(dj + 1) * TLOC], psW[:, :]),
            reads=["psW"], writes=[dst], tag=f"{tag}_ev{dj}")


def emit_phi_evac(b, t, ps, psname, dst, dstname, ncols, bias_col=None,
                  tag="phi"):
    scr1 = t["scrA"][:, :ncols]
    scr2 = t["scrA"][:, 512:512 + ncols]
    if bias_col is None:
        b.s(lambda e: e.activation(scr1, ps, AF.Relu, scale=-1.0),
            reads=[psname], writes=["scrA"], tag=f"{tag}_r")
    else:
        b.s(lambda e: e.activation(scr1, ps, AF.Relu, scale=-1.0,
                                   bias=bias_col[1]),
            reads=[psname, bias_col[2]], writes=["scrA"], tag=f"{tag}_r")
    b.s(lambda e: e.activation(scr2, scr1, AF.Exp, scale=-1.0),
        reads=["scrA"], writes=["scrA"], tag=f"{tag}_e")
    if bias_col is None:
        b.v(lambda e: e.scalar_tensor_tensor(dst, ps, 0.0, scr2, ALU.max,
                                             ALU.add),
            reads=[psname, "scrA"], writes=[dstname], tag=f"{tag}_c")
    else:
        b.v(lambda e: e.tensor_scalar(scr1, ps, bias_col[0], 0.0, ALU.add,
                                      ALU.max),
            reads=[psname, bias_col[2]], writes=["scrA"], tag=f"{tag}_m")
        b.v(lambda e: e.tensor_add(dst, scr1, scr2),
            reads=["scrA"], writes=[dstname], tag=f"{tag}_c")




BANKS8 = None


def psum_banks8(ps):
    return [
        (ps["A0"][:, 0:512], "psA0"), (ps["A1"][:, 0:512], "psA1"),
        (ps["A2"][:, 0:512], "psA2"), (ps["A3"][:, 0:512], "psA3"),
        (ps["B"][:, 0:512], "psB"), (ps["B"][:, 512:1024], "psB"),
        (ps["K"][:, 0:512], "psK"),
        (ps["W"][:, :].bitcast(F32)[:, 0:512], "psW"),
    ]


def emit_colblock_out(b, t, ps, wdram4, lhsT_fn, lhs_deps, bias_dram_row,
                      evac_fn, tag):
    banks = psum_banks8(ps)
    for cb in range(4):
        if bias_dram_row is not None and cb % 2 == 0:
            b.dma("G", t["z_row"][0:1, :],
                  bias_dram_row[0:1, cb * 512:cb * 512 + 1024], "drow",
                  reads=[], writes=["z_row"], tag=f"{tag}_br{cb}")
        for kh in range(2):
            sl = (2 * cb + kh) % 2
            b.dma("G", bass.AP(t[f"wC{sl}"], 0, [[4096, 128], [512, 8], [1, 512]]),
                  wdram4[cb, kh], f"wo{sl}", writes=[f"wC{sl}"],
                  tag=f"{tag}_w{cb}_{kh}")
            for ti in range(NT):
                pa, pname = banks[ti]
                for kd in range(8):
                    last = (bias_dram_row is None and kh == 1 and kd == 7)
                    b.mm(pa, lhsT_fn(kh * 8 + kd, ti),
                         t[f"wC{sl}"][:, kd * 512:(kd + 1) * 512],
                         start=(kh == 0 and kd == 0), stop=last,
                         inc=(kd == 7),
                         reads=list(lhs_deps) + [f"wC{sl}"], writes=[pname],
                         tag=f"{tag}_m{cb}_{kh}_{ti}")
        for ti in range(NT):
            pa, pname = banks[ti]
            if bias_dram_row is not None:
                b.mm(pa, t["ones"][0:1, 0:128],
                     t["z_row"][0:1, (cb % 2) * 512:(cb % 2) * 512 + 512],
                     start=False, stop=True,
                     reads=["ones_sb", "z_row"], writes=[pname],
                     tag=f"{tag}_b{cb}_{ti}")
            evac_fn(cb, ti, pa, pname)


def emit_l0_attn(b, d, t, ps):
    emit_l0_attn_A(b, d, t, ps)
    emit_l0_attn_BC(b, d, t, ps)


def emit_l0_attn_A(b, d, t, ps):
    p = b.p
    for st in ("wk0", "wk1", "wv0", "wv1", "wq0", "wq1", "wo0", "wo1"):
        p.dma_stream(st, mode="slot")
    p.dma_stream("dcc", mode="barrier")
    for sl in range(2):
        vext = t[f"vext{sl}"]
        b.v(lambda e, vext=vext: e.memset(
            bass.AP(vext, 128, [[vext.shape[1], 128], [129, NT]]), 1.0),
            reads=[], writes=[f"vext{sl}"], tag=f"vones{sl}")

    pk = t["big1"]
    vg_off = 4096
    for ti in range(NT):
        b.v(lambda e, ti=ti: e.memset(
            bass.AP(t["big1"], vg_off + ti * 516 + 128,
                    [[ND * TLOC, 128], [129, 4]]), 1.0),
            reads=[], writes=["big1"], tag=f"vone{ti}")
    for g in range(4):
        for part, wnm in ((0, "WkT4"), (1, "WvT4")):
            for kh in range(2):
                sl = kh
                wc = t[f"wC{sl}"]
                b.dma("G", bass.AP(wc, 0, [[4096, 128], [512, 8], [1, 512]]),
                      d[wnm][g, kh], f"wo{sl}", writes=[f"wC{sl}"],
                      tag=f"ldkv{g}_{part}_{kh}")
            for ti in range(NT):
                pa = ps[f"A{ti % 4}"]
                for kh in range(2):
                    for kd in range(8):
                        b.mm(pa[:, 0:512],
                             t["yT"][:, (kh * 8 + kd) * TLOC + ti * 128:
                                     (kh * 8 + kd) * TLOC + ti * 128 + 128],
                             t[f"wC{kh}"][:, kd * 512:(kd + 1) * 512],
                             start=(kh == 0 and kd == 0),
                             stop=(kh == 1 and kd == 7),
                             reads=["yT", f"wC{kh}"], writes=[f"psA{ti % 4}"],
                             tag=f"kv{g}_{part}_{ti}")
                if part == 0:
                    emit_phi_evac(b, t, pa[:, 0:512], f"psA{ti % 4}",
                                  pk[:, ti * 512:(ti + 1) * 512], "big1", 512,
                                  tag=f"phk{g}_{ti}")
                else:
                    b.v(lambda e, ti=ti, pa=pa: e.tensor_copy(
                        bass.AP(t["big1"], vg_off + ti * 516,
                                [[ND * TLOC, 128], [129, 4], [1, 128]]),
                        bass.AP(pa, 0, [[512, 128], [128, 4], [1, 128]])),
                        reads=[f"psA{ti % 4}"], writes=["big1"],
                        tag=f"vev{g}_{ti}")
        kvbanks = [(ps["K"][:, 0:129], "psK"), (ps["B"][:, 0:129], "psB"),
                   (ps["B"][:, 512:641], "psB")]
        for hl in range(4):
            h = g * 4 + hl
            pa, pan = kvbanks[hl % 3]
            for ti in range(NT):
                b.mm(pa,
                     pk[:, ti * 512 + hl * 128: ti * 512 + hl * 128 + 128],
                     bass.AP(t["big1"], vg_off + ti * 516 + hl * 129,
                             [[ND * TLOC, 128], [1, 129]]),
                     start=(ti == 0), stop=(ti == NT - 1),
                     reads=["big1"], writes=[pan],
                     tag=f"kvm{h}_{ti}")
            b.v(lambda e, h=h, pa=pa: e.tensor_copy(
                t["kv_loc"][:, h * 129:(h + 1) * 129], pa),
                reads=[pan], writes=["kv_loc"], tag=f"kvev{h}")

    b.dma("Y", d["ccA_i"][:, :], t["kv_loc"][:, :], "dsA", reads=["kv_loc"],
          writes=["ccA_i"], tag="kv_out")
    p._declare_sem("cc0", 1)
    b.g(lambda e: e.collective_compute(
        "AllReduce", ALU.add, replica_groups=PAIRS,
        ins=[d["ccA_i"].ap().opt()], outs=[d["ccA_o"].ap().opt()]),
        reads=["ccA_i"], writes=["ccA_o"], sem="cc0", tag="cc_kv")


def emit_l0_attn_BC(b, d, t, ps):
    b.dma("G", t["kv_gb"][:, :], d["ccA_o"][:, :], "dlA", reads=["ccA_o"],
          writes=["kv_gb"], tag="kv_in")
    b.g(lambda e: e.memset(t["zmask"][:, :], 0.0), reads=[], writes=["zmask"],
        tag="zm0")
    for h in range(16):
        b.v(lambda e, h=h: e.tensor_copy(
            t["zmask"][:, h * 16 + h: h * 16 + h + 1],
            t["kv_gb"][:, h * 129 + 128: h * 129 + 129]),
            reads=["kv_gb", "zmask"], writes=["zmask"], tag=f"zm{h}")

    for h in range(16):
        sl = h % 2
        wq = t[f"wA{sl}"]
        phiq = t[f"phiq{sl}"]
        b.dma("G", bass.AP(wq, 0, [[2048, 128], [128, 16], [1, 128]]),
              d["WqT"][h], f"wq{sl}",
              writes=[f"wA{sl}"], tag=f"ldwq{h}")
        for c in range(2):
            for kd in range(ND):
                b.mm(ps["A0"][:, :512],
                     wq[:, kd * 128:(kd + 1) * 128],
                     t["yT"][:, kd * TLOC + c * 512: kd * TLOC + c * 512 + 512],
                     start=(kd == 0), stop=(kd == ND - 1),
                     reads=["yT", f"wA{sl}"], writes=["psA0"],
                     tag=f"qproj{h}_{c}")
            emit_phi_evac(b, t, ps["A0"][:, :512], "psA0",
                          phiq[:, c * 512:(c + 1) * 512], f"phik{sl}", 512,
                          tag=f"phiq{h}_{c}")
        for c in range(2):
            b.mm(ps["B"][0:16, c * 512:(c + 1) * 512],
                 t["zmask"][:, h * 16:(h + 1) * 16],
                 phiq[:, c * 512:(c + 1) * 512],
                 start=(h == 0), stop=(h == 15),
                 reads=["zmask", f"phik{sl}"], writes=["psB"],
                 tag=f"zmm{h}_{c}")
        for c in range(2):
            for _ in range(1):
                b.mm(ps["A1"][:, :512], t["kv_gb"][:, h * 129: h * 129 + 128],
                     phiq[:, c * 512:(c + 1) * 512], start=True, stop=True,
                     reads=["kv_gb", f"phik{sl}"], writes=["psA1"],
                     tag=f"pre{h}_{c}")
            b.v(lambda e, h=h, c=c: e.tensor_copy(
                t["big1"][:, h * TLOC + c * 512: h * TLOC + (c + 1) * 512],
                ps["A1"][:, :512]),
                reads=["psA1"], writes=["big1"], tag=f"preev{h}_{c}")
    b.v(lambda e: e.tensor_scalar(t["z_sb"][:, :], ps["B"][0:16, :], 1e-6,
                                  None, ALU.add),
        reads=["psB"], writes=["z_sb"], tag="zadd")
    b.v(lambda e: e.reciprocal(t["z_sb"][:, :], t["z_sb"][:, :]),
        reads=["z_sb"], writes=["z_sb"], tag="zrec")
    b.dma("Y", d["zscr"][:, :], t["z_sb"][:, :], "dsz", reads=["z_sb"],
          writes=["zscr"], tag="zrow_o")
    for h in range(16):
        b.dma("G", t["z_row"][0:1, :], d["zscr"][h:h + 1, :], "dcc",
              reads=["zscr"], writes=["z_row"], tag=f"zrow_i{h}")
        for c in range(2):
            b.mm(ps["B"][:, c * 512:(c + 1) * 512], t["ones"][0:1, 0:128],
                 t["z_row"][0:1, c * 512:(c + 1) * 512],
                 start=True, stop=True,
                 reads=["ones_sb", "z_row"], writes=["psB"], tag=f"zb{h}_{c}")
            b.v(lambda e, h=h, c=c: e.tensor_copy(
                t["zb"][:, c * 512:(c + 1) * 512], ps["B"][:, c * 512:(c + 1) * 512]),
                reads=["psB"], writes=["vext0"], tag=f"zbev{h}_{c}")
        b.v(lambda e, h=h: e.tensor_mul(
            t["big1"][:, h * TLOC:(h + 1) * TLOC],
            t["big1"][:, h * TLOC:(h + 1) * TLOC], t["zb"][:, 0:TLOC]),
            reads=["big1", "vext0"], writes=["big1"], tag=f"zap{h}")

    def lhs_c(kd, ti):
        return t["big1"][:, kd * TLOC + ti * 128: kd * TLOC + ti * 128 + 128]

    def evac_c(cb, ti, pa, pname):
        sb = t["phiq0"][:, (ti % 2) * 512:(ti % 2) * 512 + 512]
        b.v(lambda e, pa=pa, sb=sb: e.tensor_copy(sb, pa),
            reads=[pname], writes=["phik0"], tag=f"sev{cb}_{ti}")
        b.dma("Y", d["s_scr"][ti * 128:(ti + 1) * 128,
                              cb * 512:(cb + 1) * 512], sb, "dso",
              reads=["phik0"], writes=["s_scr"], tag=f"sst{cb}_{ti}")

    emit_colblock_out(b, t, ps, d["WoT4"], lhs_c, ["big1"], None, evac_c, "sC")


def alloc_psum(b: Builder):
    ps = {}
    ps["A0"] = b.psum("psA0", (128, 512), F32)
    ps["A1"] = b.psum("psA1", (128, 512), F32)
    ps["A2"] = b.psum("psA2", (128, 512), F32)
    ps["A3"] = b.psum("psA3", (128, 512), F32)
    ps["B"] = b.psum("psB", (128, 1024), F32)
    ps["K"] = b.psum("psK", (128, 512), F32)
    ps["W"] = b.psum("psW", (128, 1024), BF16)
    return ps


def finalize(b: Builder, out_waits):
    nc = b.nc
    from contextlib import ExitStack
    es = ExitStack()
    semobjs = {}
    with es:
        for name in b.p.sem_names:
            semobjs[name] = es.enter_context(nc.semaphore(name))
        block = es.enter_context(nc.Block())
        b.p.op("Y", lambda e: e.nop(), reads=list(out_waits), writes=(),
               inc=False, tag="final_wait") if False else None
        b.p.emit(block, semobjs)
    return nc


def build_stageB():
    nc = bass.Bass("TRN2", target_bir_lowering=False, debug=False)
    b = Builder(nc)
    d = declare_l0(b)
    so = b.dout("s_out", (TLOC, D), BF16)
    t = alloc_common(b)
    ps = alloc_psum(b)
    for st in ("dx", "dmisc", "dmisc2", "dout"):
        b.p.dma_stream(st)
    load_consts(b, d, t)
    emit_ln(b, d, t, d["x"], t["ytm"], tag="ln1")
    emit_transpose_tm_fm(b, t, t["ytm"], t["yT"], ps["W"], tag="tp1")
    emit_l0_attn(b, d, t, ps)
    for ti in range(NT):
        b.dma("Y", so[ti * 128:(ti + 1) * 128, :],
              t["sbuf_s"][:, ti * D:(ti + 1) * D], "dout",
              reads=["sbuf_s"], writes=["s_out"], tag=f"so{ti}")
    b.p.op("Y", lambda e: e.nop(), reads=["s_out"], writes=(), inc=False,
           tag="fin")
    return finalize(b, [])


def alloc_ffn(b: Builder, t):
    t["sm16a"] = b.sbuf("sm16a", (16, 1024), BF16)
    t["identf"] = b.sbuf("identf", (128, 128), F32)
    t["mcols"] = b.sbuf("mcols", (16, 16), F32)
    t["mg8"] = b.sbuf("mg8", (16, 8), F32)
    t["cptT"] = t["zmask"]
    t["ws"] = None
    t["pwb"] = None
    t["u_sm"] = None
    t["tkw"] = None
    t["logit_sm"] = None
    t["eT"] = None
    t["uT"] = b.sbuf("uT", (128, 256), BF16)
    t["cpth"] = b.sbuf("cpth", (128, 256), BF16)
    t["cptl"] = b.sbuf("cptl", (128, 256), BF16)
    t["cptF"] = None
    t["yloT"] = None
    t["wsT"] = t["uT"]


def emit_l0_ffn_iter(b, d, t, ps, it, dbg=None):
    emit_ffn_topk(b, d, t, ps, it, dbg)
    emit_ffn_u(b, d, t, ps, it, dbg)
    emit_ffn_q(b, d, t, ps, it)
    emit_ffn_written(b, d, t, ps, it, dbg)
    emit_ffn_scores(b, d, t, ps, it, dbg)
    emit_ffn_xb(b, d, t, ps, it, dbg)


def _ffn_env(b, d, t, ps, it):
    class E: pass
    env = E()
    env.ws = t["kv_loc"][0:16, 0:2048]
    env.u_sm = t["wC1"][0:16, 2048:4096]
    env.tkw = t["z_sb"][0:16, 0:1024]
    env.logit_sm = t["xrot"][0:16, 0:1024]
    env.e_bf = t["sm16a"]
    env.eT = t["phik0"][:, 0:128]
    env.att_s = t["vext1"]
    env.qT = t["phik1"]
    env.sA = t["scrA"]
    return env


def emit_ffn_topk(b, d, t, ps, it, dbg=None):
    p = b.p
    env = _ffn_env(b, d, t, ps, it)
    ws, u_sm, tkw, logit_sm = env.ws, env.u_sm, env.tkw, env.logit_sm
    e_bf, eT, att_s, qT, sA = env.e_bf, env.eT, env.att_s, env.qT, env.sA
    tks = t["xrot"][:, 1024:2048]

    if dbg:
        b.dma("Y", dbg["lg"].ap(), logit_sm[:, :], "dout", reads=["xrot"],
              writes=["dbg_lg"], tag="d_lg")
    b.v(lambda e: e.tensor_copy(tkw, logit_sm[:, :]),
        reads=["xrot"], writes=["z_sb"], tag=f"f{it}tk0")
    for i in range(4):
        b.v(lambda e, i=i: e.reduce_max(
            t["mcols"][:, i:i + 1], tkw, AX.X),
            reads=["z_sb"], writes=["mcols"], tag=f"f{it}tkm{i}")
        if i < 3:
            b.v(lambda e, i=i: e.tensor_scalar(
                tks[0:16, 0:1024], tkw, t["mcols"][:, i:i + 1], None,
                ALU.is_ge), reads=["z_sb", "mcols"], writes=["xrot"],
                tag=f"f{it}tke{i}")
            b.v(lambda e: e.scalar_tensor_tensor(
                tkw, tks[0:16, 0:1024], -1e30, tkw, ALU.mult, ALU.add),
                reads=["xrot", "z_sb"], writes=["z_sb"], tag=f"f{it}tkr{i}")
    b.dma("Y", d["ccT_i"][:, :], t["mcols"][:, 0:4], "dsT", reads=["mcols"],
          writes=["ccT_i"], tag=f"f{it}tko")
    p._declare_sem(f"ccT{it}", 1)
    b.g(lambda e: e.collective_compute(
        "AllGather", ALU.bypass, replica_groups=PAIRS,
        ins=[d["ccT_i"].ap().opt()], outs=[d["ccT_o"].ap().opt()]),
        reads=["ccT_i"], writes=["ccT_o"], sem=f"ccT{it}", tag=f"f{it}ccT")
    b.dma("G", bass.AP(t["mg8"], 0, [[8, 16], [4, 2], [1, 4]]),
          bass.AP(d["ccT_o"], 0, [[4, 16], [64, 2], [1, 4]]),
          "dmg", reads=["ccT_o"], writes=["mg8"], tag=f"f{it}tki")
    b.v(lambda e: e.tensor_copy(tks[0:16, 0:8], t["mg8"][:, :]),
        reads=["mg8"], writes=["xrot"], tag=f"f{it}mg0")
    for i in range(4):
        b.v(lambda e, i=i: e.reduce_max(
            t["mcols"][:, 8 + i: 9 + i], tks[0:16, 0:8], AX.X),
            reads=["xrot"], writes=["mcols"], tag=f"f{it}mgm{i}")
        if i < 3:
            b.v(lambda e, i=i: e.tensor_scalar(
                tks[0:16, 8:16], tks[0:16, 0:8], t["mcols"][:, 8 + i:9 + i],
                None, ALU.is_ge), reads=["xrot", "mcols"], writes=["xrot"],
                tag=f"f{it}mge{i}")
            b.v(lambda e: e.scalar_tensor_tensor(
                tks[0:16, 0:8], tks[0:16, 8:16], -1e30, tks[0:16, 0:8],
                ALU.mult, ALU.add),
                reads=["xrot"], writes=["xrot"], tag=f"f{it}mgr{i}")


def emit_ffn_u(b, d, t, ps, it, dbg=None):
    p = b.p
    env = _ffn_env(b, d, t, ps, it)
    ws, u_sm, tkw, logit_sm = env.ws, env.u_sm, env.tkw, env.logit_sm
    e_bf, eT, att_s, qT, sA = env.e_bf, env.eT, env.att_s, env.qT, env.sA

    b.v(lambda e: e.tensor_scalar(
        sA[0:16, 0:1024], logit_sm[:, :], t["mcols"][:, 8:9], None,
        ALU.subtract), reads=["xrot", "mcols"], writes=["scrA"],
        tag=f"f{it}es")
    b.s(lambda e: e.activation(tkw, sA[0:16, 0:1024], AF.Exp),
        reads=["scrA"], writes=["z_sb"], tag=f"f{it}ee")
    b.v(lambda e: e.tensor_scalar(
        sA[0:16, 0:1024], logit_sm[:, :], t["mcols"][:, 11:12], None,
        ALU.is_ge), reads=["xrot", "mcols"], writes=["scrA"], tag=f"f{it}em")
    b.v(lambda e: e.tensor_mul(tkw, tkw, sA[0:16, 0:1024]),
        reads=["z_sb", "scrA"], writes=["z_sb"], tag=f"f{it}emm")
    b.v(lambda e: e.tensor_copy(e_bf[:, :], tkw),
        reads=["z_sb"], writes=["sm16a"], tag=f"f{it}ec")
    b.v(lambda e: e.tensor_scalar(
        t["mcols"][:, 0:4], t["mcols"][:, 8:12], t["mcols"][:, 8:9], None,
        ALU.subtract), reads=["mcols"], writes=["mcols"], tag=f"f{it}z0")
    b.s(lambda e: e.activation(t["mcols"][:, 0:4], t["mcols"][:, 0:4],
                               AF.Exp),
        reads=["mcols"], writes=["mcols"], tag=f"f{it}z1")
    b.v(lambda e: e.reduce_sum(t["mcols"][:, 13:14], t["mcols"][:, 0:4],
                               AX.X),
        reads=["mcols"], writes=["mcols"], tag=f"f{it}z2")
    for ti in range(NT):
        b.tr(ps["W"][:, ti * 16:(ti + 1) * 16],
             e_bf[:, ti * 128:(ti + 1) * 128], t["identb"][0:16, 0:16],
             reads=["sm16a", "identb_sb"], writes=["psW"], inc=(ti == NT - 1),
             tag=f"f{it}etr{ti}")
    b.v(lambda e: e.tensor_copy(eT, ps["W"][:, 0:128]),
        reads=["psW"], writes=["phik0"], tag=f"f{it}etc")
    for c in range(4):
        for ti in range(NT):
            ybs = t["phik1"][:, (ti % 2) * 512:(ti % 2) * 512 + 512]
            b.v(lambda e, ybs=ybs, ti=ti, c=c: e.tensor_copy(
                ybs, t["ytm"][:, ti * D + c * 512: ti * D + (c + 1) * 512]),
                reads=["ytm"], writes=["phik1"], tag=f"f{it}ubc{c}_{ti}")
            b.mm(ps[f"A{c}"][0:16, 0:512], eT[:, ti * 16:(ti + 1) * 16],
                 ybs, start=(ti == 0), stop=(ti == NT - 1), inc=True,
                 reads=["phik0", "phik1"], writes=[f"psA{c}"],
                 tag=f"f{it}up{c}_{ti}")
        b.v(lambda e, c=c: e.tensor_copy(
            u_sm[:, c * 512:(c + 1) * 512], ps[f"A{c}"][0:16, 0:512]),
            reads=[f"psA{c}"], writes=["wC1"], tag=f"f{it}upe{c}")


def emit_ffn_written(b, d, t, ps, it, dbg=None):
    p = b.p
    env = _ffn_env(b, d, t, ps, it)
    ws, u_sm, tkw, logit_sm = env.ws, env.u_sm, env.tkw, env.logit_sm
    e_bf, eT, att_s, qT, sA = env.e_bf, env.eT, env.att_s, env.qT, env.sA
    if it == 0:
        b.g(lambda e: e.memset(t["vext0"][:, 0:512], 0.0), reads=[],
            writes=["vext0"], tag="kbd0")
        b.g(lambda e: e.memset(t["kv_gb"][0:32, 0:2048], 0.0), reads=[],
            writes=["kv_gb"], tag="vbd0")

    b.dma("Y", d["ccU_i"][:, :], u_sm, "dsU", reads=["wC1"], writes=["ccU_i"],
          tag=f"f{it}uo")
    p._declare_sem(f"ccU{it}", 1)
    b.g(lambda e: e.collective_compute(
        "AllReduce", ALU.add, replica_groups=PAIRS,
        ins=[d["ccU_i"].ap().opt()], outs=[d["ccU_o"].ap().opt()]),
        reads=["ccU_i"], writes=["ccU_o"], sem=f"ccU{it}",
        tag=f"f{it}ccU")
    b.dma("G", u_sm, d["ccU_o"][:, :], "dum", reads=["ccU_o"], writes=["wC1"],
          tag=f"f{it}ui")
    b.v(lambda e: e.reciprocal(t["mcols"][:, 14:15], t["mcols"][:, 13:14]),
        reads=["mcols"], writes=["mcols"], tag=f"f{it}zr")
    b.v(lambda e: e.tensor_scalar(
        u_sm, u_sm, t["mcols"][:, 14:15], None, ALU.mult),
        reads=["wC1", "mcols"], writes=["wC1"], tag=f"f{it}us")
    if dbg:
        b.dma("Y", dbg["mc"].ap(), t["mcols"][:, :], "dout", reads=["mcols"],
              writes=["dbg_mc"], tag="d_mc")
        b.dma("Y", dbg["u"].ap(), u_sm, "dout", reads=["wC1"],
              writes=["dbg_u"], tag="d_u")
    for half in range(2):
        for j in range(8):
            kd = half * 8 + j
            b.tr(ps["W"][:, j * 16:(j + 1) * 16],
                 u_sm[:, kd * 128:(kd + 1) * 128], t["identb"][0:16, 0:16],
                 reads=["wC1", "identb_sb"], writes=["psW"], inc=(j == 7),
                 tag=f"f{it}utr{kd}")
        b.v(lambda e, half=half: e.tensor_copy(
            t["uT"][:, half * 128:(half + 1) * 128], ps["W"][:, 0:128]),
            reads=["psW"], writes=["uT"], tag=f"f{it}utc{half}")
    for kd in range(ND):
        sl = kd % 2
        b.dma("G", t[f"wB{sl}"][:, :],
              d["writeT"][kd * 128:(kd + 1) * 128, :], f"wv{sl}",
              writes=[f"wB{sl}"], tag=f"f{it}lww{kd}")
        for c in range(4):
            b.mm(ps[f"A{c}"][0:16, 0:512],
                 t["uT"][:, (kd % 8) * 16 + (kd // 8) * 128:
                         (kd % 8) * 16 + (kd // 8) * 128 + 16],
                 t[f"wB{sl}"][:, c * 512:(c + 1) * 512],
                 start=(kd == 0), stop=False, inc=(c == 3),
                 reads=["uT", f"wB{sl}"], writes=[f"psA{c}"],
                 tag=f"f{it}wr{kd}_{c}")
    for c in range(4):
        half = c // 2
        if c % 2 == 0:
            b.dma("G", t["z_row"][0:1, :],
                  d["writeb_row"][0:1, half * 1024:(half + 1) * 1024], "dcc",
                  reads=[], writes=["z_row"], tag=f"f{it}lwb{half}")
        b.mm(ps[f"A{c}"][0:16, 0:512], t["ones"][0:1, 0:16],
             t["z_row"][0:1, (c % 2) * 512:(c % 2) * 512 + 512],
             start=False, stop=True,
             reads=["ones_sb", "z_row"], writes=[f"psA{c}"],
             tag=f"f{it}wb{c}")
    for c in range(4):
        b.v(lambda e, c=c: e.tensor_tensor(
            ws[:, c * 512:(c + 1) * 512], ps[f"A{c}"][0:16, 0:512],
            ws[:, c * 512:(c + 1) * 512], ALU.add),
            reads=[f"psA{c}", "kv_loc"], writes=["kv_loc"],
            tag=f"f{it}wsa{c}")
    b.s(lambda e: e.activation(ps["B"][0:16, 0:1024], ws[:, 0:1024],
                               AF.Square, accum_out=t["mcols"][:, 15:16]),
        reads=["kv_loc"], writes=["psB", "mcols"], tag=f"f{it}wsq0")
    b.s(lambda e: e.activation(ps["B"][0:16, 0:1024], ws[:, 1024:2048],
                               AF.Square, accum_out=t["mcols"][:, 7:8]),
        reads=["kv_loc"], writes=["psB", "mcols"], tag=f"f{it}wsq1")
    b.v(lambda e: e.tensor_scalar(
        t["mcols"][:, 15:16], t["mcols"][:, 15:16], 1.0,
        t["mcols"][:, 7:8], ALU.mult, ALU.add),
        reads=["mcols"], writes=["mcols"], tag=f"f{it}wsq2")
    b.v(lambda e: e.tensor_scalar(
        t["mcols"][:, 15:16], t["mcols"][:, 15:16], 1.0 / D, None, ALU.mult),
        reads=["mcols"], writes=["mcols"], tag=f"f{it}wsq3")
    b.s(lambda e: e.activation(t["mcols"][:, 15:16], t["mcols"][:, 15:16],
                               AF.Sqrt, bias=t["c_eps6"][0:16, 0:1]),
        reads=["mcols", "c_eps6"], writes=["mcols"], tag=f"f{it}wsq4")
    b.v(lambda e: e.reciprocal(t["mcols"][:, 15:16], t["mcols"][:, 15:16]),
        reads=["mcols"], writes=["mcols"], tag=f"f{it}wsq5")
    for hf in range(2):
        b.dma("G", t["vext1"][:, 0:1024],
              d["postw_bcast"][:, hf * 1024:(hf + 1) * 1024], "dpwb",
              writes=["vext1"], tag=f"f{it}pw{hf}")
        b.v(lambda e, hf=hf: e.scalar_tensor_tensor(
            ws[:, hf * 1024:(hf + 1) * 1024],
            ws[:, hf * 1024:(hf + 1) * 1024], t["mcols"][:, 15:16],
            t["vext1"][0:16, 0:1024], ALU.mult, ALU.mult),
            reads=["kv_loc", "mcols", "vext1"], writes=["kv_loc"],
            tag=f"f{it}wsn{hf}")
    if dbg:
        b.dma("Y", dbg["ws"].ap(), ws, "dout", reads=["kv_loc"],
              writes=["dbg_ws"], tag="d_ws")
    for half in range(2):
        b.v(lambda e, half=half: e.tensor_copy(
            t["sm16a"][:, :], ws[:, half * 1024:(half + 1) * 1024]),
            reads=["kv_loc"], writes=["sm16a"], tag=f"f{it}wsbc{half}")
        for j in range(8):
            b.tr(ps["W"][:, j * 16:(j + 1) * 16],
                 t["sm16a"][:, j * 128:(j + 1) * 128], t["identb"][0:16, 0:16],
                 reads=["sm16a", "identb_sb"], writes=["psW"], inc=(j == 7),
                 tag=f"f{it}wst{half}_{j}")
        b.v(lambda e, half=half: e.tensor_copy(
            t["wsT"][:, half * 128:(half + 1) * 128], ps["W"][:, 0:128]),
            reads=["psW"], writes=["wsT"], tag=f"f{it}wstc{half}")
    banksKV = [
        (ps["A0"], "psA0"), (ps["A1"], "psA1"),
        (ps["A2"], "psA2"), (ps["A3"], "psA3"),
        (ps["B"][:, 0:512], "psB"), (ps["B"][:, 512:1024], "psB"),
        (ps["K"][0:128, 0:512], "psK"), (ps["K"], "psK"),
    ]
    for kd in range(ND):
        sl = kd % 2
        b.dma("G", t[f"wC{sl}"][:, :],
              d["inprojT"][kd * 128:(kd + 1) * 128, 2048:6144],
              f"wo{sl}", writes=[f"wC{sl}"], tag=f"f{it}lkv{kd}")
        for part in range(2):
            for c in range(4):
                idx = part * 4 + c
                pa, pan = banksKV[idx]
                po = pa[32:48, 0:512] if idx == 7 else pa[0:16, 0:512]
                b.mm(po,
                     t["wsT"][:, (kd // 8) * 128 + (kd % 8) * 16:
                              (kd // 8) * 128 + (kd % 8) * 16 + 16],
                     t[f"wC{sl}"][:, part * 2048 + c * 512:
                                  part * 2048 + (c + 1) * 512],
                     start=(kd == 0), stop=False, inc=(c == 3),
                     reads=["wsT", f"wC{sl}"], writes=[pan],
                     tag=f"f{it}kv{part}_{kd}_{c}")
    for part, ofs in enumerate([2048, 4096]):
        for c in range(4):
            half = c // 2
            idx = part * 4 + c
            pa, pan = banksKV[idx]
            po = pa[32:48, 0:512] if idx == 7 else pa[0:16, 0:512]
            if c % 2 == 0:
                b.dma("G", t["z_row"][0:1, :],
                      d["inprojb_row"][0:1, ofs + half * 1024:
                                       ofs + half * 1024 + 1024], "drow",
                      reads=[], writes=["z_row"], tag=f"f{it}lkb{part}_{half}")
            b.mm(po, t["ones"][0:1, 0:16],
                 t["z_row"][0:1, (c % 2) * 512:(c % 2) * 512 + 512],
                 start=False, stop=True,
                 reads=["ones_sb", "z_row"], writes=[pan],
                 tag=f"f{it}kvb{part}_{c}")
        for half in range(2):
            for cc in range(2):
                c = half * 2 + cc
                idx = part * 4 + c
                pa, pan = banksKV[idx]
                po = pa[32:48, 0:512] if idx == 7 else pa[0:16, 0:512]
                b.v(lambda e, po=po, cc=cc: e.tensor_copy(
                    t["sm16a"][:, cc * 512:(cc + 1) * 512], po),
                    reads=[pan], writes=["sm16a"],
                    tag=f"f{it}kve{part}_{c}")
            if part == 0:
                for j in range(8):
                    b.tr(ps["W"][:, j * 16:(j + 1) * 16],
                         t["sm16a"][:, j * 128:(j + 1) * 128],
                         t["identb"][0:16, 0:16],
                         reads=["sm16a", "identb_sb"], writes=["psW"],
                         inc=(j == 7), tag=f"f{it}kvt{half}_{j}")
                b.v(lambda e, half=half: e.tensor_copy(
                    bass.AP(t["vext0"], half * 256,
                            [[1032, 64], [32, 8], [1, 16]]),
                    bass.AP(ps["W"], 0, [[1024, 64], [16, 8], [1, 16]])),
                    reads=["psW"], writes=["vext0"], tag=f"f{it}kbdA{half}")
                b.v(lambda e, half=half: e.tensor_copy(
                    bass.AP(t["vext0"], 64 * 1032 + half * 256 + 16,
                            [[1032, 64], [32, 8], [1, 16]]),
                    bass.AP(ps["W"], 64 * 1024,
                            [[1024, 64], [16, 8], [1, 16]])),
                    reads=["psW"], writes=["vext0"], tag=f"f{it}kbdB{half}")
            else:
                b.v(lambda e, half=half: e.tensor_copy(
                    bass.AP(t["kv_gb"], half * 1024,
                            [[2064, 16], [128, 8], [1, 64]]),
                    bass.AP(t["sm16a"], 0, [[1024, 16], [128, 8], [1, 64]])),
                    reads=["sm16a"], writes=["kv_gb"], tag=f"f{it}vbdA{half}")
                b.dma("G",
                      bass.AP(t["kv_gb"], 16 * 2064 + half * 1024 + 64,
                              [[2064, 16], [128, 8], [1, 64]]),
                      bass.AP(t["sm16a"], 64, [[1024, 16], [128, 8], [1, 64]]),
                      "ds2s", reads=["sm16a"], writes=["kv_gb"],
                      tag=f"f{it}vbdB{half}")

    if dbg:
        b.dma("Y", dbg["kk"].ap(), t["vext0"][:, 0:512], "dout",
              reads=["vext0"], writes=["dbg_kk"], tag="d_kk")
        b.dma("Y", dbg["vv"].ap(), t["kv_gb"][0:32, 0:2048], "dout",
              reads=["kv_gb"], writes=["dbg_vv"], tag="d_vv")


def emit_ffn_slotattn(b, d, t, ps, it, dbg=None):
    emit_ffn_q(b, d, t, ps, it)
    emit_ffn_scores(b, d, t, ps, it, dbg)


def emit_ffn_q(b, d, t, ps, it):
    p = b.p
    env = _ffn_env(b, d, t, ps, it)
    for j in range(ND):
        sl = j % 2
        wq = t[f"wA{sl}"]
        b.dma("G", bass.AP(wq, 0, [[2048, 128], [128, 16], [1, 128]]),
              d["inprojQ"][j],
              f"wk{sl}", writes=[f"wA{sl}"], tag=f"f{it}lq{j}")
        pb, pbn = (ps["B"][:, 0:512], "psB") if j % 2 == 0 else                   (ps["B"][:, 512:1024], "psB")
        for c in range(2):
            pa, pan = psum_banks8(ps)[(2 * j + c) % 4]
            for kd in range(ND):
                b.mm(pa, wq[:, kd * 128:(kd + 1) * 128],
                     t["yT"][:, kd * TLOC + c * 512: kd * TLOC + c * 512 + 512],
                     start=(kd == 0), stop=(kd == ND - 1),
                     reads=["yT", f"wA{sl}"], writes=[pan],
                     tag=f"f{it}q{j}_{c}")
            b.v(lambda e, j=j, c=c, pa=pa: e.tensor_scalar(
                t["big1"][:, j * TLOC + c * 512: j * TLOC + (c + 1) * 512],
                pa, t["inprojb_sb"][:, j:j + 1], None, ALU.add),
                reads=[pan, "inprojb_sb"], writes=["big1"],
                tag=f"f{it}qe{j}_{c}")


def emit_ffn_scores(b, d, t, ps, it, dbg=None):
    p = b.p
    env = _ffn_env(b, d, t, ps, it)
    att_s, sA = env.att_s, env.sA
    if dbg:
        b.dma("Y", dbg["kk"].ap(), t["vext0"][:, 0:512], "dout",
              reads=["vext0"], writes=["dbg_kk"], tag="d_kk")
        b.dma("Y", dbg["vv"].ap(), t["kv_gb"][0:32, 0:2048], "dout",
              reads=["kv_gb"], writes=["dbg_vv"], tag="d_vv")
    for j in range(ND):
        qT = t["big1"][:, j * TLOC:(j + 1) * TLOC]
        for ti in range(NT):
            b.mm(ps["A2"][:, ti * 32: ti * 32 + 32],
                 qT[:, ti * 128:(ti + 1) * 128],
                 t["vext0"][:, j * 32:(j + 1) * 32],
                 start=True, stop=True, inc=(ti == NT - 1),
                 reads=["big1", "vext0"], writes=["psA2"],
                 tag=f"f{it}sc{j}_{ti}")
        b.s(lambda e: e.activation(sA[:, 0:256], ps["A2"][:, 0:256],
                                   AF.Exp, scale=0.125),
            reads=["psA2"], writes=["scrA"], tag=f"f{it}se{j}")
        b.v(lambda e: e.reduce_sum(
            sA[:, 256:272], bass.AP(t["scrA"], 0, [[2048, 128], [16, 16],
                                                   [1, 16]]), AX.X),
            reads=["scrA"], writes=["scrA"], tag=f"f{it}sd{j}")
        b.v(lambda e: e.reciprocal(sA[:, 272:288], sA[:, 256:272]),
            reads=["scrA"], writes=["scrA"], tag=f"f{it}sr{j}")
        b.v(lambda e: e.tensor_tensor(
            bass.AP(t["vext1"], 0, [[1032, 128], [16, 16], [1, 16]]),
            bass.AP(t["scrA"], 0, [[2048, 128], [16, 16], [1, 16]]),
            bass.AP(t["scrA"], 272, [[2048, 128], [1, 16], [0, 16]]),
            ALU.mult),
            reads=["scrA"], writes=["vext1"], tag=f"f{it}sa{j}")
        for g4 in range(2):
            for tl in range(4):
                ti = g4 * 4 + tl
                b.tr(ps["W"][0:32, tl * 128:(tl + 1) * 128],
                     att_s[:, ti * 32:(ti + 1) * 32], t["identb"][:, :],
                     reads=["vext1", "identb_sb"], writes=["psW"],
                     inc=(tl == 3), tag=f"f{it}st{j}_{ti}")
            b.v(lambda e: e.tensor_copy(
                att_s[0:32, 512:1024], ps["W"][0:32, 0:512]),
                reads=["psW"], writes=["vext1"], tag=f"f{it}ste{j}_{g4}")
            b.mm(ps["A3"][:, 0:512],
                 t["kv_gb"][0:32, j * 128:(j + 1) * 128],
                 att_s[0:32, 512:1024],
                 start=True, stop=True,
                 reads=["kv_gb", "vext1"], writes=["psA3"],
                 tag=f"f{it}cx{j}_{g4}")
            b.v(lambda e, j=j, g4=g4: e.tensor_copy(
                t["big1"][:, j * TLOC + g4 * 512: j * TLOC + (g4 + 1) * 512],
                ps["A3"][:, 0:512]),
                reads=["psA3"], writes=["big1"], tag=f"f{it}cxe{j}_{g4}")
    if dbg:
        b.dma("Y", dbg["ctx"].ap(), t["big1"][:, :], "dout", reads=["big1"],
              writes=["dbg_ctx"], tag="d_ctx")


def emit_ffn_xb(b, d, t, ps, it, dbg=None):
    p = b.p
    env = _ffn_env(b, d, t, ps, it)
    ws, u_sm, tkw, logit_sm = env.ws, env.u_sm, env.tkw, env.logit_sm
    e_bf, eT, att_s, qT, sA = env.e_bf, env.eT, env.att_s, env.qT, env.sA

    def lhs_x(kd, ti):
        return t["big1"][:, kd * TLOC + ti * 128: kd * TLOC + ti * 128 + 128]

    def evac_x(cb, ti, pa, pname):
        b.v(lambda e, cb=cb, ti=ti, pa=pa: e.tensor_tensor(
            t["ytm"][:, ti * D + cb * 512: ti * D + (cb + 1) * 512], pa,
            t["ytm"][:, ti * D + cb * 512: ti * D + (cb + 1) * 512], ALU.add),
            reads=[pname, "ytm"], writes=["ytm"], tag=f"f{it}ya{cb}_{ti}")
        b.s(lambda e, cb=cb, ti=ti: e.activation(
            ps["A3"][:, 0:512] if False else t["scrA"][:, 1024:1536],
            t["ytm"][:, ti * D + cb * 512: ti * D + (cb + 1) * 512],
            AF.Square, accum_out=t["c_tmp"][:, ti:ti + 1]),
            reads=["ytm"], writes=["scrA", "c_tmp"],
            tag=f"f{it}ysq{cb}_{ti}")
        if cb == 0:
            b.v(lambda e, ti=ti: e.tensor_copy(
                t["c_ssq"][:, ti:ti + 1], t["c_tmp"][:, ti:ti + 1]),
                reads=["c_tmp"], writes=["c_ssq"], tag=f"f{it}yac{cb}_{ti}")
        else:
            b.v(lambda e, ti=ti: e.tensor_add(
                t["c_ssq"][:, ti:ti + 1], t["c_ssq"][:, ti:ti + 1],
                t["c_tmp"][:, ti:ti + 1]),
                reads=["c_tmp", "c_ssq"], writes=["c_ssq"],
                tag=f"f{it}yac{cb}_{ti}")

    emit_colblock_out(b, t, ps, d["outpT4"], lhs_x, ["big1"],
                      d["outpb_row"], evac_x, f"xb{it}")
    for ti in range(NT):
        b.v(lambda e, ti=ti: e.tensor_scalar(
            t["c_ssq"][:, ti:ti + 1], t["c_ssq"][:, ti:ti + 1], 1.0 / D, None,
            ALU.mult), reads=["c_ssq"], writes=["c_ssq"], tag=f"f{it}yn0{ti}")
        b.s(lambda e, ti=ti: e.activation(
            t["c_ssq"][:, ti:ti + 1], t["c_ssq"][:, ti:ti + 1], AF.Sqrt,
            bias=t["c_eps6"][:, 0:1]),
            reads=["c_ssq", "c_eps6"], writes=["c_ssq"], tag=f"f{it}yn1{ti}")
        b.v(lambda e, ti=ti: e.reciprocal(
            t["c_rstd"][:, ti:ti + 1], t["c_ssq"][:, ti:ti + 1]),
            reads=["c_ssq"], writes=["c_rstd"], tag=f"f{it}yn2{ti}")
        for hf in range(2):
            if ti == 0:
                b.dma("G", t["vext1"][:, 0:1024] if hf == 0 else
                      t["phik1"][:, 0:1024],
                      d["postw_bcast"][:, hf * 1024:(hf + 1) * 1024], "dpwb",
                      writes=["vext1" if hf == 0 else "phik1"],
                      tag=f"f{it}ypw{hf}")
            buf = t["vext1"][:, 0:1024] if hf == 0 else t["phik1"][:, 0:1024]
            bname = "vext1" if hf == 0 else "phik1"
            b.v(lambda e, ti=ti, hf=hf, buf=buf: e.scalar_tensor_tensor(
                t["ytm"][:, ti * D + hf * 1024: ti * D + (hf + 1) * 1024],
                t["ytm"][:, ti * D + hf * 1024: ti * D + (hf + 1) * 1024],
                t["c_rstd"][:, ti:ti + 1], buf, ALU.mult, ALU.mult),
                reads=["ytm", "c_rstd", bname], writes=["ytm"],
                tag=f"f{it}yap{ti}_{hf}")


def emit_l0_ffn_prep(b, d, t):
    b.dma("Y", t["kv_loc"][0:16, 0:2048], d["workspace"][:, :], "dws",
          writes=["kv_loc"], tag="ldws")
    b.dma("Y", t["inprojb_sb"][:, :], d["inprojb_col"][:, :], "dib",
          writes=["inprojb_sb"], tag="ldib")



def emit_l0_combine(b, d, t, x1_out, fuse_ln=False, ps_g=None):
    for ti in range(NT):
        xr = t["xrot"][:, 0:D] if ti % 2 == 0 else t["kv_loc"][:, 0:2048]
        xrn = "xrot" if ti % 2 == 0 else "kv_loc"
        st = f"dx{ti % 2}"
        b.dma("Y", xr, d["x"][ti * 128:(ti + 1) * 128, :], st,
              writes=[xrn], tag=f"cb_ld{ti}")
        for c in range(2):
            sl = c * 1024
            sbuf = t["phiq1"][:, 0:1024] if c == 0 else t["phiq0"][:, 0:1024]
            sbn = "phik1" if c == 0 else "phik0"
            scr = t["scrA"][:, 0:1024] if c == 0 else t["scrA"][:, 1024:2048]
            b.dma("Y", sbuf, d["s_scr"][ti * 128:(ti + 1) * 128,
                                        sl:sl + 1024], f"dsi{c}",
                  reads=["s_scr"], writes=[sbn], tag=f"cb_ls{ti}_{c}")
            b.s(lambda e, ti=ti, sl=sl, scr=scr: e.activation(
                scr, t["ytm"][:, ti * D + sl: ti * D + sl + 1024], AF.Silu),
                reads=["ytm"], writes=["scrA"], tag=f"cb_si{ti}_{c}")
            b.v(lambda e, sbuf=sbuf, scr=scr: e.tensor_tensor(
                scr, scr, sbuf, ALU.mult),
                reads=["scrA", sbn], writes=["scrA"],
                tag=f"cb_m{ti}_{c}")
            b.v(lambda e, xr=xr, sl=sl, scr=scr: e.tensor_add(
                xr[:, sl:sl + 1024], xr[:, sl:sl + 1024], scr),
                reads=["scrA", xrn], writes=[xrn], tag=f"cb_a{ti}_{c}")
        b.dma("Y", x1_out[ti * 128:(ti + 1) * 128, :], xr,
              "dout", reads=[xrn], writes=["x1_out"], tag=f"cb_st{ti}")
        if fuse_ln:
            emit_ln_tile(b, t, ti, tag="lnf", xr=xr, xrn=xrn)


def emit_ln_tile(b, t, ti, eps=1e-5, tag="ln", xr=None, xrn=None):
    if xr is None:
        xr = t["xrot"][:, 0:D]
        xrn = "xrot"
    xname = xrn
    b.s(lambda e, xr=xr, ti=ti: e.activation(
        t["scrA"][:, 0:1024], xr[:, 0:1024], AF.Square,
        accum_out=t["c_ssq"][:, ti:ti + 1]),
        reads=[xname], writes=["scrA", "c_ssq"], tag=f"{tag}_sqa{ti}")
    b.s(lambda e, xr=xr, ti=ti: e.activation(
        t["scrA"][:, 0:1024], xr[:, 1024:2048], AF.Square,
        accum_out=t["c_tmp"][:, ti:ti + 1]),
        reads=[xname, "scrA"], writes=["scrA", "c_tmp"], tag=f"{tag}_sqb{ti}")
    b.v(lambda e, ti=ti: e.tensor_scalar(
        t["c_ssq"][:, ti:ti + 1], t["c_ssq"][:, ti:ti + 1], 1.0,
        t["c_tmp"][:, ti:ti + 1], ALU.mult, ALU.add),
        reads=["c_ssq", "c_tmp"], writes=["c_ssq"], tag=f"{tag}_ss{ti}")
    b.v(lambda e, xr=xr, ti=ti: e.reduce_sum(
        t["c_nm"][:, ti:ti + 1], xr, AX.X),
        reads=[xname], writes=["c_nm"], tag=f"{tag}_sum{ti}")
    b.v(lambda e, ti=ti: e.tensor_scalar(
        t["c_nm"][:, ti:ti + 1], t["c_nm"][:, ti:ti + 1], -1.0 / D, None,
        ALU.mult), reads=["c_nm"], writes=["c_nm"], tag=f"{tag}_nm{ti}")
    b.v(lambda e, ti=ti: e.tensor_mul(
        t["c_tmp"][:, ti:ti + 1], t["c_nm"][:, ti:ti + 1],
        t["c_nm"][:, ti:ti + 1]),
        reads=["c_nm"], writes=["c_tmp"], tag=f"{tag}_msq{ti}")
    b.v(lambda e, ti=ti: e.tensor_scalar(
        t["c_var"][:, ti:ti + 1], t["c_ssq"][:, ti:ti + 1], 1.0 / D,
        t["c_tmp"][:, ti:ti + 1], ALU.mult, ALU.subtract),
        reads=["c_ssq", "c_tmp"], writes=["c_var"], tag=f"{tag}_var{ti}")
    b.s(lambda e, ti=ti: e.activation(
        t["c_var"][:, ti:ti + 1], t["c_var"][:, ti:ti + 1], AF.Sqrt,
        bias=t["c_eps5"][:, 0:1]), reads=["c_var", "c_eps5"],
        writes=["c_var"], tag=f"{tag}_sd{ti}")
    b.v(lambda e, ti=ti: e.reciprocal(
        t["c_rstd"][:, ti:ti + 1], t["c_var"][:, ti:ti + 1]),
        reads=["c_var"], writes=["c_rstd"], tag=f"{tag}_rs{ti}")
    b.v(lambda e, xr=xr, ti=ti: e.tensor_scalar(
        t["ytm"][:, ti * D:(ti + 1) * D], xr, t["c_nm"][:, ti:ti + 1],
        t["c_rstd"][:, ti:ti + 1], ALU.add, ALU.mult),
        reads=[xname, "c_nm", "c_rstd"], writes=["ytm"], tag=f"{tag}_ap{ti}")


def build_stageC():
    nc = bass.Bass("TRN2", target_bir_lowering=False, debug=False)
    b = Builder(nc)
    d = declare_l0(b)
    d["inprojb_row"] = b.din("inprojb_row", (1, 3 * D))
    d["postw_bcast"] = b.din("postw_bcast", (128, D))
    d["identf"] = b.din("identf_in", (128, 128))
    x1o = b.dout("x1_out", (TLOC, D), F32)
    d["s_scr"] = b.dint("s_scr", (TLOC, D), BF16)
    t = alloc_common(b)
    alloc_ffn(b, t)
    t["competeb_sb"] = b.sbuf("competeb_sb", (16, 1), F32)
    t["inprojb_sb"] = b.sbuf("inprojb_sb", (128, 48), F32)
    t["prew_sb"] = t["inprojb_sb"][:, 16:32]
    ps = alloc_psum(b)
    for st in ("dx", "dmisc", "dmisc2", "dout"):
        b.p.dma_stream(st, mode="slot" if st != "dout" else "barrier")
    for st in ("dws", "dcpt", "dpw", "dpwb", "dcb", "dib", "dif", "drow",
               "ds2s", "dsA", "dlA", "dsz", "dsT", "dmg", "dsU", "dum", "dcptF", "dso", "dsi"):
        b.p.dma_stream(st, mode="slot")
    load_consts(b, d, t)
    emit_ln(b, d, t, d["x"], t["ytm"], tag="ln1")
    emit_cpt_prep(b, d, t)
    emit_tp_logits(b, d, t, ps, 0)
    emit_l0_attn(b, d, t, ps)
    emit_l0_ffn_prep(b, d, t)
    emit_l0_ffn_iter(b, d, t, ps, 0)
    emit_tp_logits(b, d, t, ps, 1)
    emit_l0_ffn_iter(b, d, t, ps, 1)
    emit_l0_combine(b, d, t, x1o)
    b.p.op("Y", lambda e: e.nop(), reads=["x1_out"], writes=(), inc=False,
           tag="fin")
    return finalize(b, [])


def build_stageC_dbg():
    nc = bass.Bass("TRN2", target_bir_lowering=False, debug=False)
    b = Builder(nc)
    d = declare_l0(b)
    d["inprojb_row"] = b.din("inprojb_row", (1, 3 * D))
    d["postw_bcast"] = b.din("postw_bcast", (128, D))
    d["identf"] = b.din("identf_in", (128, 128))
    x1o = b.dout("x1_out", (TLOC, D), F32)
    dbg = {
        "lg": b.dout("dbg_lg", (16, TLOC), F32),
        "mc": b.dout("dbg_mc", (16, 16), F32),
        "u": b.dout("dbg_u", (16, D), BF16),
        "ws": b.dout("dbg_ws", (16, D), F32),
        "kk": b.dout("dbg_kk", (128, 512), BF16),
        "vv": b.dout("dbg_vv", (32, D), BF16),
        "ctx": b.dout("dbg_ctx", (128, ND * TLOC), BF16),
        "att": b.dout("dbg_att", (128, 64), BF16),
        "q": b.dout("dbg_q", (128, TLOC), BF16),
    }
    dbg_y = b.dout("dbg_y", (TLOC, D), F32)
    d["s_scr"] = b.dint("s_scr", (TLOC, D), BF16)
    t = alloc_common(b)
    alloc_ffn(b, t)
    t["competeb_sb"] = b.sbuf("competeb_sb", (16, 1), F32)
    t["inprojb_sb"] = b.sbuf("inprojb_sb", (128, 48), F32)
    t["prew_sb"] = t["inprojb_sb"][:, 16:32]
    ps = alloc_psum(b)
    for st in ("dx", "dmisc", "dmisc2"):
        b.p.dma_stream(st, mode="slot")
    b.p.dma_stream("dout", mode="barrier")
    for st in ("dws", "dcpt", "dpw", "dpwb", "dcb", "dib", "dif", "drow",
               "ds2s", "dsA", "dlA", "dsz", "dsT", "dmg", "dsU", "dum", "dcptF", "dso", "dsi"):
        b.p.dma_stream(st, mode="slot")
    load_consts(b, d, t)
    emit_ln(b, d, t, d["x"], t["ytm"], tag="ln1")
    emit_cpt_prep(b, d, t)
    emit_tp_logits(b, d, t, ps, 0)
    emit_l0_attn(b, d, t, ps)
    emit_l0_ffn_prep(b, d, t)
    emit_l0_ffn_iter(b, d, t, ps, 0, dbg=dbg)
    for ti in range(NT):
        b.dma("Y", dbg_y[ti * 128:(ti + 1) * 128, :],
              t["ytm"][:, ti * D:(ti + 1) * D], "dout", reads=["ytm"],
              writes=["dbg_y"], tag=f"dy{ti}")
    emit_tp_logits(b, d, t, ps, 1)
    emit_l0_ffn_iter(b, d, t, ps, 1)
    emit_l0_combine(b, d, t, x1o)
    b.p.op("Y", lambda e: e.nop(),
           reads=["x1_out", "dbg_lg", "dbg_mc", "dbg_u", "dbg_ws", "dbg_y",
                  "dbg_kk", "dbg_vv", "dbg_ctx", "dbg_att", "dbg_q"],
           writes=(), inc=False, tag="fin")
    return finalize(b, [])


def emit_cpt_prep(b, d, t):
    cptF = t["xrot"][:, 1024:1280]
    b.dma("Y", t["identf"][:, :], d["identf"][:, :], "dif",
          writes=["identf"], tag="ldif")
    b.dma("Y", t["competeb_sb"][:, :], d["competeb_col"][:, :], "dcb",
          writes=["competeb_sb"], tag="ldcb")
    b.dma("Y", bass.AP(t["xrot"], 1024, [[2048, 128], [16, 16], [1, 16]]),
          d["competeT"].ap().rearrange("(k p) c -> p k c", p=128), "dcptF",
          writes=["xrot"], tag="ldcptF")
    b.dma("Y", t["prew_sb"], d["prew_col"][:, :], "dpw",
          writes=["inprojb_sb"], tag="ldprew")
    for kd in range(ND):
        b.v(lambda e, kd=kd, cptF=cptF: e.tensor_scalar(
            cptF[:, kd * 16:(kd + 1) * 16], cptF[:, kd * 16:(kd + 1) * 16],
            t["inprojb_sb"][:, 16 + kd:17 + kd], None, ALU.mult),
            reads=["xrot", "inprojb_sb"], writes=["xrot"], tag=f"cpts{kd}")
    b.v(lambda e, cptF=cptF: e.tensor_copy(t["cpth"][:, :], cptF),
        reads=["xrot"], writes=["cpth"], tag="cpth")
    b.v(lambda e, cptF=cptF: e.tensor_tensor(t["cptl"][:, :], cptF,
                                             t["cpth"][:, :], ALU.subtract),
        reads=["xrot", "cpth"], writes=["cptl"], tag="cptl")


def emit_tp_logits(b, d, t, ps, it):
    logit_sm = t["xrot"][0:16, 0:1024]
    sA = t["scrA"]
    for ti in range(NT):
        for c in range(2):
            hi = t["phik1"][:, 0:1024]
            lo = t["vext1"][:, 0:1024]
            b.s(lambda e, ti=ti, c=c, hi=hi: e.copy(
                hi, t["ytm"][:, ti * D + c * 1024: ti * D + (c + 1) * 1024]),
                reads=["ytm"], writes=["phik1"], tag=f"tl{it}h{ti}_{c}")
            b.v(lambda e, ti=ti, c=c, hi=hi, lo=lo: e.tensor_tensor(
                lo, t["ytm"][:, ti * D + c * 1024: ti * D + (c + 1) * 1024],
                hi, ALU.subtract),
                reads=["ytm", "phik1"], writes=["vext1"], tag=f"tl{it}l{ti}_{c}")
            for j in range(8):
                b.tr(ps["W"][:, j * 128:(j + 1) * 128],
                     hi[:, j * 128:(j + 1) * 128], t["identb"][:, :],
                     reads=["phik1", "identb_sb"], writes=["psW"],
                     inc=(j == 7), tag=f"tl{it}th{ti}_{c}_{j}")
            b.v(lambda e, ti=ti, c=c: e.tensor_copy(
                bass.AP(t["yT"], c * 8 * TLOC + ti * 128,
                        [[ND * TLOC, 128], [TLOC, 8], [1, 128]]),
                ps["W"][:, :]),
                reads=["psW"], writes=["yT"], tag=f"tl{it}eh{ti}_{c}")
            for j in range(8):
                b.tr(ps["W"][:, j * 128:(j + 1) * 128],
                     lo[:, j * 128:(j + 1) * 128], t["identb"][:, :],
                     reads=["vext1", "identb_sb"], writes=["psW"],
                     inc=(j == 7), tag=f"tl{it}tl{ti}_{c}_{j}")
            b.v(lambda e, ti=ti, c=c: e.tensor_copy(
                t["wC0"][:, c * 1024:(c + 1) * 1024], ps["W"][:, :]),
                reads=["psW"], writes=["wC0"], tag=f"tl{it}el{ti}_{c}")
        for kd in range(ND):
            yh = t["yT"][:, kd * TLOC + ti * 128: kd * TLOC + ti * 128 + 128]
            yl = t["wC0"][:, kd * 128:(kd + 1) * 128]
            b.mm(ps["A2"][:, 0:16], yh, t["cpth"][:, kd * 16:(kd + 1) * 16],
                 start=(kd == 0), stop=False, inc=False,
                 reads=["yT", "cpth"], writes=["psA2"], tag=f"tl{it}m1_{ti}")
            b.mm(ps["A2"][:, 0:16], yh, t["cptl"][:, kd * 16:(kd + 1) * 16],
                 start=False, stop=False, inc=False,
                 reads=["yT", "cptl"], writes=["psA2"], tag=f"tl{it}m2_{ti}")
            b.mm(ps["A2"][:, 0:16], yl, t["cpth"][:, kd * 16:(kd + 1) * 16],
                 start=False, stop=(kd == ND - 1),
                 reads=["wC0", "cpth"], writes=["psA2"], tag=f"tl{it}m3_{ti}")
        b.s(lambda e, ti=ti: e.activation(
            ps["B"][:, 0:1024], t["ytm"][:, ti * D: ti * D + 1024], AF.Square,
            accum_out=t["c_tmp"][:, ti:ti + 1]),
            reads=["ytm"], writes=["psB", "c_tmp"], tag=f"tl{it}ra{ti}")
        b.s(lambda e, ti=ti: e.activation(
            ps["B"][:, 0:1024], t["ytm"][:, ti * D + 1024: ti * D + 2048],
            AF.Square, accum_out=t["c_var"][:, ti:ti + 1]),
            reads=["ytm"], writes=["psB", "c_var"], tag=f"tl{it}rb{ti}")
        b.v(lambda e, ti=ti: e.tensor_scalar(
            t["c_ssq"][:, ti:ti + 1], t["c_tmp"][:, ti:ti + 1], 1.0,
            t["c_var"][:, ti:ti + 1], ALU.mult, ALU.add),
            reads=["c_tmp", "c_var"], writes=["c_ssq"], tag=f"tl{it}rc{ti}")
        b.v(lambda e, ti=ti: e.tensor_scalar(
            t["c_ssq"][:, ti:ti + 1], t["c_ssq"][:, ti:ti + 1], 1.0 / D, None,
            ALU.mult), reads=["c_ssq"], writes=["c_ssq"], tag=f"tl{it}rd{ti}")
        b.s(lambda e, ti=ti: e.activation(
            t["c_ssq"][:, ti:ti + 1], t["c_ssq"][:, ti:ti + 1], AF.Sqrt,
            bias=t["c_eps6"][:, 0:1]),
            reads=["c_ssq", "c_eps6"], writes=["c_ssq"], tag=f"tl{it}re{ti}")
        b.v(lambda e, ti=ti: e.reciprocal(
            t["c_rstd"][:, ti:ti + 1], t["c_ssq"][:, ti:ti + 1]),
            reads=["c_ssq"], writes=["c_rstd"], tag=f"tl{it}rf{ti}")
        b.v(lambda e, ti=ti: e.tensor_scalar(
            sA[:, 0:16], ps["A2"][:, 0:16], t["c_rstd"][:, ti:ti + 1], None,
            ALU.mult), reads=["psA2", "c_rstd"], writes=["scrA"],
            tag=f"tl{it}sc{ti}")
        b.tr(ps["K"][0:16, 0:128], sA[:, 0:16], t["identf"][:, :],
             reads=["scrA", "identf"], writes=["psK"], tag=f"tl{it}tr{ti}")
        b.v(lambda e, ti=ti: e.tensor_scalar(
            logit_sm[:, ti * 128:(ti + 1) * 128], ps["K"][0:16, 0:128],
            t["competeb_sb"][:, 0:1], None, ALU.add),
            reads=["psK", "competeb_sb"], writes=["xrot"], tag=f"tl{it}ev{ti}")


def host_prep_l1(ins):
    P = {}
    P["Wq1T"] = tile_cols(np.ascontiguousarray(ins["L1_attn_Wq"].T), 128)
    P["Wk1T"] = tile_cols(np.ascontiguousarray(ins["L1_attn_Wk"].T), 128)
    P["Wv1T"] = tile_cols(np.ascontiguousarray(ins["L1_attn_Wv"].T), 128)
    P["Wo1T4"] = tile_cols_halfk(np.ascontiguousarray(ins["L1_attn_Wo"].T), 512)
    P["gateT"] = tile_cols(np.ascontiguousarray(ins["L1_ffn_gate_w"].T), 256)
    P["upT"] = tile_cols(np.ascontiguousarray(ins["L1_ffn_up_w"].T), 256)
    import ml_dtypes
    dT = np.ascontiguousarray(ins["L1_ffn_down_w"].T)
    P["downT"] = np.ascontiguousarray(
        dT.reshape(44, 128, 2, 1024).transpose(2, 0, 1, 3)).astype(
            ml_dtypes.bfloat16)
    l1b = np.concatenate([ins["L1_attn_bq"], ins["L1_attn_bk"],
                          ins["L1_attn_bv"]])
    P["l1b_col"] = col_layout(l1b)
    P["bo_row"] = np.ascontiguousarray(ins["L1_attn_bo"][None, :])
    P["gates_col"] = col_layout(ins["L1_ffn_gate_s"])
    P["ups_col"] = col_layout(ins["L1_ffn_up_s"])
    P["ds_bcast"] = np.ascontiguousarray(
        np.broadcast_to(ins["L1_ffn_down_s"][None, :], (128, 2048)))
    return P


def declare_l1(b: Builder):
    d = {}
    for n, sh in [("Wq1T", (16, 128, 16, 128)), ("Wk1T", (16, 128, 16, 128)),
                  ("Wv1T", (16, 128, 16, 128)), ("Wo1T4", (4, 2, 128, 8, 512)),
                  ("gateT", (22, 128, 16, 256)), ("upT", (22, 128, 16, 256)),
                  ("l1b_col", (128, 48)),
                  ("bo_row", (1, D)), ("gates_col", (128, 44)),
                  ("ups_col", (128, 44)), ("ds_bcast", (128, D))]:
        d[n] = b.din(n, sh)
    d["downT"] = b.din("downT", (2, 44, 128, 1024), BF16)
    d["halfsel"] = b.din("halfsel", (128, 1))
    d["ccC_i"] = b.dint("ccC_i", (128, 16))
    d["ccC_o"] = b.dint("ccC_o", (256, 16))
    return d


def emit_tp_plain(b, d, t, ps, tag="tpp"):
    for ti in range(NT):
        for c in range(2):
            hi = t["phik1"][:, 0:1024]
            b.v(lambda e, ti=ti, c=c, hi=hi: e.tensor_copy(
                hi, t["ytm"][:, ti * D + c * 1024: ti * D + (c + 1) * 1024]),
                reads=["ytm"], writes=["phik1"], tag=f"{tag}h{ti}_{c}")
            for j in range(8):
                b.tr(ps["W"][:, j * 128:(j + 1) * 128],
                     hi[:, j * 128:(j + 1) * 128], t["identb"][:, :],
                     reads=["phik1", "identb_sb"], writes=["psW"],
                     inc=(j == 7), tag=f"{tag}t{ti}_{c}_{j}")
            b.v(lambda e, ti=ti, c=c: e.tensor_copy(
                bass.AP(t["yT"], c * 8 * TLOC + ti * 128,
                        [[ND * TLOC, 128], [TLOC, 8], [1, 128]]),
                ps["W"][:, :]),
                reads=["psW"], writes=["yT"], tag=f"{tag}e{ti}_{c}")


def emit_l1_attn(b, d1, t, ps):
    p = b.p
    b.dma("Y", t["inprojb_sb"][:, :], d1["l1b_col"][:, :], "dib",
          writes=["inprojb_sb"], tag="l1b")
    b.v(lambda e: e.tensor_scalar(t["c_ssq"][:, :], t["inprojb_sb"][:, 0:16],
                                  -1.0, None, ALU.mult),
        reads=["inprojb_sb"], writes=["c_ssq"], tag="l1nbq")
    b.v(lambda e: e.tensor_scalar(t["c_tmp"][:, :], t["inprojb_sb"][:, 16:32],
                                  -1.0, None, ALU.mult),
        reads=["inprojb_sb"], writes=["c_tmp"], tag="l1nbk")
    phik = t["xrot"][:, 0:1024]
    vbuf = t["xrot"][:, 1024:2048]
    for j in range(ND):
        sl = j % 2
        wk, wv = t[f"wA{sl}"], t[f"wB{sl}"]
        b.dma("G", bass.AP(wk, 0, [[2048, 128], [128, 16], [1, 128]]),
              d1["Wk1T"][j], f"wk{sl}",
              writes=[f"wA{sl}"], tag=f"l1k{j}")
        b.dma("G", bass.AP(wv, 0, [[2048, 128], [128, 16], [1, 128]]),
              d1["Wv1T"][j], f"wv{sl}",
              writes=[f"wB{sl}"], tag=f"l1v{j}")
        for c in range(2):
            for kd in range(ND):
                b.mm(ps["B"][:, c * 512:(c + 1) * 512],
                     wk[:, kd * 128:(kd + 1) * 128],
                     t["yT"][:, kd * TLOC + c * 512: kd * TLOC + c * 512 + 512],
                     start=(kd == 0), stop=(kd == ND - 1),
                     reads=["yT", f"wA{sl}"], writes=["psB"],
                     tag=f"l1kp{j}_{c}")
            emit_phi_evac(b, t, ps["B"][:, c * 512:(c + 1) * 512], "psB",
                          phik[:, c * 512:(c + 1) * 512], "xrot", 512,
                          bias_col=(t["inprojb_sb"][:, 16 + j:17 + j],
                                    t["c_tmp"][:, j:j + 1], "c_tmp"),
                          tag=f"l1pk{j}_{c}")
        for c in range(2):
            for kd in range(ND):
                b.mm(ps["A2"][:, 0:512],
                     wv[:, kd * 128:(kd + 1) * 128],
                     t["yT"][:, kd * TLOC + c * 512: kd * TLOC + c * 512 + 512],
                     start=(kd == 0), stop=(kd == ND - 1),
                     reads=["yT", f"wB{sl}"], writes=["psA2"],
                     tag=f"l1vp{j}_{c}")
            b.v(lambda e, j=j, c=c: e.tensor_scalar(
                vbuf[:, c * 512:(c + 1) * 512], ps["A2"][:, 0:512],
                t["inprojb_sb"][:, 32 + j:33 + j], None, ALU.add),
                reads=["psA2", "inprojb_sb"], writes=["xrot"],
                tag=f"l1ve{j}_{c}")
        b.v(lambda e, j=j: e.tensor_mul(
            t["big1"][:, j * TLOC:(j + 1) * TLOC], phik, vbuf),
            reads=["xrot"], writes=["big1"], tag=f"l1u{j}")
        b.v(lambda e, j=j: e.reduce_sum(
            t["totc"][:, j:j + 1], t["big1"][:, j * TLOC:(j + 1) * TLOC],
            AX.X), reads=["big1"], writes=["kv_loc"], tag=f"l1t{j}")
    b.dma("Y", d1["ccC_i"][:, :], t["totc"], "dsC", reads=["kv_loc"],
          writes=["ccC_i"], tag="l1to")
    p._declare_sem("ccC", 1)
    b.g(lambda e: e.collective_compute(
        "AllGather", ALU.bypass, replica_groups=PAIRS,
        ins=[d1["ccC_i"].ap().opt()], outs=[d1["ccC_o"].ap().opt()]),
        reads=["ccC_i"], writes=["ccC_o"], sem="ccC", tag="l1cc")
    b.dma("Y", t["carry_sb"], d1["ccC_o"][0:128, :], "dlC",
          reads=["ccC_o"], writes=["kv_loc"], tag="l1ci")
    b.dma("Y", t["hs_sb"], d1["halfsel"][:, :], "dhs", writes=["c_var"],
          tag="l1hs2")
    b.v(lambda e: e.tensor_scalar(t["carry_sb"], t["carry_sb"], t["hs_sb"],
                                  None, ALU.mult),
        reads=["kv_loc", "c_var"], writes=["kv_loc"], tag="l1cs")
    for j in range(ND):
        sl = j % 2
        wq = t[f"wA{sl}"]
        b.dma("G", bass.AP(wq, 0, [[2048, 128], [128, 16], [1, 128]]),
              d1["Wq1T"][j], f"wk{sl}",
              writes=[f"wA{sl}"], tag=f"l1q{j}")
        for c in range(2):
            for kd in range(ND):
                b.mm(ps["B"][:, c * 512:(c + 1) * 512],
                     wq[:, kd * 128:(kd + 1) * 128],
                     t["yT"][:, kd * TLOC + c * 512: kd * TLOC + c * 512 + 512],
                     start=(kd == 0), stop=(kd == ND - 1),
                     reads=["yT", f"wA{sl}"], writes=["psB"],
                     tag=f"l1qp{j}_{c}")
            emit_phi_evac(b, t, ps["B"][:, c * 512:(c + 1) * 512], "psB",
                          phik[:, c * 512:(c + 1) * 512], "xrot", 512,
                          bias_col=(t["inprojb_sb"][:, j:j + 1],
                                    t["c_ssq"][:, j:j + 1], "c_ssq"),
                          tag=f"l1pq{j}_{c}")
        b.v(lambda e, j=j: e.tensor_tensor_scan(
            t["scrA"][:, 0:1024], t["big1"][:, j * TLOC:(j + 1) * TLOC],
            t["big1"][:, j * TLOC:(j + 1) * TLOC],
            t["carry_sb"][:, j:j + 1] if False else t["carry_sb"][:, j:j + 1],
            ALU.add, ALU.bypass),
            reads=["big1", "kv_loc"], writes=["scrA"], tag=f"l1s{j}")
        b.v(lambda e, j=j: e.scalar_tensor_tensor(
            t["big1"][:, j * TLOC:(j + 1) * TLOC], phik,
            0.08838834764831845, t["scrA"][:, 0:1024], ALU.mult, ALU.mult),
            reads=["xrot", "scrA"], writes=["big1"], tag=f"l1c{j}")
    def lhs_s2(kd, ti):
        return t["big1"][:, kd * TLOC + ti * 128: kd * TLOC + ti * 128 + 128]

    def evac_s2(cb, ti, pa, pname):
        sb = t["phiq0"][:, (ti % 2) * 512:(ti % 2) * 512 + 512]
        b.v(lambda e, pa=pa, sb=sb: e.tensor_copy(sb, pa),
            reads=[pname], writes=["phik0"], tag=f"l1se{cb}_{ti}")
        b.dma("Y", b.dram["s_scr"][ti * 128:(ti + 1) * 128,
                                   cb * 512:(cb + 1) * 512], sb, "dso",
              reads=["phik0"], writes=["s_scr"], tag=f"l1ss{cb}_{ti}")

    emit_colblock_out(b, t, ps, d1["Wo1T4"], lhs_s2, ["big1"],
                      d1["bo_row"], evac_s2, "s2")


def emit_l1_ffn(b, d1, t, ps, out_dram, x1_dram):
    NI = 44
    for st in ("hv0", "hv1", "hv2", "hv3"):
        b.p.dma_stream(st, mode="slot")
    hdnb = t["ytm"][:, :].bitcast(BF16)
    b.dma("G", t["kv_gb"][:, 0:2048], d1["ds_bcast"][:, :], "ddsb",
          writes=["kv_gb"], tag="ldds")
    b.dma("Y", t["gs_sb"], d1["gates_col"][:, :], "dgs", writes=["kv_loc"],
          tag="ldgs")
    b.dma("Y", t["us_sb"], d1["ups_col"][:, :], "dus", writes=["kv_loc"],
          tag="ldus")
    banks = psum_banks8(ps)
    up_slots = (t["big1"][:, 8192:12288], t["big1"][:, 12288:16384])
    for half in range(2):
        toff = half * 512
        for i2 in range(22):
            sl = i2 % 2
            wg = t[f"wC{sl}"]
            wu = up_slots[sl]
            b.dma("G", bass.AP(wg, 0, [[4096, 128], [256, 16], [1, 256]]),
                  d1["gateT"][i2], f"wo{sl}",
                  writes=[f"wC{sl}"], tag=f"g{half}_{i2}")
            b.dma("G", bass.AP(t["big1"], 8192 + sl * 4096,
                               [[ND * TLOC, 128], [256, 16], [1, 256]]),
                  d1["upT"][i2], f"wk{sl}",
                  writes=["big1"], tag=f"u{half}_{i2}")
            for io in range(2):
                i = i2 * 2 + io
                pg, pgn = banks[i % 4]
                pu, pun = banks[4 + i % 4]
                for kd in range(ND):
                    b.mm(pg, wg[:, kd * 256 + io * 128: kd * 256 + io * 128 + 128],
                         t["yT"][:, kd * TLOC + toff: kd * TLOC + toff + 512],
                         start=(kd == 0), stop=(kd == ND - 1),
                         reads=["yT", f"wC{sl}"], writes=[pgn],
                         tag=f"gm{half}_{i}")
                b.s(lambda e, i=i, io=io, pg=pg: e.activation(
                    t["scrA"][:, io * 512:io * 512 + 512], pg, AF.Silu,
                    scale=t["gs_sb"][:, i:i + 1]),
                    reads=[pgn, "kv_loc"], writes=["scrA"],
                    tag=f"gsi{half}_{i}")
                for kd in range(ND):
                    b.mm(pu, bass.AP(t["big1"],
                                     8192 + sl * 4096 + kd * 256 + io * 128,
                                     [[ND * TLOC, 128], [1, 128]]),
                         t["yT"][:, kd * TLOC + toff: kd * TLOC + toff + 512],
                         start=(kd == 0), stop=(kd == ND - 1),
                         reads=["yT", "big1"], writes=[pun],
                         tag=f"um{half}_{i}")
                b.v(lambda e, i=i, io=io, pu=pu: e.scalar_tensor_tensor(
                    hdnb[:, i * 512:(i + 1) * 512], pu, t["us_sb"][:, i:i + 1],
                    t["scrA"][:, io * 512:io * 512 + 512],
                    ALU.mult, ALU.mult),
                    reads=[pun, "kv_loc", "scrA"], writes=["ytm"],
                    tag=f"hd{half}_{i}")
        for cp in range(2):
            for kd in range(NI):
                sl = kd % 4
                wnm = ("wB0", "wB1", "wA0", "wA1")[sl]
                wdn = t[wnm]
                b.dma("Y", wdn[:, 0:1024], d1["downT"][cp, kd], f"hv{sl}",
                      writes=[wnm], tag=f"dn{half}_{cp}_{kd}")
                for tt in range(4):
                    for ci in range(2):
                        pa, pname = banks[tt * 2 + ci]
                        b.mm(pa, hdnb[:, kd * 512 + tt * 128:
                                      kd * 512 + tt * 128 + 128],
                             wdn[:, ci * 512:(ci + 1) * 512],
                             start=(kd == 0), stop=(kd == NI - 1),
                             inc=(ci == 1),
                             reads=["ytm", wnm], writes=[pname],
                             tag=f"dm{half}_{cp}_{kd}_{tt}")
            for tt in range(4):
                ti = half * 4 + tt
                for ci in range(2):
                    c = cp * 2 + ci
                    pa, pname = banks[tt * 2 + ci]
                    xr = t["xrot"][:, (c % 2) * 1024:(c % 2) * 1024 + 512]
                    xrn = f"xrot{c % 2}"
                    b.dma("Y", xr,
                          x1_dram[ti * 128:(ti + 1) * 128,
                                  c * 512:(c + 1) * 512], f"dx{c % 2}",
                          writes=[xrn], reads=["x1_out"],
                          tag=f"dcx{ti}_{c}")
                    sb2 = t["phiq1"][:, (c % 2) * 512:(c % 2) * 512 + 512]
                    sbn = f"phik1{c % 2}"
                    b.dma("Y", sb2,
                          b.dram["s_scr"][ti * 128:(ti + 1) * 128,
                                          c * 512:(c + 1) * 512],
                          f"dsi{c % 2}", reads=["s_scr"], writes=[sbn],
                          tag=f"dc_s{ti}_{c}")
                    b.v(lambda e, c=c, pa=pa: e.tensor_tensor(
                        t["scrA"][:, 0:512], pa,
                        t["kv_gb"][:, c * 512:(c + 1) * 512], ALU.mult),
                        reads=[pname, "kv_gb"], writes=["scrA"],
                        tag=f"dc_m{ti}_{c}")
                    b.s(lambda e: e.activation(
                        t["scrA"][:, 512:1024], t["scrA"][:, 0:512], AF.Silu),
                        reads=["scrA"], writes=["scrA"], tag=f"dc_si{ti}_{c}")
                    b.v(lambda e, sb2=sb2: e.tensor_tensor(
                        t["scrA"][:, 0:512], t["scrA"][:, 512:1024], sb2,
                        ALU.mult),
                        reads=["scrA", sbn], writes=["scrA"],
                        tag=f"dc_g{ti}_{c}")
                    b.v(lambda e, xr=xr: e.tensor_add(xr, xr,
                                                      t["scrA"][:, 0:512]),
                        reads=["scrA", xrn], writes=[xrn],
                        tag=f"dc_a{ti}_{c}")
                    b.dma("Y", out_dram[ti * 128:(ti + 1) * 128,
                                        c * 512:(c + 1) * 512], xr, "dout",
                          reads=[xrn], writes=["out"], tag=f"dco{ti}_{c}")


def build_stageD():
    nc = bass.Bass("TRN2", target_bir_lowering=False, debug=False)
    b = Builder(nc)
    x1 = b.din("x1", (TLOC, D))
    d1 = declare_l1(b)
    out = b.dout("out", (TLOC, D), F32)
    b.dram["s_scr"] = b.dint("s_scr", (TLOC, D), BF16)
    d = {"identb": b.din("identb", (128, 128), BF16),
         "ones_row": b.din("ones_row", (1, D), BF16)}
    t = alloc_common(b)
    alloc_ffn(b, t)
    t["competeb_sb"] = b.sbuf("competeb_sb", (16, 1), F32)
    t["inprojb_sb"] = b.sbuf("inprojb_sb", (128, 48), F32)
    t["prew_sb"] = t["inprojb_sb"][:, 16:32]
    t["totc"] = t["kv_loc"][:, 0:16]
    t["carry_sb"] = t["kv_loc"][:, 0:16]
    t["hs_sb"] = t["c_var"][:, 0:1]
    t["gs_sb"] = t["kv_loc"][:, 16:60]
    t["us_sb"] = t["kv_loc"][:, 60:104]
    ps = alloc_psum(b)
    for st in ("dx", "dmisc", "dmisc2"):
        b.p.dma_stream(st, mode="slot")
    b.p.dma_stream("dout", mode="barrier")
    for st in ("dws", "dpw", "dpwb", "dcb", "dib", "dif", "drow", "ds2s",
               "dsA", "dlA", "dsz", "dsT", "dmg", "dsU", "dum", "dcptF",
               "dso", "dsi", "dsC", "dlC", "dhs", "ddsb", "dgs", "dus",
               "dx0", "dx1", "dsi0", "dsi1",
               "wk0", "wk1", "wv0", "wv1", "wq0", "wq1", "wo0", "wo1"):
        b.p.dma_stream(st, mode="slot")
    load_consts(b, d, t)
    emit_ln(b, d, t, x1, t["ytm"], tag="ln2")
    emit_tp_plain(b, d, t, ps, tag="tpp")
    emit_l1_attn(b, d1, t, ps)
    emit_l1_ffn(b, d1, t, ps, out, x1)
    b.p.op("Y", lambda e: e.nop(), reads=["out"], writes=(), inc=False,
           tag="fin")
    return finalize(b, [])


def build_fused():
    nc = bass.Bass("TRN2", target_bir_lowering=False, debug=False)
    b = Builder(nc)
    d = declare_l0(b)
    d["inprojb_row"] = b.din("inprojb_row", (1, 3 * D))
    d["postw_bcast"] = b.din("postw_bcast", (128, D))
    d["identf"] = b.din("identf_in", (128, 128))
    d1 = declare_l1(b)
    out = b.dout("out", (TLOC, D), F32)
    d["s_scr"] = b.dint("s_scr", (TLOC, D), BF16)
    x1s = b.dint("x1_scr", (TLOC, D), F32)
    t = alloc_common(b)
    alloc_ffn(b, t)
    t["competeb_sb"] = b.sbuf("competeb_sb", (16, 1), F32)
    t["inprojb_sb"] = b.sbuf("inprojb_sb", (128, 48), F32)
    t["prew_sb"] = t["inprojb_sb"][:, 16:32]
    t["totc"] = t["kv_loc"][:, 0:16]
    t["carry_sb"] = t["kv_loc"][:, 0:16]
    t["hs_sb"] = t["c_var"][:, 0:1]
    t["gs_sb"] = t["kv_loc"][:, 16:60]
    t["us_sb"] = t["kv_loc"][:, 60:104]
    ps = alloc_psum(b)
    for st in ("dx", "dmisc", "dmisc2"):
        b.p.dma_stream(st, mode="slot")
    b.p.dma_stream("dout", mode="barrier")
    b.p.dma_stream("dx1", mode="slot")
    for st in ("dws", "dpw", "dpwb", "dcb", "dib", "dif", "drow", "ds2s",
               "dsA", "dlA", "dsz", "dsT", "dmg", "dsU", "dum", "dcptF",
               "dso", "dsi", "dsC", "dlC", "dhs", "ddsb", "dgs", "dus",
               "dx0", "dx1", "dsi0", "dsi1"):
        b.p.dma_stream(st, mode="slot")
    load_consts(b, d, t)
    emit_ln(b, d, t, d["x"], t["ytm"], tag="ln1")
    emit_cpt_prep(b, d, t)
    emit_tp_logits(b, d, t, ps, 0)
    emit_l0_attn_A(b, d, t, ps)
    emit_l0_ffn_prep(b, d, t)
    emit_ffn_topk(b, d, t, ps, 0)
    emit_l0_attn_BC(b, d, t, ps)
    emit_ffn_u(b, d, t, ps, 0)
    emit_ffn_q(b, d, t, ps, 0)
    emit_ffn_written(b, d, t, ps, 0)
    emit_ffn_scores(b, d, t, ps, 0)
    emit_ffn_xb(b, d, t, ps, 0)
    emit_tp_logits(b, d, t, ps, 1)
    emit_l0_ffn_iter(b, d, t, ps, 1)
    emit_l0_combine(b, d, t, x1s, fuse_ln=True)
    emit_tp_plain(b, d, t, ps, tag="tpp")
    emit_l1_attn(b, d1, t, ps)
    emit_l1_ffn(b, d1, t, ps, out, x1s)
    b.p.op("Y", lambda e: e.nop(), reads=["out"], writes=(), inc=False,
           tag="fin")
    return finalize(b, [])





_NC_CACHE = {}


def _get_nc(name, fn):
    if name not in _NC_CACHE:
        _NC_CACHE[name] = fn()
    return _NC_CACHE[name]


def kernel(**inputs):
    from concourse.bass_utils import run_bass_kernel_spmd
    ins = inputs
    _ = (ins.get("L0_attn_cos"), ins.get("L0_attn_sin"),
         ins.get("L0_ffn_ig_w"), ins.get("L0_ffn_ig_b"))
    P = host_prep_l0(ins)
    P["inprojb_row"] = np.ascontiguousarray(ins["L0_ffn_in_proj_b"][None, :])
    P["postw_bcast"] = np.ascontiguousarray(
        np.broadcast_to(ins["L0_ffn_norm_post_w"][None, :], (128, 2048)))
    P["identf_in"] = np.eye(128, dtype=np.float32)
    P.update(host_prep_l1(ins))
    shards = shard_inputs(np.asarray(ins["L0_attn_x"], np.float32))
    names = ("WqT", "WkT4", "WvT4", "WoT4", "inprojT", "inprojQ", "writeT",
             "outpT4", "competeT", "prew_col", "postw_col", "postw_row",
             "inprojb_col", "competeb_col", "writeb_row", "outpb_row",
             "workspace", "identb", "ones_row", "inprojb_row", "postw_bcast",
             "identf_in", "Wq1T", "Wk1T", "Wv1T", "Wo1T4", "gateT", "upT",
             "downT", "l1b_col", "bo_row", "gates_col", "ups_col", "ds_bcast")
    in_maps = []
    for c in range(8):
        m = {k: P[k] for k in names}
        m["x"] = shards[c]
        m["halfsel"] = np.full((128, 1), float(c % 2), np.float32)
        in_maps.append(m)
    nc = _get_nc("fused", build_fused)
    r = run_bass_kernel_spmd(nc, in_maps, core_ids=list(range(8)))
    out = np.zeros((4, 2048, 2048), np.float32)
    for c in range(8):
        b_, h = c // 2, c % 2
        out[b_, h * 1024:(h + 1) * 1024, :] = r.results[c]["out"]
    return out


# version 19
# speedup vs baseline: 1.0228x; 1.0228x over previous
import numpy as np
import concourse.bass as bass
import concourse.mybir as mybir

F32 = mybir.dt.float32
BF16 = mybir.dt.bfloat16
AF = mybir.ActivationFunctionType
ALU = mybir.AluOpType
AX = mybir.AxisListType

TLOC = 1024
NT = TLOC // 128
D = 2048
ND = D // 128
PAIRS = [[0, 1], [2, 3], [4, 5], [6, 7]]

PEND = object()
BARRIER = object()


class Op:
    __slots__ = ("eng", "fn", "waits", "inc", "tag")

    def __init__(self, eng, fn, waits, inc, tag):
        self.eng, self.fn, self.waits, self.inc, self.tag = eng, fn, waits, inc, tag


class Prog:
    def __init__(self, nc):
        self.nc = nc
        self.ops = []
        self.sem_counts = {}
        self.sem_step = {}
        self.sem_mode = {}
        self.observed = {e: {} for e in ("T", "V", "S", "G", "Y")}
        self.W = {}
        self.R = {}
        self.sem_names = []
        self._engine_sem = {"T": "sT", "V": "sV", "S": "sS", "G": "sG", "Y": "sY"}
        self._snap = {}
        self._pendT = []
        for s in ("sT", "sV", "sS", "sG", "sY"):
            self._declare_sem(s, 1)

    def _declare_sem(self, name, step, mode="barrier"):
        if name not in self.sem_counts:
            self.sem_counts[name] = 0
            self.sem_step[name] = step
            self.sem_names.append(name)
            self._snap[name] = {}
            self.sem_mode[name] = mode

    def dma_stream(self, name, mode="barrier"):
        self._declare_sem(name, 16, mode)

    def _need(self, eng, dep, waits, tag=""):
        sem, val = dep
        if val is PEND:
            if eng == "T":
                return
            raise RuntimeError(
                f"op {tag}: engine {eng} depends on un-inc'd T chain; "
                f"give the chain's last matmul inc=True")
        if val is BARRIER:
            val = self.sem_counts[sem]
        if eng == "T" and sem == "sT":
            return
        if self.observed[eng].get(sem, 0) >= val:
            return
        snap = self._snap[sem].get(val)
        if snap:
            for s2, v2 in snap.items():
                if self.observed[eng].get(s2, 0) < v2:
                    self.observed[eng][s2] = v2
        self.observed[eng][sem] = val
        waits.append((sem, val))

    def _buf_names(self, bufs):
        out = []
        for b in bufs:
            if b is None:
                continue
            if isinstance(b, str):
                out.append(b)
            else:
                t = getattr(b, "tensor", b)
                out.append(t.name if hasattr(t, "name") else str(t))
        return out

    def op(self, eng, fn, reads=(), writes=(), inc=True, sem=None, tag="",
           war=True):
        waits = []
        rnames = self._buf_names(reads)
        wnames = self._buf_names(writes)
        for n in rnames:
            if n in self.W:
                self._need(eng, self.W[n], waits, tag=tag + "/r:" + n)
        for n in wnames:
            if n in self.W:
                self._need(eng, self.W[n], waits, tag=tag + "/w:" + n)
            if war:
                for r in self.R.get(n, ()):
                    self._need(eng, r, waits, tag=tag + "/war:" + n)
        sem_name = sem if sem is not None else self._engine_sem[eng]
        assert sem_name in self.sem_counts, f"undeclared stream {sem_name}"
        is_dma_stream = sem is not None
        if inc:
            self.sem_counts[sem_name] += self.sem_step[sem_name]
            val = self.sem_counts[sem_name]
            inc_rec = (sem_name, self.sem_step[sem_name])
            snap = dict(self.observed[eng])
            snap[sem_name] = val
            self._snap[sem_name][val] = snap
            if is_dma_stream and self.sem_mode[sem_name] == "barrier":
                done = (sem_name, BARRIER)
            else:
                done = (sem_name, val)
            if eng == "T" and sem_name == "sT":
                for cell, key in self._pendT:
                    cur = cell[key]
                    if isinstance(cur, tuple) and cur[1] is PEND:
                        cell[key] = done
                self._pendT = []
        else:
            assert eng in ("T", "Y") and sem is None, "inc=False only for T chains / final nop"
            inc_rec = None
            done = (sem_name, PEND)
        for n in wnames:
            self.W[n] = done
            self.R[n] = []
            if done[1] is PEND:
                self._pendT.append((self.W, n))
        for n in rnames:
            lst = self.R.setdefault(n, [])
            lst.append(done)
            if done[1] is PEND:
                self._pendT.append((lst, len(lst) - 1))
        self.ops.append(Op(eng, fn, waits, inc_rec, tag))
        return done

    def emit(self, block, semobjs):
        streams = {e: [] for e in ("T", "V", "S", "G", "Y")}
        for o in self.ops:
            streams[o.eng].append(o)

        def run(engine_obj, ops):
            for o in ops:
                for sem, val in o.waits:
                    engine_obj.wait_ge(semobjs[sem], val)
                ins = o.fn(engine_obj)
                if o.inc is not None:
                    assert ins is not None, f"op {o.tag} returned no instruction"
                    ins.then_inc(semobjs[o.inc[0]], o.inc[1])

        if streams["T"]:
            @block.tensor
            def _(t):
                run(t, streams["T"])
        if streams["V"]:
            @block.vector
            def _(v):
                run(v, streams["V"])
        if streams["S"]:
            @block.scalar
            def _(s):
                run(s, streams["S"])
        if streams["G"]:
            @block.gpsimd
            def _(g):
                run(g, streams["G"])
        if streams["Y"]:
            @block.sync
            def _(y):
                run(y, streams["Y"])


class Builder:
    def __init__(self, nc):
        self.nc = nc
        self.p = Prog(nc)
        self.sb = {}
        self.psd = {}
        self.dram = {}
        self.input_names = []

    def sbuf(self, name, shape, dtype):
        h = self.nc.alloc_sbuf_tensor(name, list(shape), dtype)
        self.sb[name] = h
        return h

    def psum(self, name, shape, dtype=F32):
        h = self.nc.alloc_psum_tensor(name, list(shape), dtype)
        self.psd[name] = h
        return h

    def din(self, name, shape, dtype=F32):
        h = self.nc.dram_tensor(name, list(shape), dtype, kind="ExternalInput")
        self.dram[name] = h
        self.input_names.append(name)
        return h

    def dout(self, name, shape, dtype=F32):
        h = self.nc.dram_tensor(name, list(shape), dtype, kind="ExternalOutput")
        self.dram[name] = h
        return h

    def dint(self, name, shape, dtype=F32):
        h = self.nc.dram_tensor(name, list(shape), dtype)
        self.dram[name] = h
        return h

    def dma(self, eng, out, in_, stream, reads=(), writes=(), tag="dma", **kw):
        return self.p.op(
            eng, lambda e: e.dma_start(out=out, in_=in_, **kw),
            reads=reads, writes=writes, sem=stream, tag=tag)

    def mm(self, out, lhsT, rhs, start, stop, reads, writes, inc=None,
           tag="mm", tile_position=None, war=True):
        if inc is None:
            inc = stop
        return self.p.op(
            "T",
            lambda e: e.matmul(out, lhsT, rhs, start=start, stop=stop,
                               tile_position=tile_position),
            reads=reads, writes=writes, inc=inc, tag=tag, war=war)

    def tr(self, out, in_, ident, reads, writes, inc=True, tag="tr"):
        return self.p.op(
            "T", lambda e: e.transpose(out, in_, ident),
            reads=reads, writes=writes, inc=inc, tag=tag)

    def v(self, fn, reads, writes, inc=True, tag="v"):
        return self.p.op("V", fn, reads=reads, writes=writes, inc=inc, tag=tag)

    def s(self, fn, reads, writes, inc=True, tag="s"):
        return self.p.op("S", fn, reads=reads, writes=writes, inc=inc, tag=tag)

    def g(self, fn, reads, writes, inc=True, tag="g", sem=None):
        return self.p.op("G", fn, reads=reads, writes=writes, inc=inc, tag=tag,
                         sem=sem)



def tile_cols(W, cw):
    Din, O = W.shape
    a = W.reshape(Din // 128, 128, O // cw, cw)
    return np.ascontiguousarray(a.transpose(2, 1, 0, 3))


def tile_cols_halfk(W, cw):
    Din, O = W.shape
    a = W.reshape(2, Din // 256, 128, O // cw, cw)
    return np.ascontiguousarray(a.transpose(3, 0, 2, 1, 4))


def col_layout(w):
    return np.ascontiguousarray(w.reshape(-1, 128).T)


def host_prep_l0(ins):
    import ml_dtypes
    P = {}
    P["WqT"] = tile_cols(np.ascontiguousarray(ins["L0_attn_Wq"].T), 128)
    P["WkT4"] = tile_cols_halfk(np.ascontiguousarray(ins["L0_attn_Wk"].T), 512)
    P["WvT4"] = tile_cols_halfk(np.ascontiguousarray(ins["L0_attn_Wv"].T), 512)
    P["WoT4"] = tile_cols_halfk(np.ascontiguousarray(ins["L0_attn_Wo"].T), 512)
    P["inprojT"] = np.ascontiguousarray(ins["L0_ffn_in_proj_w"].T)
    P["inprojQ"] = tile_cols(P["inprojT"][:, 0:2048], 128)
    P["outpT4"] = tile_cols_halfk(np.ascontiguousarray(ins["L0_ffn_out_proj_w"].T), 512)
    P["writeT"] = np.ascontiguousarray(
        ins["L0_ffn_write_w"].T).astype(ml_dtypes.bfloat16)
    P["outpT"] = np.ascontiguousarray(ins["L0_ffn_out_proj_w"].T)
    P["competeT"] = np.ascontiguousarray(ins["L0_ffn_compete_w"].T)
    P["prew_col"] = col_layout(ins["L0_ffn_norm_pre_w"])
    P["postw_col"] = col_layout(ins["L0_ffn_norm_post_w"])
    P["postw_row"] = np.ascontiguousarray(ins["L0_ffn_norm_post_w"][None, :])
    P["inprojb_col"] = col_layout(ins["L0_ffn_in_proj_b"])
    P["competeb_col"] = np.ascontiguousarray(ins["L0_ffn_compete_b"][:, None])
    P["writeb_row"] = np.ascontiguousarray(ins["L0_ffn_write_b"][None, :])
    P["outpb_row"] = np.ascontiguousarray(ins["L0_ffn_out_proj_b"][None, :])
    P["workspace"] = np.ascontiguousarray(ins["L0_ffn_workspace"])
    P["identb"] = np.eye(128, dtype=ml_dtypes.bfloat16)
    P["ones_row"] = np.ones((1, 2048), dtype=ml_dtypes.bfloat16)
    return P


def shard_inputs(x_full):
    shards = []
    for c in range(8):
        b, h = c // 2, c % 2
        shards.append(np.ascontiguousarray(x_full[b, h * 1024:(h + 1) * 1024, :]))
    return shards



def declare_l0(b: Builder):
    d = {}
    d["x"] = b.din("x", (TLOC, D))
    for n, sh in [("WqT", (16, 128, 16, 128)), ("WkT4", (4, 2, 128, 8, 512)),
                  ("WvT4", (4, 2, 128, 8, 512)), ("WoT4", (4, 2, 128, 8, 512)),
                  ("inprojT", (D, 3 * D)), ("inprojQ", (16, 128, 16, 128)),
                  ("outpT4", (4, 2, 128, 8, 512)), ("competeT", (D, 16))]:
        d[n] = b.din(n, sh)
    d["writeT"] = b.din("writeT", (D, D), BF16)
    for n, sh in [("prew_col", (128, 16)), ("postw_col", (128, 16)),
                  ("postw_row", (1, D)), ("inprojb_col", (128, 48)),
                  ("competeb_col", (16, 1)), ("writeb_row", (1, D)),
                  ("outpb_row", (1, D)), ("workspace", (16, D))]:
        d[n] = b.din(n, sh)
    d["identb"] = b.din("identb", (128, 128), BF16)
    d["ones_row"] = b.din("ones_row", (1, D), BF16)
    d["ccA_i"] = b.dint("ccA_i", (128, 2064))
    d["ccA_o"] = b.dint("ccA_o", (128, 2064))
    d["ccT_i"] = b.dint("ccT_i", (16, 4))
    d["ccT_o"] = b.dint("ccT_o", (32, 4))
    d["ccU_i"] = b.dint("ccU_i", (16, D), BF16)
    d["ccU_o"] = b.dint("ccU_o", (16, D), BF16)
    d["ccZ_i"] = b.dint("ccZ_i", (16, 1))
    d["ccZ_o"] = b.dint("ccZ_o", (16, 1))
    d["zscr"] = b.dint("zscr", (16, TLOC))
    return d


def alloc_common(b: Builder):
    t = {}
    t["ytm"] = b.sbuf("ytm", (128, NT * D), F32)
    t["yT"] = b.sbuf("yT", (128, ND * TLOC), BF16)
    t["big1"] = b.sbuf("big1", (128, ND * TLOC), BF16)
    t["wA0"] = b.sbuf("wA0", (128, 2048), BF16)
    t["wA1"] = b.sbuf("wA1", (128, 2048), BF16)
    t["wB0"] = b.sbuf("wB0", (128, 2048), BF16)
    t["wB1"] = b.sbuf("wB1", (128, 2048), BF16)
    t["wC0"] = b.sbuf("wC0", (128, 4096), BF16)
    t["wC1"] = b.sbuf("wC1", (128, 4096), BF16)
    t["phik0"] = b.sbuf("phik0", (128, NT * 128), BF16)
    t["phik1"] = b.sbuf("phik1", (128, NT * 128), BF16)
    t["vext0"] = b.sbuf("vext0", (128, NT * 129), BF16)
    t["vext1"] = b.sbuf("vext1", (128, NT * 129), BF16)
    t["phiq0"] = t["phik0"]
    t["phiq1"] = t["phik1"]
    t["zb"] = t["vext0"]
    t["zmask"] = b.sbuf("zmask", (128, 256), BF16)
    t["kv_gb"] = b.sbuf("kv_gb", (128, 2064), BF16)
    t["kv_loc"] = b.sbuf("kv_loc", (128, 2064), F32)
    t["z_sb"] = b.sbuf("z_sb", (16, TLOC), F32)
    t["z_row"] = b.sbuf("z_row", (1, TLOC), BF16)
    t["xrot"] = b.sbuf("xrot", (128, D), F32)
    t["scrA"] = b.sbuf("scrA", (128, D), F32)
    for n in ("c_ssq", "c_nm", "c_var", "c_rstd", "c_tmp"):
        t[n] = b.sbuf(n, (128, 16), F32)
    t["c_eps5"] = b.sbuf("c_eps5", (128, 1), F32)
    t["c_eps6"] = b.sbuf("c_eps6", (128, 1), F32)
    t["identb"] = b.sbuf("identb_sb", (128, 128), BF16)
    t["ones"] = b.sbuf("ones_sb", (1, 128), BF16)
    return t


def load_consts(b, d, t):
    b.dma("Y", t["identb"][:, :], d["identb"][:, :], "dmisc", writes=["identb_sb"])
    b.dma("G", t["ones"][:, :], d["ones_row"][0:1, 0:128], "dmisc2", writes=["ones_sb"])
    b.v(lambda e: e.memset(t["c_eps5"][:, :], 1e-5), reads=[], writes=["c_eps5"])
    b.v(lambda e: e.memset(t["c_eps6"][:, :], 1e-6), reads=[], writes=["c_eps6"])


def emit_ln(b, d, t, src_dram, dst_ytm, eps=1e-5, tag="ln", src_dep=()):
    for ti in range(NT):
        xr = t["xrot"][:, 0:D]
        xname = "xrot"
        b.dma("Y", xr, src_dram[ti * 128:(ti + 1) * 128, :], "dx",
              writes=[xname], reads=list(src_dep), tag=f"{tag}_ld{ti}")
        sq = t["scrA"]
        b.s(lambda e, xr=xr, sq=sq, ti=ti: e.activation(
            sq[:, :], xr, AF.Square, accum_out=t["c_ssq"][:, ti:ti + 1]),
            reads=[xname], writes=["scrA", "c_ssq"], tag=f"{tag}_sq{ti}")
        b.v(lambda e, xr=xr, ti=ti: e.reduce_sum(
            t["c_nm"][:, ti:ti + 1], xr, AX.X),
            reads=[xname], writes=["c_nm"], tag=f"{tag}_sum{ti}")
        b.v(lambda e, ti=ti: e.tensor_scalar(
            t["c_nm"][:, ti:ti + 1], t["c_nm"][:, ti:ti + 1], -1.0 / D, None,
            ALU.mult), reads=["c_nm"], writes=["c_nm"], tag=f"{tag}_nm{ti}")
        b.v(lambda e, ti=ti: e.tensor_mul(
            t["c_tmp"][:, ti:ti + 1], t["c_nm"][:, ti:ti + 1],
            t["c_nm"][:, ti:ti + 1]),
            reads=["c_nm"], writes=["c_tmp"], tag=f"{tag}_msq{ti}")
        b.v(lambda e, ti=ti: e.tensor_scalar(
            t["c_var"][:, ti:ti + 1], t["c_ssq"][:, ti:ti + 1], 1.0 / D,
            t["c_tmp"][:, ti:ti + 1], ALU.mult, ALU.subtract),
            reads=["c_ssq", "c_tmp"], writes=["c_var"], tag=f"{tag}_var{ti}")
        b.s(lambda e, ti=ti: e.activation(
            t["c_var"][:, ti:ti + 1], t["c_var"][:, ti:ti + 1], AF.Sqrt,
            bias=t["c_eps5"][:, 0:1]), reads=["c_var", "c_eps5"],
            writes=["c_var"], tag=f"{tag}_sd{ti}")
        b.v(lambda e, ti=ti: e.reciprocal(
            t["c_rstd"][:, ti:ti + 1], t["c_var"][:, ti:ti + 1]),
            reads=["c_var"], writes=["c_rstd"], tag=f"{tag}_rs{ti}")
        dst = dst_ytm[:, ti * D:(ti + 1) * D]
        b.v(lambda e, xr=xr, dst=dst, ti=ti: e.tensor_scalar(
            dst, xr, t["c_nm"][:, ti:ti + 1], t["c_rstd"][:, ti:ti + 1],
            ALU.add, ALU.mult),
            reads=[xname, "c_nm", "c_rstd"], writes=["ytm"],
            tag=f"{tag}_ap{ti}")


def emit_transpose_tm_fm(b, t, src, dst, psW, tag="tp"):
    for dj in range(ND):
        for ti in range(NT):
            b.tr(psW[:, ti * 128:(ti + 1) * 128],
                 src[:, ti * D + dj * 128: ti * D + dj * 128 + 128],
                 t["identb"][:, :],
                 reads=[src, "identb_sb"], writes=["psW"], inc=(ti == NT - 1),
                 tag=f"{tag}_t{dj}_{ti}")
        b.v(lambda e, dj=dj: e.tensor_copy(
            dst[:, dj * TLOC:(dj + 1) * TLOC], psW[:, :]),
            reads=["psW"], writes=[dst], tag=f"{tag}_ev{dj}")


def emit_phi_evac(b, t, ps, psname, dst, dstname, ncols, bias_col=None,
                  tag="phi"):
    scr1 = t["scrA"][:, :ncols]
    scr2 = t["scrA"][:, 512:512 + ncols]
    if bias_col is None:
        b.s(lambda e: e.activation(scr1, ps, AF.Relu, scale=-1.0),
            reads=[psname], writes=["scrA"], tag=f"{tag}_r")
    else:
        b.s(lambda e: e.activation(scr1, ps, AF.Relu, scale=-1.0,
                                   bias=bias_col[1]),
            reads=[psname, bias_col[2]], writes=["scrA"], tag=f"{tag}_r")
    b.s(lambda e: e.activation(scr2, scr1, AF.Exp, scale=-1.0),
        reads=["scrA"], writes=["scrA"], tag=f"{tag}_e")
    if bias_col is None:
        b.v(lambda e: e.scalar_tensor_tensor(dst, ps, 0.0, scr2, ALU.max,
                                             ALU.add),
            reads=[psname, "scrA"], writes=[dstname], tag=f"{tag}_c")
    else:
        b.v(lambda e: e.tensor_scalar(scr1, ps, bias_col[0], 0.0, ALU.add,
                                      ALU.max),
            reads=[psname, bias_col[2]], writes=["scrA"], tag=f"{tag}_m")
        b.v(lambda e: e.tensor_add(dst, scr1, scr2),
            reads=["scrA"], writes=[dstname], tag=f"{tag}_c")




BANKS8 = None


def psum_banks8(ps):
    return [
        (ps["A0"][:, 0:512], "psA0"), (ps["A1"][:, 0:512], "psA1"),
        (ps["A2"][:, 0:512], "psA2"), (ps["A3"][:, 0:512], "psA3"),
        (ps["B"][:, 0:512], "psB"), (ps["B"][:, 512:1024], "psB"),
        (ps["K"][:, 0:512], "psK"),
        (ps["W"][:, :].bitcast(F32)[:, 0:512], "psW"),
    ]


def emit_colblock_out(b, t, ps, wdram4, lhsT_fn, lhs_deps, bias_dram_row,
                      evac_fn, tag):
    banks = psum_banks8(ps)
    for cb in range(4):
        if bias_dram_row is not None and cb % 2 == 0:
            b.dma("G", t["z_row"][0:1, :],
                  bias_dram_row[0:1, cb * 512:cb * 512 + 1024], "drow",
                  reads=[], writes=["z_row"], tag=f"{tag}_br{cb}")
        for kh in range(2):
            sl = (2 * cb + kh) % 2
            b.dma("G", bass.AP(t[f"wC{sl}"], 0, [[4096, 128], [512, 8], [1, 512]]),
                  wdram4[cb, kh], f"wo{sl}", writes=[f"wC{sl}"],
                  tag=f"{tag}_w{cb}_{kh}")
            for ti in range(NT):
                pa, pname = banks[ti]
                for kd in range(8):
                    last = (bias_dram_row is None and kh == 1 and kd == 7)
                    b.mm(pa, lhsT_fn(kh * 8 + kd, ti),
                         t[f"wC{sl}"][:, kd * 512:(kd + 1) * 512],
                         start=(kh == 0 and kd == 0), stop=last,
                         inc=(kd == 7),
                         reads=list(lhs_deps) + [f"wC{sl}"], writes=[pname],
                         tag=f"{tag}_m{cb}_{kh}_{ti}")
        for ti in range(NT):
            pa, pname = banks[ti]
            if bias_dram_row is not None:
                b.mm(pa, t["ones"][0:1, 0:128],
                     t["z_row"][0:1, (cb % 2) * 512:(cb % 2) * 512 + 512],
                     start=False, stop=True,
                     reads=["ones_sb", "z_row"], writes=[pname],
                     tag=f"{tag}_b{cb}_{ti}")
            evac_fn(cb, ti, pa, pname)


def emit_l0_attn(b, d, t, ps):
    emit_l0_attn_A(b, d, t, ps)
    emit_l0_attn_BC(b, d, t, ps)


def emit_l0_attn_A(b, d, t, ps):
    p = b.p
    for st in ("wk0", "wk1", "wv0", "wv1", "wq0", "wq1", "wo0", "wo1",
               "hw0", "hw1"):
        p.dma_stream(st, mode="slot")
    p.dma_stream("dcc", mode="barrier")
    for sl in range(2):
        vext = t[f"vext{sl}"]
        b.v(lambda e, vext=vext: e.memset(
            bass.AP(vext, 128, [[vext.shape[1], 128], [129, NT]]), 1.0),
            reads=[], writes=[f"vext{sl}"], tag=f"vones{sl}")

    pk = t["big1"]
    vg_off = 4096
    for ti in range(NT):
        b.v(lambda e, ti=ti: e.memset(
            bass.AP(t["big1"], vg_off + ti * 516 + 128,
                    [[ND * TLOC, 128], [129, 4]]), 1.0),
            reads=[], writes=["big1"], tag=f"vone{ti}")
    for g in range(4):
        for part, wnm in ((0, "WkT4"), (1, "WvT4")):
            for kh in range(2):
                sl = kh
                wc = t[f"wC{sl}"]
                b.dma("G", bass.AP(wc, 0, [[4096, 128], [512, 8], [1, 512]]),
                      d[wnm][g, kh], f"wo{sl}", writes=[f"wC{sl}"],
                      tag=f"ldkv{g}_{part}_{kh}")
            for ti in range(NT):
                pa = ps[f"A{ti % 4}"]
                for kh in range(2):
                    for kd in range(8):
                        b.mm(pa[:, 0:512],
                             t["yT"][:, (kh * 8 + kd) * TLOC + ti * 128:
                                     (kh * 8 + kd) * TLOC + ti * 128 + 128],
                             t[f"wC{kh}"][:, kd * 512:(kd + 1) * 512],
                             start=(kh == 0 and kd == 0),
                             stop=(kh == 1 and kd == 7),
                             reads=["yT", f"wC{kh}"], writes=[f"psA{ti % 4}"],
                             tag=f"kv{g}_{part}_{ti}")
                if part == 0:
                    emit_phi_evac(b, t, pa[:, 0:512], f"psA{ti % 4}",
                                  pk[:, ti * 512:(ti + 1) * 512], "big1", 512,
                                  tag=f"phk{g}_{ti}")
                else:
                    b.v(lambda e, ti=ti, pa=pa: e.tensor_copy(
                        bass.AP(t["big1"], vg_off + ti * 516,
                                [[ND * TLOC, 128], [129, 4], [1, 128]]),
                        bass.AP(pa, 0, [[512, 128], [128, 4], [1, 128]])),
                        reads=[f"psA{ti % 4}"], writes=["big1"],
                        tag=f"vev{g}_{ti}")
        kvbanks = [(ps["K"][:, 0:129], "psK"), (ps["B"][:, 0:129], "psB"),
                   (ps["B"][:, 512:641], "psB")]
        for hl in range(4):
            h = g * 4 + hl
            pa, pan = kvbanks[hl % 3]
            for ti in range(NT):
                b.mm(pa,
                     pk[:, ti * 512 + hl * 128: ti * 512 + hl * 128 + 128],
                     bass.AP(t["big1"], vg_off + ti * 516 + hl * 129,
                             [[ND * TLOC, 128], [1, 129]]),
                     start=(ti == 0), stop=(ti == NT - 1),
                     reads=["big1"], writes=[pan],
                     tag=f"kvm{h}_{ti}")
            b.v(lambda e, h=h, pa=pa: e.tensor_copy(
                t["kv_loc"][:, h * 129:(h + 1) * 129], pa),
                reads=[pan], writes=["kv_loc"], tag=f"kvev{h}")

    b.dma("Y", d["ccA_i"][:, :], t["kv_loc"][:, :], "dsA", reads=["kv_loc"],
          writes=["ccA_i"], tag="kv_out")
    p._declare_sem("cc0", 1)
    b.g(lambda e: e.collective_compute(
        "AllReduce", ALU.add, replica_groups=PAIRS,
        ins=[d["ccA_i"].ap().opt()], outs=[d["ccA_o"].ap().opt()]),
        reads=["ccA_i"], writes=["ccA_o"], sem="cc0", tag="cc_kv")


def emit_l0_attn_BC(b, d, t, ps):
    b.dma("G", t["kv_gb"][:, :], d["ccA_o"][:, :], "dlA", reads=["ccA_o"],
          writes=["kv_gb"], tag="kv_in")
    b.g(lambda e: e.memset(t["zmask"][:, :], 0.0), reads=[], writes=["zmask"],
        tag="zm0")
    for h in range(16):
        b.v(lambda e, h=h: e.tensor_copy(
            t["zmask"][:, h * 16 + h: h * 16 + h + 1],
            t["kv_gb"][:, h * 129 + 128: h * 129 + 129]),
            reads=["kv_gb", "zmask"], writes=["zmask"], tag=f"zm{h}")

    for h in range(16):
        sl = h % 2
        wq = t[f"wA{sl}"]
        phiq = t[f"phiq{sl}"]
        b.dma("G", bass.AP(wq, 0, [[2048, 128], [128, 16], [1, 128]]),
              d["WqT"][h], f"wq{sl}",
              writes=[f"wA{sl}"], tag=f"ldwq{h}")
        for c in range(2):
            for kd in range(ND):
                b.mm(ps["A0"][:, :512],
                     wq[:, kd * 128:(kd + 1) * 128],
                     t["yT"][:, kd * TLOC + c * 512: kd * TLOC + c * 512 + 512],
                     start=(kd == 0), stop=(kd == ND - 1),
                     reads=["yT", f"wA{sl}"], writes=["psA0"],
                     tag=f"qproj{h}_{c}")
            emit_phi_evac(b, t, ps["A0"][:, :512], "psA0",
                          phiq[:, c * 512:(c + 1) * 512], f"phik{sl}", 512,
                          tag=f"phiq{h}_{c}")
        for c in range(2):
            b.mm(ps["B"][0:16, c * 512:(c + 1) * 512],
                 t["zmask"][:, h * 16:(h + 1) * 16],
                 phiq[:, c * 512:(c + 1) * 512],
                 start=(h == 0), stop=(h == 15),
                 reads=["zmask", f"phik{sl}"], writes=["psB"],
                 tag=f"zmm{h}_{c}")
        for c in range(2):
            for _ in range(1):
                b.mm(ps["A1"][:, :512], t["kv_gb"][:, h * 129: h * 129 + 128],
                     phiq[:, c * 512:(c + 1) * 512], start=True, stop=True,
                     reads=["kv_gb", f"phik{sl}"], writes=["psA1"],
                     tag=f"pre{h}_{c}")
            b.v(lambda e, h=h, c=c: e.tensor_copy(
                t["big1"][:, h * TLOC + c * 512: h * TLOC + (c + 1) * 512],
                ps["A1"][:, :512]),
                reads=["psA1"], writes=["big1"], tag=f"preev{h}_{c}")
    b.v(lambda e: e.tensor_scalar(t["z_sb"][:, :], ps["B"][0:16, :], 1e-6,
                                  None, ALU.add),
        reads=["psB"], writes=["z_sb"], tag="zadd")
    b.v(lambda e: e.reciprocal(t["z_sb"][:, :], t["z_sb"][:, :]),
        reads=["z_sb"], writes=["z_sb"], tag="zrec")
    b.dma("Y", d["zscr"][:, :], t["z_sb"][:, :], "dsz", reads=["z_sb"],
          writes=["zscr"], tag="zrow_o")
    for h in range(16):
        b.dma("G", t["z_row"][0:1, :], d["zscr"][h:h + 1, :], "dcc",
              reads=["zscr"], writes=["z_row"], tag=f"zrow_i{h}")
        for c in range(2):
            b.mm(ps["B"][:, c * 512:(c + 1) * 512], t["ones"][0:1, 0:128],
                 t["z_row"][0:1, c * 512:(c + 1) * 512],
                 start=True, stop=True,
                 reads=["ones_sb", "z_row"], writes=["psB"], tag=f"zb{h}_{c}")
            b.v(lambda e, h=h, c=c: e.tensor_copy(
                t["zb"][:, c * 512:(c + 1) * 512], ps["B"][:, c * 512:(c + 1) * 512]),
                reads=["psB"], writes=["vext0"], tag=f"zbev{h}_{c}")
        b.v(lambda e, h=h: e.tensor_mul(
            t["big1"][:, h * TLOC:(h + 1) * TLOC],
            t["big1"][:, h * TLOC:(h + 1) * TLOC], t["zb"][:, 0:TLOC]),
            reads=["big1", "vext0"], writes=["big1"], tag=f"zap{h}")

    def lhs_c(kd, ti):
        return t["big1"][:, kd * TLOC + ti * 128: kd * TLOC + ti * 128 + 128]

    def evac_c(cb, ti, pa, pname):
        sb = t["phiq0"][:, (ti % 2) * 512:(ti % 2) * 512 + 512]
        b.v(lambda e, pa=pa, sb=sb: e.tensor_copy(sb, pa),
            reads=[pname], writes=["phik0"], tag=f"sev{cb}_{ti}")
        b.dma("Y", d["s_scr"][ti * 128:(ti + 1) * 128,
                              cb * 512:(cb + 1) * 512], sb, "dso",
              reads=["phik0"], writes=["s_scr"], tag=f"sst{cb}_{ti}")

    emit_colblock_out(b, t, ps, d["WoT4"], lhs_c, ["big1"], None, evac_c, "sC")


def alloc_psum(b: Builder):
    ps = {}
    ps["A0"] = b.psum("psA0", (128, 512), F32)
    ps["A1"] = b.psum("psA1", (128, 512), F32)
    ps["A2"] = b.psum("psA2", (128, 512), F32)
    ps["A3"] = b.psum("psA3", (128, 512), F32)
    ps["B"] = b.psum("psB", (128, 1024), F32)
    ps["K"] = b.psum("psK", (128, 512), F32)
    ps["W"] = b.psum("psW", (128, 1024), BF16)
    return ps


def finalize(b: Builder, out_waits):
    nc = b.nc
    from contextlib import ExitStack
    es = ExitStack()
    semobjs = {}
    with es:
        for name in b.p.sem_names:
            semobjs[name] = es.enter_context(nc.semaphore(name))
        block = es.enter_context(nc.Block())
        b.p.op("Y", lambda e: e.nop(), reads=list(out_waits), writes=(),
               inc=False, tag="final_wait") if False else None
        b.p.emit(block, semobjs)
    return nc


def build_stageB():
    nc = bass.Bass("TRN2", target_bir_lowering=False, debug=False)
    b = Builder(nc)
    d = declare_l0(b)
    so = b.dout("s_out", (TLOC, D), BF16)
    t = alloc_common(b)
    ps = alloc_psum(b)
    for st in ("dx", "dmisc", "dmisc2", "dout"):
        b.p.dma_stream(st)
    load_consts(b, d, t)
    emit_ln(b, d, t, d["x"], t["ytm"], tag="ln1")
    emit_transpose_tm_fm(b, t, t["ytm"], t["yT"], ps["W"], tag="tp1")
    emit_l0_attn(b, d, t, ps)
    for ti in range(NT):
        b.dma("Y", so[ti * 128:(ti + 1) * 128, :],
              t["sbuf_s"][:, ti * D:(ti + 1) * D], "dout",
              reads=["sbuf_s"], writes=["s_out"], tag=f"so{ti}")
    b.p.op("Y", lambda e: e.nop(), reads=["s_out"], writes=(), inc=False,
           tag="fin")
    return finalize(b, [])


def alloc_ffn(b: Builder, t):
    t["sm16a"] = b.sbuf("sm16a", (16, 1024), BF16)
    t["identf"] = b.sbuf("identf", (128, 128), F32)
    t["mcols"] = b.sbuf("mcols", (16, 16), F32)
    t["mg8"] = b.sbuf("mg8", (16, 8), F32)
    t["cptT"] = t["zmask"]
    t["ws"] = None
    t["pwb"] = None
    t["u_sm"] = None
    t["tkw"] = None
    t["logit_sm"] = None
    t["eT"] = None
    t["uT"] = b.sbuf("uT", (128, 256), BF16)
    t["cpth"] = b.sbuf("cpth", (128, 256), BF16)
    t["cptl"] = b.sbuf("cptl", (128, 256), BF16)
    t["cptF"] = None
    t["yloT"] = None
    t["wsT"] = t["uT"]


def emit_l0_ffn_iter(b, d, t, ps, it, dbg=None):
    emit_ffn_topk(b, d, t, ps, it, dbg)
    emit_ffn_u(b, d, t, ps, it, dbg)
    emit_ffn_q(b, d, t, ps, it)
    emit_ffn_written(b, d, t, ps, it, dbg)
    emit_ffn_scores(b, d, t, ps, it, dbg)
    emit_ffn_xb(b, d, t, ps, it, dbg)


def _ffn_env(b, d, t, ps, it):
    class E: pass
    env = E()
    env.ws = t["kv_loc"][0:16, 0:2048]
    env.u_sm = t["wC1"][0:16, 2048:4096]
    env.tkw = t["z_sb"][0:16, 0:1024]
    env.logit_sm = t["xrot"][0:16, 0:1024]
    env.e_bf = t["sm16a"]
    env.eT = t["phik0"][:, 0:128]
    env.att_s = t["vext1"]
    env.qT = t["phik1"]
    env.sA = t["scrA"]
    return env


def emit_ffn_topk(b, d, t, ps, it, dbg=None):
    p = b.p
    env = _ffn_env(b, d, t, ps, it)
    ws, u_sm, tkw, logit_sm = env.ws, env.u_sm, env.tkw, env.logit_sm
    e_bf, eT, att_s, qT, sA = env.e_bf, env.eT, env.att_s, env.qT, env.sA
    tks = t["xrot"][:, 1024:2048]

    if dbg:
        b.dma("Y", dbg["lg"].ap(), logit_sm[:, :], "dout", reads=["xrot"],
              writes=["dbg_lg"], tag="d_lg")
    b.v(lambda e: e.tensor_copy(tkw, logit_sm[:, :]),
        reads=["xrot"], writes=["z_sb"], tag=f"f{it}tk0")
    for i in range(4):
        b.v(lambda e, i=i: e.reduce_max(
            t["mcols"][:, i:i + 1], tkw, AX.X),
            reads=["z_sb"], writes=["mcols"], tag=f"f{it}tkm{i}")
        if i < 3:
            b.v(lambda e, i=i: e.tensor_scalar(
                tks[0:16, 0:1024], tkw, t["mcols"][:, i:i + 1], None,
                ALU.is_ge), reads=["z_sb", "mcols"], writes=["xrot"],
                tag=f"f{it}tke{i}")
            b.v(lambda e: e.scalar_tensor_tensor(
                tkw, tks[0:16, 0:1024], -1e30, tkw, ALU.mult, ALU.add),
                reads=["xrot", "z_sb"], writes=["z_sb"], tag=f"f{it}tkr{i}")
    b.dma("Y", d["ccT_i"][:, :], t["mcols"][:, 0:4], "dsT", reads=["mcols"],
          writes=["ccT_i"], tag=f"f{it}tko")
    p._declare_sem(f"ccT{it}", 1)
    b.g(lambda e: e.collective_compute(
        "AllGather", ALU.bypass, replica_groups=PAIRS,
        ins=[d["ccT_i"].ap().opt()], outs=[d["ccT_o"].ap().opt()]),
        reads=["ccT_i"], writes=["ccT_o"], sem=f"ccT{it}", tag=f"f{it}ccT")
    b.dma("G", bass.AP(t["mg8"], 0, [[8, 16], [4, 2], [1, 4]]),
          bass.AP(d["ccT_o"], 0, [[4, 16], [64, 2], [1, 4]]),
          "dmg", reads=["ccT_o"], writes=["mg8"], tag=f"f{it}tki")
    b.v(lambda e: e.tensor_copy(tks[0:16, 0:8], t["mg8"][:, :]),
        reads=["mg8"], writes=["xrot"], tag=f"f{it}mg0")
    for i in range(4):
        b.v(lambda e, i=i: e.reduce_max(
            t["mcols"][:, 8 + i: 9 + i], tks[0:16, 0:8], AX.X),
            reads=["xrot"], writes=["mcols"], tag=f"f{it}mgm{i}")
        if i < 3:
            b.v(lambda e, i=i: e.tensor_scalar(
                tks[0:16, 8:16], tks[0:16, 0:8], t["mcols"][:, 8 + i:9 + i],
                None, ALU.is_ge), reads=["xrot", "mcols"], writes=["xrot"],
                tag=f"f{it}mge{i}")
            b.v(lambda e: e.scalar_tensor_tensor(
                tks[0:16, 0:8], tks[0:16, 8:16], -1e30, tks[0:16, 0:8],
                ALU.mult, ALU.add),
                reads=["xrot"], writes=["xrot"], tag=f"f{it}mgr{i}")


def emit_ffn_u(b, d, t, ps, it, dbg=None):
    p = b.p
    env = _ffn_env(b, d, t, ps, it)
    ws, u_sm, tkw, logit_sm = env.ws, env.u_sm, env.tkw, env.logit_sm
    e_bf, eT, att_s, qT, sA = env.e_bf, env.eT, env.att_s, env.qT, env.sA

    b.v(lambda e: e.tensor_scalar(
        sA[0:16, 0:1024], logit_sm[:, :], t["mcols"][:, 8:9], None,
        ALU.subtract), reads=["xrot", "mcols"], writes=["scrA"],
        tag=f"f{it}es")
    b.s(lambda e: e.activation(tkw, sA[0:16, 0:1024], AF.Exp),
        reads=["scrA"], writes=["z_sb"], tag=f"f{it}ee")
    b.v(lambda e: e.tensor_scalar(
        sA[0:16, 0:1024], logit_sm[:, :], t["mcols"][:, 11:12], None,
        ALU.is_ge), reads=["xrot", "mcols"], writes=["scrA"], tag=f"f{it}em")
    b.v(lambda e: e.tensor_mul(tkw, tkw, sA[0:16, 0:1024]),
        reads=["z_sb", "scrA"], writes=["z_sb"], tag=f"f{it}emm")
    b.v(lambda e: e.tensor_copy(e_bf[:, :], tkw),
        reads=["z_sb"], writes=["sm16a"], tag=f"f{it}ec")
    b.v(lambda e: e.tensor_scalar(
        t["mcols"][:, 0:4], t["mcols"][:, 8:12], t["mcols"][:, 8:9], None,
        ALU.subtract), reads=["mcols"], writes=["mcols"], tag=f"f{it}z0")
    b.s(lambda e: e.activation(t["mcols"][:, 0:4], t["mcols"][:, 0:4],
                               AF.Exp),
        reads=["mcols"], writes=["mcols"], tag=f"f{it}z1")
    b.v(lambda e: e.reduce_sum(t["mcols"][:, 13:14], t["mcols"][:, 0:4],
                               AX.X),
        reads=["mcols"], writes=["mcols"], tag=f"f{it}z2")
    for ti in range(NT):
        b.tr(ps["W"][:, ti * 16:(ti + 1) * 16],
             e_bf[:, ti * 128:(ti + 1) * 128], t["identb"][0:16, 0:16],
             reads=["sm16a", "identb_sb"], writes=["psW"], inc=(ti == NT - 1),
             tag=f"f{it}etr{ti}")
    b.v(lambda e: e.tensor_copy(eT, ps["W"][:, 0:128]),
        reads=["psW"], writes=["phik0"], tag=f"f{it}etc")
    for c in range(4):
        for ti in range(NT):
            ybs = t["phik1"][:, (ti % 2) * 512:(ti % 2) * 512 + 512]
            b.v(lambda e, ybs=ybs, ti=ti, c=c: e.tensor_copy(
                ybs, t["ytm"][:, ti * D + c * 512: ti * D + (c + 1) * 512]),
                reads=["ytm"], writes=["phik1"], tag=f"f{it}ubc{c}_{ti}")
            b.mm(ps[f"A{c}"][0:16, 0:512], eT[:, ti * 16:(ti + 1) * 16],
                 ybs, start=(ti == 0), stop=(ti == NT - 1), inc=True,
                 reads=["phik0", "phik1"], writes=[f"psA{c}"],
                 tag=f"f{it}up{c}_{ti}")
        b.v(lambda e, c=c: e.tensor_copy(
            u_sm[:, c * 512:(c + 1) * 512], ps[f"A{c}"][0:16, 0:512]),
            reads=[f"psA{c}"], writes=["wC1"], tag=f"f{it}upe{c}")


def emit_ffn_written(b, d, t, ps, it, dbg=None):
    p = b.p
    env = _ffn_env(b, d, t, ps, it)
    ws, u_sm, tkw, logit_sm = env.ws, env.u_sm, env.tkw, env.logit_sm
    e_bf, eT, att_s, qT, sA = env.e_bf, env.eT, env.att_s, env.qT, env.sA
    if it == 0:
        b.g(lambda e: e.memset(t["vext0"][:, 0:512], 0.0), reads=[],
            writes=["vext0"], tag="kbd0")
        b.g(lambda e: e.memset(t["kv_gb"][0:32, 0:2048], 0.0), reads=[],
            writes=["kv_gb"], tag="vbd0")

    b.dma("Y", d["ccU_i"][:, :], u_sm, "dsU", reads=["wC1"], writes=["ccU_i"],
          tag=f"f{it}uo")
    p._declare_sem(f"ccU{it}", 1)
    b.g(lambda e: e.collective_compute(
        "AllReduce", ALU.add, replica_groups=PAIRS,
        ins=[d["ccU_i"].ap().opt()], outs=[d["ccU_o"].ap().opt()]),
        reads=["ccU_i"], writes=["ccU_o"], sem=f"ccU{it}",
        tag=f"f{it}ccU")
    b.dma("G", u_sm, d["ccU_o"][:, :], "dum", reads=["ccU_o"], writes=["wC1"],
          tag=f"f{it}ui")
    b.v(lambda e: e.reciprocal(t["mcols"][:, 14:15], t["mcols"][:, 13:14]),
        reads=["mcols"], writes=["mcols"], tag=f"f{it}zr")
    b.v(lambda e: e.tensor_scalar(
        u_sm, u_sm, t["mcols"][:, 14:15], None, ALU.mult),
        reads=["wC1", "mcols"], writes=["wC1"], tag=f"f{it}us")
    if dbg:
        b.dma("Y", dbg["mc"].ap(), t["mcols"][:, :], "dout", reads=["mcols"],
              writes=["dbg_mc"], tag="d_mc")
        b.dma("Y", dbg["u"].ap(), u_sm, "dout", reads=["wC1"],
              writes=["dbg_u"], tag="d_u")
    for half in range(2):
        for j in range(8):
            kd = half * 8 + j
            b.tr(ps["W"][:, j * 16:(j + 1) * 16],
                 u_sm[:, kd * 128:(kd + 1) * 128], t["identb"][0:16, 0:16],
                 reads=["wC1", "identb_sb"], writes=["psW"], inc=(j == 7),
                 tag=f"f{it}utr{kd}")
        b.v(lambda e, half=half: e.tensor_copy(
            t["uT"][:, half * 128:(half + 1) * 128], ps["W"][:, 0:128]),
            reads=["psW"], writes=["uT"], tag=f"f{it}utc{half}")
    for kd in range(ND):
        sl = kd % 2
        b.dma("Y", t[f"wB{sl}"][:, :],
              d["writeT"][kd * 128:(kd + 1) * 128, :], f"hw{sl}",
              writes=[f"wB{sl}"], tag=f"f{it}lww{kd}")
        for c in range(4):
            b.mm(ps[f"A{c}"][0:16, 0:512],
                 t["uT"][:, (kd % 8) * 16 + (kd // 8) * 128:
                         (kd % 8) * 16 + (kd // 8) * 128 + 16],
                 t[f"wB{sl}"][:, c * 512:(c + 1) * 512],
                 start=(kd == 0), stop=False, inc=(c == 3),
                 reads=["uT", f"wB{sl}"], writes=[f"psA{c}"],
                 tag=f"f{it}wr{kd}_{c}")
    for c in range(4):
        half = c // 2
        if c % 2 == 0:
            b.dma("G", t["z_row"][0:1, :],
                  d["writeb_row"][0:1, half * 1024:(half + 1) * 1024], "dcc",
                  reads=[], writes=["z_row"], tag=f"f{it}lwb{half}")
        b.mm(ps[f"A{c}"][0:16, 0:512], t["ones"][0:1, 0:16],
             t["z_row"][0:1, (c % 2) * 512:(c % 2) * 512 + 512],
             start=False, stop=True,
             reads=["ones_sb", "z_row"], writes=[f"psA{c}"],
             tag=f"f{it}wb{c}")
    for c in range(4):
        b.v(lambda e, c=c: e.tensor_tensor(
            ws[:, c * 512:(c + 1) * 512], ps[f"A{c}"][0:16, 0:512],
            ws[:, c * 512:(c + 1) * 512], ALU.add),
            reads=[f"psA{c}", "kv_loc"], writes=["kv_loc"],
            tag=f"f{it}wsa{c}")
    b.s(lambda e: e.activation(ps["B"][0:16, 0:1024], ws[:, 0:1024],
                               AF.Square, accum_out=t["mcols"][:, 15:16]),
        reads=["kv_loc"], writes=["psB", "mcols"], tag=f"f{it}wsq0")
    b.s(lambda e: e.activation(ps["B"][0:16, 0:1024], ws[:, 1024:2048],
                               AF.Square, accum_out=t["mcols"][:, 7:8]),
        reads=["kv_loc"], writes=["psB", "mcols"], tag=f"f{it}wsq1")
    b.v(lambda e: e.tensor_scalar(
        t["mcols"][:, 15:16], t["mcols"][:, 15:16], 1.0,
        t["mcols"][:, 7:8], ALU.mult, ALU.add),
        reads=["mcols"], writes=["mcols"], tag=f"f{it}wsq2")
    b.v(lambda e: e.tensor_scalar(
        t["mcols"][:, 15:16], t["mcols"][:, 15:16], 1.0 / D, None, ALU.mult),
        reads=["mcols"], writes=["mcols"], tag=f"f{it}wsq3")
    b.s(lambda e: e.activation(t["mcols"][:, 15:16], t["mcols"][:, 15:16],
                               AF.Sqrt, bias=t["c_eps6"][0:16, 0:1]),
        reads=["mcols", "c_eps6"], writes=["mcols"], tag=f"f{it}wsq4")
    b.v(lambda e: e.reciprocal(t["mcols"][:, 15:16], t["mcols"][:, 15:16]),
        reads=["mcols"], writes=["mcols"], tag=f"f{it}wsq5")
    for hf in range(2):
        b.dma("G", t["vext1"][:, 0:1024],
              d["postw_bcast"][:, hf * 1024:(hf + 1) * 1024], "dpwb",
              writes=["vext1"], tag=f"f{it}pw{hf}")
        b.v(lambda e, hf=hf: e.scalar_tensor_tensor(
            ws[:, hf * 1024:(hf + 1) * 1024],
            ws[:, hf * 1024:(hf + 1) * 1024], t["mcols"][:, 15:16],
            t["vext1"][0:16, 0:1024], ALU.mult, ALU.mult),
            reads=["kv_loc", "mcols", "vext1"], writes=["kv_loc"],
            tag=f"f{it}wsn{hf}")
    if dbg:
        b.dma("Y", dbg["ws"].ap(), ws, "dout", reads=["kv_loc"],
              writes=["dbg_ws"], tag="d_ws")
    for half in range(2):
        b.v(lambda e, half=half: e.tensor_copy(
            t["sm16a"][:, :], ws[:, half * 1024:(half + 1) * 1024]),
            reads=["kv_loc"], writes=["sm16a"], tag=f"f{it}wsbc{half}")
        for j in range(8):
            b.tr(ps["W"][:, j * 16:(j + 1) * 16],
                 t["sm16a"][:, j * 128:(j + 1) * 128], t["identb"][0:16, 0:16],
                 reads=["sm16a", "identb_sb"], writes=["psW"], inc=(j == 7),
                 tag=f"f{it}wst{half}_{j}")
        b.v(lambda e, half=half: e.tensor_copy(
            t["wsT"][:, half * 128:(half + 1) * 128], ps["W"][:, 0:128]),
            reads=["psW"], writes=["wsT"], tag=f"f{it}wstc{half}")
    banksKV = [
        (ps["A0"], "psA0"), (ps["A1"], "psA1"),
        (ps["A2"], "psA2"), (ps["A3"], "psA3"),
        (ps["B"][:, 0:512], "psB"), (ps["B"][:, 512:1024], "psB"),
        (ps["K"][0:128, 0:512], "psK"), (ps["K"], "psK"),
    ]
    for kd in range(ND):
        sl = kd % 2
        b.dma("G", t[f"wC{sl}"][:, :],
              d["inprojT"][kd * 128:(kd + 1) * 128, 2048:6144],
              f"wo{sl}", writes=[f"wC{sl}"], tag=f"f{it}lkv{kd}")
        for part in range(2):
            for c in range(4):
                idx = part * 4 + c
                pa, pan = banksKV[idx]
                po = pa[32:48, 0:512] if idx == 7 else pa[0:16, 0:512]
                b.mm(po,
                     t["wsT"][:, (kd // 8) * 128 + (kd % 8) * 16:
                              (kd // 8) * 128 + (kd % 8) * 16 + 16],
                     t[f"wC{sl}"][:, part * 2048 + c * 512:
                                  part * 2048 + (c + 1) * 512],
                     start=(kd == 0), stop=False, inc=(c == 3),
                     reads=["wsT", f"wC{sl}"], writes=[pan],
                     tag=f"f{it}kv{part}_{kd}_{c}")
    for part, ofs in enumerate([2048, 4096]):
        for c in range(4):
            half = c // 2
            idx = part * 4 + c
            pa, pan = banksKV[idx]
            po = pa[32:48, 0:512] if idx == 7 else pa[0:16, 0:512]
            if c % 2 == 0:
                b.dma("G", t["z_row"][0:1, :],
                      d["inprojb_row"][0:1, ofs + half * 1024:
                                       ofs + half * 1024 + 1024], "drow",
                      reads=[], writes=["z_row"], tag=f"f{it}lkb{part}_{half}")
            b.mm(po, t["ones"][0:1, 0:16],
                 t["z_row"][0:1, (c % 2) * 512:(c % 2) * 512 + 512],
                 start=False, stop=True,
                 reads=["ones_sb", "z_row"], writes=[pan],
                 tag=f"f{it}kvb{part}_{c}")
        for half in range(2):
            for cc in range(2):
                c = half * 2 + cc
                idx = part * 4 + c
                pa, pan = banksKV[idx]
                po = pa[32:48, 0:512] if idx == 7 else pa[0:16, 0:512]
                b.v(lambda e, po=po, cc=cc: e.tensor_copy(
                    t["sm16a"][:, cc * 512:(cc + 1) * 512], po),
                    reads=[pan], writes=["sm16a"],
                    tag=f"f{it}kve{part}_{c}")
            if part == 0:
                for j in range(8):
                    b.tr(ps["W"][:, j * 16:(j + 1) * 16],
                         t["sm16a"][:, j * 128:(j + 1) * 128],
                         t["identb"][0:16, 0:16],
                         reads=["sm16a", "identb_sb"], writes=["psW"],
                         inc=(j == 7), tag=f"f{it}kvt{half}_{j}")
                b.v(lambda e, half=half: e.tensor_copy(
                    bass.AP(t["vext0"], half * 256,
                            [[1032, 64], [32, 8], [1, 16]]),
                    bass.AP(ps["W"], 0, [[1024, 64], [16, 8], [1, 16]])),
                    reads=["psW"], writes=["vext0"], tag=f"f{it}kbdA{half}")
                b.v(lambda e, half=half: e.tensor_copy(
                    bass.AP(t["vext0"], 64 * 1032 + half * 256 + 16,
                            [[1032, 64], [32, 8], [1, 16]]),
                    bass.AP(ps["W"], 64 * 1024,
                            [[1024, 64], [16, 8], [1, 16]])),
                    reads=["psW"], writes=["vext0"], tag=f"f{it}kbdB{half}")
            else:
                b.v(lambda e, half=half: e.tensor_copy(
                    bass.AP(t["kv_gb"], half * 1024,
                            [[2064, 16], [128, 8], [1, 64]]),
                    bass.AP(t["sm16a"], 0, [[1024, 16], [128, 8], [1, 64]])),
                    reads=["sm16a"], writes=["kv_gb"], tag=f"f{it}vbdA{half}")
                b.dma("G",
                      bass.AP(t["kv_gb"], 16 * 2064 + half * 1024 + 64,
                              [[2064, 16], [128, 8], [1, 64]]),
                      bass.AP(t["sm16a"], 64, [[1024, 16], [128, 8], [1, 64]]),
                      "ds2s", reads=["sm16a"], writes=["kv_gb"],
                      tag=f"f{it}vbdB{half}")

    if dbg:
        b.dma("Y", dbg["kk"].ap(), t["vext0"][:, 0:512], "dout",
              reads=["vext0"], writes=["dbg_kk"], tag="d_kk")
        b.dma("Y", dbg["vv"].ap(), t["kv_gb"][0:32, 0:2048], "dout",
              reads=["kv_gb"], writes=["dbg_vv"], tag="d_vv")


def emit_ffn_slotattn(b, d, t, ps, it, dbg=None):
    emit_ffn_q(b, d, t, ps, it)
    emit_ffn_scores(b, d, t, ps, it, dbg)


def emit_ffn_q(b, d, t, ps, it):
    p = b.p
    env = _ffn_env(b, d, t, ps, it)
    for j in range(ND):
        sl = j % 2
        wq = t[f"wA{sl}"]
        b.dma("G", bass.AP(wq, 0, [[2048, 128], [128, 16], [1, 128]]),
              d["inprojQ"][j],
              f"wk{sl}", writes=[f"wA{sl}"], tag=f"f{it}lq{j}")
        pb, pbn = (ps["B"][:, 0:512], "psB") if j % 2 == 0 else                   (ps["B"][:, 512:1024], "psB")
        for c in range(2):
            pa, pan = psum_banks8(ps)[(2 * j + c) % 4]
            for kd in range(ND):
                b.mm(pa, wq[:, kd * 128:(kd + 1) * 128],
                     t["yT"][:, kd * TLOC + c * 512: kd * TLOC + c * 512 + 512],
                     start=(kd == 0), stop=(kd == ND - 1),
                     reads=["yT", f"wA{sl}"], writes=[pan],
                     tag=f"f{it}q{j}_{c}")
            b.v(lambda e, j=j, c=c, pa=pa: e.tensor_scalar(
                t["big1"][:, j * TLOC + c * 512: j * TLOC + (c + 1) * 512],
                pa, t["inprojb_sb"][:, j:j + 1], None, ALU.add),
                reads=[pan, "inprojb_sb"], writes=["big1"],
                tag=f"f{it}qe{j}_{c}")


def emit_ffn_scores(b, d, t, ps, it, dbg=None):
    p = b.p
    env = _ffn_env(b, d, t, ps, it)
    att_s, sA = env.att_s, env.sA
    if dbg:
        b.dma("Y", dbg["kk"].ap(), t["vext0"][:, 0:512], "dout",
              reads=["vext0"], writes=["dbg_kk"], tag="d_kk")
        b.dma("Y", dbg["vv"].ap(), t["kv_gb"][0:32, 0:2048], "dout",
              reads=["kv_gb"], writes=["dbg_vv"], tag="d_vv")
    for j in range(ND):
        qT = t["big1"][:, j * TLOC:(j + 1) * TLOC]
        for ti in range(NT):
            b.mm(ps["A2"][:, ti * 32: ti * 32 + 32],
                 qT[:, ti * 128:(ti + 1) * 128],
                 t["vext0"][:, j * 32:(j + 1) * 32],
                 start=True, stop=True, inc=(ti == NT - 1),
                 reads=["big1", "vext0"], writes=["psA2"],
                 tag=f"f{it}sc{j}_{ti}")
        b.s(lambda e: e.activation(sA[:, 0:256], ps["A2"][:, 0:256],
                                   AF.Exp, scale=0.125),
            reads=["psA2"], writes=["scrA"], tag=f"f{it}se{j}")
        b.v(lambda e: e.reduce_sum(
            sA[:, 256:272], bass.AP(t["scrA"], 0, [[2048, 128], [16, 16],
                                                   [1, 16]]), AX.X),
            reads=["scrA"], writes=["scrA"], tag=f"f{it}sd{j}")
        b.v(lambda e: e.reciprocal(sA[:, 272:288], sA[:, 256:272]),
            reads=["scrA"], writes=["scrA"], tag=f"f{it}sr{j}")
        b.v(lambda e: e.tensor_tensor(
            bass.AP(t["vext1"], 0, [[1032, 128], [16, 16], [1, 16]]),
            bass.AP(t["scrA"], 0, [[2048, 128], [16, 16], [1, 16]]),
            bass.AP(t["scrA"], 272, [[2048, 128], [1, 16], [0, 16]]),
            ALU.mult),
            reads=["scrA"], writes=["vext1"], tag=f"f{it}sa{j}")
        for g4 in range(2):
            for tl in range(4):
                ti = g4 * 4 + tl
                b.tr(ps["W"][0:32, tl * 128:(tl + 1) * 128],
                     att_s[:, ti * 32:(ti + 1) * 32], t["identb"][:, :],
                     reads=["vext1", "identb_sb"], writes=["psW"],
                     inc=(tl == 3), tag=f"f{it}st{j}_{ti}")
            b.v(lambda e: e.tensor_copy(
                att_s[0:32, 512:1024], ps["W"][0:32, 0:512]),
                reads=["psW"], writes=["vext1"], tag=f"f{it}ste{j}_{g4}")
            b.mm(ps["A3"][:, 0:512],
                 t["kv_gb"][0:32, j * 128:(j + 1) * 128],
                 att_s[0:32, 512:1024],
                 start=True, stop=True,
                 reads=["kv_gb", "vext1"], writes=["psA3"],
                 tag=f"f{it}cx{j}_{g4}")
            b.v(lambda e, j=j, g4=g4: e.tensor_copy(
                t["big1"][:, j * TLOC + g4 * 512: j * TLOC + (g4 + 1) * 512],
                ps["A3"][:, 0:512]),
                reads=["psA3"], writes=["big1"], tag=f"f{it}cxe{j}_{g4}")
    if dbg:
        b.dma("Y", dbg["ctx"].ap(), t["big1"][:, :], "dout", reads=["big1"],
              writes=["dbg_ctx"], tag="d_ctx")


def emit_ffn_xb(b, d, t, ps, it, dbg=None):
    p = b.p
    env = _ffn_env(b, d, t, ps, it)
    ws, u_sm, tkw, logit_sm = env.ws, env.u_sm, env.tkw, env.logit_sm
    e_bf, eT, att_s, qT, sA = env.e_bf, env.eT, env.att_s, env.qT, env.sA

    def lhs_x(kd, ti):
        return t["big1"][:, kd * TLOC + ti * 128: kd * TLOC + ti * 128 + 128]

    def evac_x(cb, ti, pa, pname):
        b.v(lambda e, cb=cb, ti=ti, pa=pa: e.tensor_tensor(
            t["ytm"][:, ti * D + cb * 512: ti * D + (cb + 1) * 512], pa,
            t["ytm"][:, ti * D + cb * 512: ti * D + (cb + 1) * 512], ALU.add),
            reads=[pname, "ytm"], writes=["ytm"], tag=f"f{it}ya{cb}_{ti}")
        b.s(lambda e, cb=cb, ti=ti: e.activation(
            ps["A3"][:, 0:512] if False else t["scrA"][:, 1024:1536],
            t["ytm"][:, ti * D + cb * 512: ti * D + (cb + 1) * 512],
            AF.Square, accum_out=t["c_tmp"][:, ti:ti + 1]),
            reads=["ytm"], writes=["scrA", "c_tmp"],
            tag=f"f{it}ysq{cb}_{ti}")
        if cb == 0:
            b.v(lambda e, ti=ti: e.tensor_copy(
                t["c_ssq"][:, ti:ti + 1], t["c_tmp"][:, ti:ti + 1]),
                reads=["c_tmp"], writes=["c_ssq"], tag=f"f{it}yac{cb}_{ti}")
        else:
            b.v(lambda e, ti=ti: e.tensor_add(
                t["c_ssq"][:, ti:ti + 1], t["c_ssq"][:, ti:ti + 1],
                t["c_tmp"][:, ti:ti + 1]),
                reads=["c_tmp", "c_ssq"], writes=["c_ssq"],
                tag=f"f{it}yac{cb}_{ti}")

    emit_colblock_out(b, t, ps, d["outpT4"], lhs_x, ["big1"],
                      d["outpb_row"], evac_x, f"xb{it}")
    for ti in range(NT):
        b.v(lambda e, ti=ti: e.tensor_scalar(
            t["c_ssq"][:, ti:ti + 1], t["c_ssq"][:, ti:ti + 1], 1.0 / D, None,
            ALU.mult), reads=["c_ssq"], writes=["c_ssq"], tag=f"f{it}yn0{ti}")
        b.s(lambda e, ti=ti: e.activation(
            t["c_ssq"][:, ti:ti + 1], t["c_ssq"][:, ti:ti + 1], AF.Sqrt,
            bias=t["c_eps6"][:, 0:1]),
            reads=["c_ssq", "c_eps6"], writes=["c_ssq"], tag=f"f{it}yn1{ti}")
        b.v(lambda e, ti=ti: e.reciprocal(
            t["c_rstd"][:, ti:ti + 1], t["c_ssq"][:, ti:ti + 1]),
            reads=["c_ssq"], writes=["c_rstd"], tag=f"f{it}yn2{ti}")
        for hf in range(2):
            if ti == 0:
                b.dma("G", t["vext1"][:, 0:1024] if hf == 0 else
                      t["phik1"][:, 0:1024],
                      d["postw_bcast"][:, hf * 1024:(hf + 1) * 1024], "dpwb",
                      writes=["vext1" if hf == 0 else "phik1"],
                      tag=f"f{it}ypw{hf}")
            buf = t["vext1"][:, 0:1024] if hf == 0 else t["phik1"][:, 0:1024]
            bname = "vext1" if hf == 0 else "phik1"
            b.v(lambda e, ti=ti, hf=hf, buf=buf: e.scalar_tensor_tensor(
                t["ytm"][:, ti * D + hf * 1024: ti * D + (hf + 1) * 1024],
                t["ytm"][:, ti * D + hf * 1024: ti * D + (hf + 1) * 1024],
                t["c_rstd"][:, ti:ti + 1], buf, ALU.mult, ALU.mult),
                reads=["ytm", "c_rstd", bname], writes=["ytm"],
                tag=f"f{it}yap{ti}_{hf}")


def emit_l0_ffn_prep(b, d, t):
    b.dma("Y", t["kv_loc"][0:16, 0:2048], d["workspace"][:, :], "dws",
          writes=["kv_loc"], tag="ldws")
    b.dma("Y", t["inprojb_sb"][:, :], d["inprojb_col"][:, :], "dib",
          writes=["inprojb_sb"], tag="ldib")



def emit_l0_combine(b, d, t, x1_out, fuse_ln=False, ps_g=None):
    for ti in range(NT):
        xr = t["xrot"][:, 0:D] if ti % 2 == 0 else t["kv_loc"][:, 0:2048]
        xrn = "xrot" if ti % 2 == 0 else "kv_loc"
        st = f"dx{ti % 2}"
        b.dma("Y", xr, d["x"][ti * 128:(ti + 1) * 128, :], st,
              writes=[xrn], tag=f"cb_ld{ti}")
        for c in range(2):
            sl = c * 1024
            sbuf = t["phiq1"][:, 0:1024] if c == 0 else t["phiq0"][:, 0:1024]
            sbn = "phik1" if c == 0 else "phik0"
            scr = t["scrA"][:, 0:1024] if c == 0 else t["scrA"][:, 1024:2048]
            b.dma("Y", sbuf, d["s_scr"][ti * 128:(ti + 1) * 128,
                                        sl:sl + 1024], f"dsi{c}",
                  reads=["s_scr"], writes=[sbn], tag=f"cb_ls{ti}_{c}")
            b.s(lambda e, ti=ti, sl=sl, scr=scr: e.activation(
                scr, t["ytm"][:, ti * D + sl: ti * D + sl + 1024], AF.Silu),
                reads=["ytm"], writes=["scrA"], tag=f"cb_si{ti}_{c}")
            b.v(lambda e, sbuf=sbuf, scr=scr: e.tensor_tensor(
                scr, scr, sbuf, ALU.mult),
                reads=["scrA", sbn], writes=["scrA"],
                tag=f"cb_m{ti}_{c}")
            b.v(lambda e, xr=xr, sl=sl, scr=scr: e.tensor_add(
                xr[:, sl:sl + 1024], xr[:, sl:sl + 1024], scr),
                reads=["scrA", xrn], writes=[xrn], tag=f"cb_a{ti}_{c}")
        b.dma("Y", x1_out[ti * 128:(ti + 1) * 128, :], xr,
              "dout", reads=[xrn], writes=["x1_out"], tag=f"cb_st{ti}")
        if fuse_ln:
            emit_ln_tile(b, t, ti, tag="lnf", xr=xr, xrn=xrn)


def emit_ln_tile(b, t, ti, eps=1e-5, tag="ln", xr=None, xrn=None):
    if xr is None:
        xr = t["xrot"][:, 0:D]
        xrn = "xrot"
    xname = xrn
    b.s(lambda e, xr=xr, ti=ti: e.activation(
        t["scrA"][:, 0:1024], xr[:, 0:1024], AF.Square,
        accum_out=t["c_ssq"][:, ti:ti + 1]),
        reads=[xname], writes=["scrA", "c_ssq"], tag=f"{tag}_sqa{ti}")
    b.s(lambda e, xr=xr, ti=ti: e.activation(
        t["scrA"][:, 0:1024], xr[:, 1024:2048], AF.Square,
        accum_out=t["c_tmp"][:, ti:ti + 1]),
        reads=[xname, "scrA"], writes=["scrA", "c_tmp"], tag=f"{tag}_sqb{ti}")
    b.v(lambda e, ti=ti: e.tensor_scalar(
        t["c_ssq"][:, ti:ti + 1], t["c_ssq"][:, ti:ti + 1], 1.0,
        t["c_tmp"][:, ti:ti + 1], ALU.mult, ALU.add),
        reads=["c_ssq", "c_tmp"], writes=["c_ssq"], tag=f"{tag}_ss{ti}")
    b.v(lambda e, xr=xr, ti=ti: e.reduce_sum(
        t["c_nm"][:, ti:ti + 1], xr, AX.X),
        reads=[xname], writes=["c_nm"], tag=f"{tag}_sum{ti}")
    b.v(lambda e, ti=ti: e.tensor_scalar(
        t["c_nm"][:, ti:ti + 1], t["c_nm"][:, ti:ti + 1], -1.0 / D, None,
        ALU.mult), reads=["c_nm"], writes=["c_nm"], tag=f"{tag}_nm{ti}")
    b.v(lambda e, ti=ti: e.tensor_mul(
        t["c_tmp"][:, ti:ti + 1], t["c_nm"][:, ti:ti + 1],
        t["c_nm"][:, ti:ti + 1]),
        reads=["c_nm"], writes=["c_tmp"], tag=f"{tag}_msq{ti}")
    b.v(lambda e, ti=ti: e.tensor_scalar(
        t["c_var"][:, ti:ti + 1], t["c_ssq"][:, ti:ti + 1], 1.0 / D,
        t["c_tmp"][:, ti:ti + 1], ALU.mult, ALU.subtract),
        reads=["c_ssq", "c_tmp"], writes=["c_var"], tag=f"{tag}_var{ti}")
    b.s(lambda e, ti=ti: e.activation(
        t["c_var"][:, ti:ti + 1], t["c_var"][:, ti:ti + 1], AF.Sqrt,
        bias=t["c_eps5"][:, 0:1]), reads=["c_var", "c_eps5"],
        writes=["c_var"], tag=f"{tag}_sd{ti}")
    b.v(lambda e, ti=ti: e.reciprocal(
        t["c_rstd"][:, ti:ti + 1], t["c_var"][:, ti:ti + 1]),
        reads=["c_var"], writes=["c_rstd"], tag=f"{tag}_rs{ti}")
    b.v(lambda e, xr=xr, ti=ti: e.tensor_scalar(
        t["ytm"][:, ti * D:(ti + 1) * D], xr, t["c_nm"][:, ti:ti + 1],
        t["c_rstd"][:, ti:ti + 1], ALU.add, ALU.mult),
        reads=[xname, "c_nm", "c_rstd"], writes=["ytm"], tag=f"{tag}_ap{ti}")


def build_stageC():
    nc = bass.Bass("TRN2", target_bir_lowering=False, debug=False)
    b = Builder(nc)
    d = declare_l0(b)
    d["inprojb_row"] = b.din("inprojb_row", (1, 3 * D))
    d["postw_bcast"] = b.din("postw_bcast", (128, D))
    d["identf"] = b.din("identf_in", (128, 128))
    x1o = b.dout("x1_out", (TLOC, D), F32)
    d["s_scr"] = b.dint("s_scr", (TLOC, D), BF16)
    t = alloc_common(b)
    alloc_ffn(b, t)
    t["competeb_sb"] = b.sbuf("competeb_sb", (16, 1), F32)
    t["inprojb_sb"] = b.sbuf("inprojb_sb", (128, 48), F32)
    t["prew_sb"] = t["inprojb_sb"][:, 16:32]
    ps = alloc_psum(b)
    for st in ("dx", "dmisc", "dmisc2", "dout"):
        b.p.dma_stream(st, mode="slot" if st != "dout" else "barrier")
    for st in ("dws", "dcpt", "dpw", "dpwb", "dcb", "dib", "dif", "drow",
               "ds2s", "dsA", "dlA", "dsz", "dsT", "dmg", "dsU", "dum", "dcptF", "dso", "dsi"):
        b.p.dma_stream(st, mode="slot")
    load_consts(b, d, t)
    emit_ln(b, d, t, d["x"], t["ytm"], tag="ln1")
    emit_cpt_prep(b, d, t)
    emit_tp_logits(b, d, t, ps, 0)
    emit_l0_attn(b, d, t, ps)
    emit_l0_ffn_prep(b, d, t)
    emit_l0_ffn_iter(b, d, t, ps, 0)
    emit_tp_logits(b, d, t, ps, 1)
    emit_l0_ffn_iter(b, d, t, ps, 1)
    emit_l0_combine(b, d, t, x1o)
    b.p.op("Y", lambda e: e.nop(), reads=["x1_out"], writes=(), inc=False,
           tag="fin")
    return finalize(b, [])


def build_stageC_dbg():
    nc = bass.Bass("TRN2", target_bir_lowering=False, debug=False)
    b = Builder(nc)
    d = declare_l0(b)
    d["inprojb_row"] = b.din("inprojb_row", (1, 3 * D))
    d["postw_bcast"] = b.din("postw_bcast", (128, D))
    d["identf"] = b.din("identf_in", (128, 128))
    x1o = b.dout("x1_out", (TLOC, D), F32)
    dbg = {
        "lg": b.dout("dbg_lg", (16, TLOC), F32),
        "mc": b.dout("dbg_mc", (16, 16), F32),
        "u": b.dout("dbg_u", (16, D), BF16),
        "ws": b.dout("dbg_ws", (16, D), F32),
        "kk": b.dout("dbg_kk", (128, 512), BF16),
        "vv": b.dout("dbg_vv", (32, D), BF16),
        "ctx": b.dout("dbg_ctx", (128, ND * TLOC), BF16),
        "att": b.dout("dbg_att", (128, 64), BF16),
        "q": b.dout("dbg_q", (128, TLOC), BF16),
    }
    dbg_y = b.dout("dbg_y", (TLOC, D), F32)
    d["s_scr"] = b.dint("s_scr", (TLOC, D), BF16)
    t = alloc_common(b)
    alloc_ffn(b, t)
    t["competeb_sb"] = b.sbuf("competeb_sb", (16, 1), F32)
    t["inprojb_sb"] = b.sbuf("inprojb_sb", (128, 48), F32)
    t["prew_sb"] = t["inprojb_sb"][:, 16:32]
    ps = alloc_psum(b)
    for st in ("dx", "dmisc", "dmisc2"):
        b.p.dma_stream(st, mode="slot")
    b.p.dma_stream("dout", mode="barrier")
    for st in ("dws", "dcpt", "dpw", "dpwb", "dcb", "dib", "dif", "drow",
               "ds2s", "dsA", "dlA", "dsz", "dsT", "dmg", "dsU", "dum", "dcptF", "dso", "dsi"):
        b.p.dma_stream(st, mode="slot")
    load_consts(b, d, t)
    emit_ln(b, d, t, d["x"], t["ytm"], tag="ln1")
    emit_cpt_prep(b, d, t)
    emit_tp_logits(b, d, t, ps, 0)
    emit_l0_attn(b, d, t, ps)
    emit_l0_ffn_prep(b, d, t)
    emit_l0_ffn_iter(b, d, t, ps, 0, dbg=dbg)
    for ti in range(NT):
        b.dma("Y", dbg_y[ti * 128:(ti + 1) * 128, :],
              t["ytm"][:, ti * D:(ti + 1) * D], "dout", reads=["ytm"],
              writes=["dbg_y"], tag=f"dy{ti}")
    emit_tp_logits(b, d, t, ps, 1)
    emit_l0_ffn_iter(b, d, t, ps, 1)
    emit_l0_combine(b, d, t, x1o)
    b.p.op("Y", lambda e: e.nop(),
           reads=["x1_out", "dbg_lg", "dbg_mc", "dbg_u", "dbg_ws", "dbg_y",
                  "dbg_kk", "dbg_vv", "dbg_ctx", "dbg_att", "dbg_q"],
           writes=(), inc=False, tag="fin")
    return finalize(b, [])


def emit_cpt_prep(b, d, t):
    cptF = t["xrot"][:, 1024:1280]
    b.dma("Y", t["identf"][:, :], d["identf"][:, :], "dif",
          writes=["identf"], tag="ldif")
    b.dma("Y", t["competeb_sb"][:, :], d["competeb_col"][:, :], "dcb",
          writes=["competeb_sb"], tag="ldcb")
    b.dma("Y", bass.AP(t["xrot"], 1024, [[2048, 128], [16, 16], [1, 16]]),
          d["competeT"].ap().rearrange("(k p) c -> p k c", p=128), "dcptF",
          writes=["xrot"], tag="ldcptF")
    b.dma("Y", t["prew_sb"], d["prew_col"][:, :], "dpw",
          writes=["inprojb_sb"], tag="ldprew")
    for kd in range(ND):
        b.v(lambda e, kd=kd, cptF=cptF: e.tensor_scalar(
            cptF[:, kd * 16:(kd + 1) * 16], cptF[:, kd * 16:(kd + 1) * 16],
            t["inprojb_sb"][:, 16 + kd:17 + kd], None, ALU.mult),
            reads=["xrot", "inprojb_sb"], writes=["xrot"], tag=f"cpts{kd}")
    b.v(lambda e, cptF=cptF: e.tensor_copy(t["cpth"][:, :], cptF),
        reads=["xrot"], writes=["cpth"], tag="cpth")
    b.v(lambda e, cptF=cptF: e.tensor_tensor(t["cptl"][:, :], cptF,
                                             t["cpth"][:, :], ALU.subtract),
        reads=["xrot", "cpth"], writes=["cptl"], tag="cptl")


def emit_tp_logits(b, d, t, ps, it):
    logit_sm = t["xrot"][0:16, 0:1024]
    sA = t["scrA"]
    for ti in range(NT):
        for c in range(2):
            hi = t["phik1"][:, 0:1024]
            lo = t["vext1"][:, 0:1024]
            b.s(lambda e, ti=ti, c=c, hi=hi: e.copy(
                hi, t["ytm"][:, ti * D + c * 1024: ti * D + (c + 1) * 1024]),
                reads=["ytm"], writes=["phik1"], tag=f"tl{it}h{ti}_{c}")
            b.v(lambda e, ti=ti, c=c, hi=hi, lo=lo: e.tensor_tensor(
                lo, t["ytm"][:, ti * D + c * 1024: ti * D + (c + 1) * 1024],
                hi, ALU.subtract),
                reads=["ytm", "phik1"], writes=["vext1"], tag=f"tl{it}l{ti}_{c}")
            for j in range(8):
                b.tr(ps["W"][:, j * 128:(j + 1) * 128],
                     hi[:, j * 128:(j + 1) * 128], t["identb"][:, :],
                     reads=["phik1", "identb_sb"], writes=["psW"],
                     inc=(j == 7), tag=f"tl{it}th{ti}_{c}_{j}")
            b.v(lambda e, ti=ti, c=c: e.tensor_copy(
                bass.AP(t["yT"], c * 8 * TLOC + ti * 128,
                        [[ND * TLOC, 128], [TLOC, 8], [1, 128]]),
                ps["W"][:, :]),
                reads=["psW"], writes=["yT"], tag=f"tl{it}eh{ti}_{c}")
            for j in range(8):
                b.tr(ps["W"][:, j * 128:(j + 1) * 128],
                     lo[:, j * 128:(j + 1) * 128], t["identb"][:, :],
                     reads=["vext1", "identb_sb"], writes=["psW"],
                     inc=(j == 7), tag=f"tl{it}tl{ti}_{c}_{j}")
            b.v(lambda e, ti=ti, c=c: e.tensor_copy(
                t["wC0"][:, c * 1024:(c + 1) * 1024], ps["W"][:, :]),
                reads=["psW"], writes=["wC0"], tag=f"tl{it}el{ti}_{c}")
        for kd in range(ND):
            yh = t["yT"][:, kd * TLOC + ti * 128: kd * TLOC + ti * 128 + 128]
            yl = t["wC0"][:, kd * 128:(kd + 1) * 128]
            b.mm(ps["A2"][:, 0:16], yh, t["cpth"][:, kd * 16:(kd + 1) * 16],
                 start=(kd == 0), stop=False, inc=False,
                 reads=["yT", "cpth"], writes=["psA2"], tag=f"tl{it}m1_{ti}")
            b.mm(ps["A2"][:, 0:16], yh, t["cptl"][:, kd * 16:(kd + 1) * 16],
                 start=False, stop=False, inc=False,
                 reads=["yT", "cptl"], writes=["psA2"], tag=f"tl{it}m2_{ti}")
            b.mm(ps["A2"][:, 0:16], yl, t["cpth"][:, kd * 16:(kd + 1) * 16],
                 start=False, stop=(kd == ND - 1),
                 reads=["wC0", "cpth"], writes=["psA2"], tag=f"tl{it}m3_{ti}")
        b.s(lambda e, ti=ti: e.activation(
            ps["B"][:, 0:1024], t["ytm"][:, ti * D: ti * D + 1024], AF.Square,
            accum_out=t["c_tmp"][:, ti:ti + 1]),
            reads=["ytm"], writes=["psB", "c_tmp"], tag=f"tl{it}ra{ti}")
        b.s(lambda e, ti=ti: e.activation(
            ps["B"][:, 0:1024], t["ytm"][:, ti * D + 1024: ti * D + 2048],
            AF.Square, accum_out=t["c_var"][:, ti:ti + 1]),
            reads=["ytm"], writes=["psB", "c_var"], tag=f"tl{it}rb{ti}")
        b.v(lambda e, ti=ti: e.tensor_scalar(
            t["c_ssq"][:, ti:ti + 1], t["c_tmp"][:, ti:ti + 1], 1.0,
            t["c_var"][:, ti:ti + 1], ALU.mult, ALU.add),
            reads=["c_tmp", "c_var"], writes=["c_ssq"], tag=f"tl{it}rc{ti}")
        b.v(lambda e, ti=ti: e.tensor_scalar(
            t["c_ssq"][:, ti:ti + 1], t["c_ssq"][:, ti:ti + 1], 1.0 / D, None,
            ALU.mult), reads=["c_ssq"], writes=["c_ssq"], tag=f"tl{it}rd{ti}")
        b.s(lambda e, ti=ti: e.activation(
            t["c_ssq"][:, ti:ti + 1], t["c_ssq"][:, ti:ti + 1], AF.Sqrt,
            bias=t["c_eps6"][:, 0:1]),
            reads=["c_ssq", "c_eps6"], writes=["c_ssq"], tag=f"tl{it}re{ti}")
        b.v(lambda e, ti=ti: e.reciprocal(
            t["c_rstd"][:, ti:ti + 1], t["c_ssq"][:, ti:ti + 1]),
            reads=["c_ssq"], writes=["c_rstd"], tag=f"tl{it}rf{ti}")
        b.v(lambda e, ti=ti: e.tensor_scalar(
            sA[:, 0:16], ps["A2"][:, 0:16], t["c_rstd"][:, ti:ti + 1], None,
            ALU.mult), reads=["psA2", "c_rstd"], writes=["scrA"],
            tag=f"tl{it}sc{ti}")
        b.tr(ps["K"][0:16, 0:128], sA[:, 0:16], t["identf"][:, :],
             reads=["scrA", "identf"], writes=["psK"], tag=f"tl{it}tr{ti}")
        b.v(lambda e, ti=ti: e.tensor_scalar(
            logit_sm[:, ti * 128:(ti + 1) * 128], ps["K"][0:16, 0:128],
            t["competeb_sb"][:, 0:1], None, ALU.add),
            reads=["psK", "competeb_sb"], writes=["xrot"], tag=f"tl{it}ev{ti}")


def host_prep_l1(ins):
    P = {}
    P["Wq1T"] = tile_cols(np.ascontiguousarray(ins["L1_attn_Wq"].T), 128)
    P["Wk1T"] = tile_cols(np.ascontiguousarray(ins["L1_attn_Wk"].T), 128)
    P["Wv1T"] = tile_cols(np.ascontiguousarray(ins["L1_attn_Wv"].T), 128)
    P["Wo1T4"] = tile_cols_halfk(np.ascontiguousarray(ins["L1_attn_Wo"].T), 512)
    import ml_dtypes
    P["gateT"] = tile_cols(np.ascontiguousarray(
        ins["L1_ffn_gate_w"].T), 256).astype(ml_dtypes.bfloat16)
    P["upT"] = tile_cols(np.ascontiguousarray(
        ins["L1_ffn_up_w"].T), 256).astype(ml_dtypes.bfloat16)
    dT = np.ascontiguousarray(ins["L1_ffn_down_w"].T)
    P["downT"] = np.ascontiguousarray(
        dT.reshape(44, 128, 2, 1024).transpose(2, 0, 1, 3)).astype(
            ml_dtypes.bfloat16)
    l1b = np.concatenate([ins["L1_attn_bq"], ins["L1_attn_bk"],
                          ins["L1_attn_bv"]])
    P["l1b_col"] = col_layout(l1b)
    P["bo_row"] = np.ascontiguousarray(ins["L1_attn_bo"][None, :])
    P["gates_col"] = col_layout(ins["L1_ffn_gate_s"])
    P["ups_col"] = col_layout(ins["L1_ffn_up_s"])
    P["ds_bcast"] = np.ascontiguousarray(
        np.broadcast_to(ins["L1_ffn_down_s"][None, :], (128, 2048)))
    return P


def declare_l1(b: Builder):
    d = {}
    for n, sh in [("Wq1T", (16, 128, 16, 128)), ("Wk1T", (16, 128, 16, 128)),
                  ("Wv1T", (16, 128, 16, 128)), ("Wo1T4", (4, 2, 128, 8, 512)),
                  ("l1b_col", (128, 48)),
                  ("bo_row", (1, D)), ("gates_col", (128, 44)),
                  ("ups_col", (128, 44)), ("ds_bcast", (128, D))]:
        d[n] = b.din(n, sh)
    d["downT"] = b.din("downT", (2, 44, 128, 1024), BF16)
    d["gateT"] = b.din("gateT", (22, 128, 16, 256), BF16)
    d["upT"] = b.din("upT", (22, 128, 16, 256), BF16)
    d["halfsel"] = b.din("halfsel", (128, 1))
    d["ccC_i"] = b.dint("ccC_i", (128, 16))
    d["ccC_o"] = b.dint("ccC_o", (256, 16))
    return d


def emit_tp_plain(b, d, t, ps, tag="tpp"):
    for ti in range(NT):
        for c in range(2):
            hi = t["phik1"][:, 0:1024]
            b.v(lambda e, ti=ti, c=c, hi=hi: e.tensor_copy(
                hi, t["ytm"][:, ti * D + c * 1024: ti * D + (c + 1) * 1024]),
                reads=["ytm"], writes=["phik1"], tag=f"{tag}h{ti}_{c}")
            for j in range(8):
                b.tr(ps["W"][:, j * 128:(j + 1) * 128],
                     hi[:, j * 128:(j + 1) * 128], t["identb"][:, :],
                     reads=["phik1", "identb_sb"], writes=["psW"],
                     inc=(j == 7), tag=f"{tag}t{ti}_{c}_{j}")
            b.v(lambda e, ti=ti, c=c: e.tensor_copy(
                bass.AP(t["yT"], c * 8 * TLOC + ti * 128,
                        [[ND * TLOC, 128], [TLOC, 8], [1, 128]]),
                ps["W"][:, :]),
                reads=["psW"], writes=["yT"], tag=f"{tag}e{ti}_{c}")


def emit_l1_attn(b, d1, t, ps):
    p = b.p
    b.dma("Y", t["inprojb_sb"][:, :], d1["l1b_col"][:, :], "dib",
          writes=["inprojb_sb"], tag="l1b")
    b.v(lambda e: e.tensor_scalar(t["c_ssq"][:, :], t["inprojb_sb"][:, 0:16],
                                  -1.0, None, ALU.mult),
        reads=["inprojb_sb"], writes=["c_ssq"], tag="l1nbq")
    b.v(lambda e: e.tensor_scalar(t["c_tmp"][:, :], t["inprojb_sb"][:, 16:32],
                                  -1.0, None, ALU.mult),
        reads=["inprojb_sb"], writes=["c_tmp"], tag="l1nbk")
    phik = t["xrot"][:, 0:1024]
    vbuf = t["xrot"][:, 1024:2048]
    for j in range(ND):
        sl = j % 2
        wk, wv = t[f"wA{sl}"], t[f"wB{sl}"]
        b.dma("G", bass.AP(wk, 0, [[2048, 128], [128, 16], [1, 128]]),
              d1["Wk1T"][j], f"wk{sl}",
              writes=[f"wA{sl}"], tag=f"l1k{j}")
        b.dma("G", bass.AP(wv, 0, [[2048, 128], [128, 16], [1, 128]]),
              d1["Wv1T"][j], f"wv{sl}",
              writes=[f"wB{sl}"], tag=f"l1v{j}")
        for c in range(2):
            for kd in range(ND):
                b.mm(ps["B"][:, c * 512:(c + 1) * 512],
                     wk[:, kd * 128:(kd + 1) * 128],
                     t["yT"][:, kd * TLOC + c * 512: kd * TLOC + c * 512 + 512],
                     start=(kd == 0), stop=(kd == ND - 1),
                     reads=["yT", f"wA{sl}"], writes=["psB"],
                     tag=f"l1kp{j}_{c}")
            emit_phi_evac(b, t, ps["B"][:, c * 512:(c + 1) * 512], "psB",
                          phik[:, c * 512:(c + 1) * 512], "xrot", 512,
                          bias_col=(t["inprojb_sb"][:, 16 + j:17 + j],
                                    t["c_tmp"][:, j:j + 1], "c_tmp"),
                          tag=f"l1pk{j}_{c}")
        for c in range(2):
            for kd in range(ND):
                b.mm(ps["A2"][:, 0:512],
                     wv[:, kd * 128:(kd + 1) * 128],
                     t["yT"][:, kd * TLOC + c * 512: kd * TLOC + c * 512 + 512],
                     start=(kd == 0), stop=(kd == ND - 1),
                     reads=["yT", f"wB{sl}"], writes=["psA2"],
                     tag=f"l1vp{j}_{c}")
            b.v(lambda e, j=j, c=c: e.tensor_scalar(
                vbuf[:, c * 512:(c + 1) * 512], ps["A2"][:, 0:512],
                t["inprojb_sb"][:, 32 + j:33 + j], None, ALU.add),
                reads=["psA2", "inprojb_sb"], writes=["xrot"],
                tag=f"l1ve{j}_{c}")
        b.v(lambda e, j=j: e.tensor_mul(
            t["big1"][:, j * TLOC:(j + 1) * TLOC], phik, vbuf),
            reads=["xrot"], writes=["big1"], tag=f"l1u{j}")
        b.v(lambda e, j=j: e.reduce_sum(
            t["totc"][:, j:j + 1], t["big1"][:, j * TLOC:(j + 1) * TLOC],
            AX.X), reads=["big1"], writes=["kv_loc"], tag=f"l1t{j}")
    b.dma("Y", d1["ccC_i"][:, :], t["totc"], "dsC", reads=["kv_loc"],
          writes=["ccC_i"], tag="l1to")
    p._declare_sem("ccC", 1)
    b.g(lambda e: e.collective_compute(
        "AllGather", ALU.bypass, replica_groups=PAIRS,
        ins=[d1["ccC_i"].ap().opt()], outs=[d1["ccC_o"].ap().opt()]),
        reads=["ccC_i"], writes=["ccC_o"], sem="ccC", tag="l1cc")
    b.dma("Y", t["carry_sb"], d1["ccC_o"][0:128, :], "dlC",
          reads=["ccC_o"], writes=["kv_loc"], tag="l1ci")
    b.dma("Y", t["hs_sb"], d1["halfsel"][:, :], "dhs", writes=["c_var"],
          tag="l1hs2")
    b.v(lambda e: e.tensor_scalar(t["carry_sb"], t["carry_sb"], t["hs_sb"],
                                  None, ALU.mult),
        reads=["kv_loc", "c_var"], writes=["kv_loc"], tag="l1cs")
    for j in range(ND):
        sl = j % 2
        wq = t[f"wA{sl}"]
        b.dma("G", bass.AP(wq, 0, [[2048, 128], [128, 16], [1, 128]]),
              d1["Wq1T"][j], f"wk{sl}",
              writes=[f"wA{sl}"], tag=f"l1q{j}")
        for c in range(2):
            for kd in range(ND):
                b.mm(ps["B"][:, c * 512:(c + 1) * 512],
                     wq[:, kd * 128:(kd + 1) * 128],
                     t["yT"][:, kd * TLOC + c * 512: kd * TLOC + c * 512 + 512],
                     start=(kd == 0), stop=(kd == ND - 1),
                     reads=["yT", f"wA{sl}"], writes=["psB"],
                     tag=f"l1qp{j}_{c}")
            emit_phi_evac(b, t, ps["B"][:, c * 512:(c + 1) * 512], "psB",
                          phik[:, c * 512:(c + 1) * 512], "xrot", 512,
                          bias_col=(t["inprojb_sb"][:, j:j + 1],
                                    t["c_ssq"][:, j:j + 1], "c_ssq"),
                          tag=f"l1pq{j}_{c}")
        b.v(lambda e, j=j: e.tensor_tensor_scan(
            t["scrA"][:, 0:1024], t["big1"][:, j * TLOC:(j + 1) * TLOC],
            t["big1"][:, j * TLOC:(j + 1) * TLOC],
            t["carry_sb"][:, j:j + 1] if False else t["carry_sb"][:, j:j + 1],
            ALU.add, ALU.bypass),
            reads=["big1", "kv_loc"], writes=["scrA"], tag=f"l1s{j}")
        b.v(lambda e, j=j: e.scalar_tensor_tensor(
            t["big1"][:, j * TLOC:(j + 1) * TLOC], phik,
            0.08838834764831845, t["scrA"][:, 0:1024], ALU.mult, ALU.mult),
            reads=["xrot", "scrA"], writes=["big1"], tag=f"l1c{j}")
    def lhs_s2(kd, ti):
        return t["big1"][:, kd * TLOC + ti * 128: kd * TLOC + ti * 128 + 128]

    def evac_s2(cb, ti, pa, pname):
        sb = t["phiq0"][:, (ti % 2) * 512:(ti % 2) * 512 + 512]
        b.v(lambda e, pa=pa, sb=sb: e.tensor_copy(sb, pa),
            reads=[pname], writes=["phik0"], tag=f"l1se{cb}_{ti}")
        b.dma("Y", b.dram["s_scr"][ti * 128:(ti + 1) * 128,
                                   cb * 512:(cb + 1) * 512], sb, "dso",
              reads=["phik0"], writes=["s_scr"], tag=f"l1ss{cb}_{ti}")

    emit_colblock_out(b, t, ps, d1["Wo1T4"], lhs_s2, ["big1"],
                      d1["bo_row"], evac_s2, "s2")


def emit_l1_ffn(b, d1, t, ps, out_dram, x1_dram):
    NI = 44
    for st in ("hv0", "hv1", "hv2", "hv3", "ho0", "ho1", "hk0", "hk1"):
        b.p.dma_stream(st, mode="slot")
    hdnb = t["ytm"][:, :].bitcast(BF16)
    b.dma("G", t["kv_gb"][:, 0:2048], d1["ds_bcast"][:, :], "ddsb",
          writes=["kv_gb"], tag="ldds")
    b.dma("Y", t["gs_sb"], d1["gates_col"][:, :], "dgs", writes=["kv_loc"],
          tag="ldgs")
    b.dma("Y", t["us_sb"], d1["ups_col"][:, :], "dus", writes=["kv_loc"],
          tag="ldus")
    banks = psum_banks8(ps)
    up_slots = (t["big1"][:, 8192:12288], t["big1"][:, 12288:16384])
    for half in range(2):
        toff = half * 512
        for i2 in range(22):
            sl = i2 % 2
            wg = t[f"wC{sl}"]
            wu = up_slots[sl]
            b.dma("Y", bass.AP(wg, 0, [[4096, 128], [256, 16], [1, 256]]),
                  d1["gateT"][i2], f"ho{sl}",
                  writes=[f"wC{sl}"], tag=f"g{half}_{i2}")
            b.dma("Y", bass.AP(t["big1"], 8192 + sl * 4096,
                               [[ND * TLOC, 128], [256, 16], [1, 256]]),
                  d1["upT"][i2], f"hk{sl}",
                  writes=["big1"], tag=f"u{half}_{i2}")
            for io in range(2):
                i = i2 * 2 + io
                pg, pgn = banks[i % 4]
                pu, pun = banks[4 + i % 4]
                for kd in range(ND):
                    b.mm(pg, wg[:, kd * 256 + io * 128: kd * 256 + io * 128 + 128],
                         t["yT"][:, kd * TLOC + toff: kd * TLOC + toff + 512],
                         start=(kd == 0), stop=(kd == ND - 1),
                         reads=["yT", f"wC{sl}"], writes=[pgn],
                         tag=f"gm{half}_{i}")
                b.s(lambda e, i=i, io=io, pg=pg: e.activation(
                    t["scrA"][:, io * 512:io * 512 + 512], pg, AF.Silu,
                    scale=t["gs_sb"][:, i:i + 1]),
                    reads=[pgn, "kv_loc"], writes=["scrA"],
                    tag=f"gsi{half}_{i}")
                for kd in range(ND):
                    b.mm(pu, bass.AP(t["big1"],
                                     8192 + sl * 4096 + kd * 256 + io * 128,
                                     [[ND * TLOC, 128], [1, 128]]),
                         t["yT"][:, kd * TLOC + toff: kd * TLOC + toff + 512],
                         start=(kd == 0), stop=(kd == ND - 1),
                         reads=["yT", "big1"], writes=[pun],
                         tag=f"um{half}_{i}")
                b.v(lambda e, i=i, io=io, pu=pu: e.scalar_tensor_tensor(
                    hdnb[:, i * 512:(i + 1) * 512], pu, t["us_sb"][:, i:i + 1],
                    t["scrA"][:, io * 512:io * 512 + 512],
                    ALU.mult, ALU.mult),
                    reads=[pun, "kv_loc", "scrA"], writes=["ytm"],
                    tag=f"hd{half}_{i}")
        for cp in range(2):
            for kd in range(NI):
                sl = kd % 4
                wnm = ("wB0", "wB1", "wA0", "wA1")[sl]
                wdn = t[wnm]
                b.dma("Y", wdn[:, 0:1024], d1["downT"][cp, kd], f"hv{sl}",
                      writes=[wnm], tag=f"dn{half}_{cp}_{kd}")
                for tt in range(4):
                    for ci in range(2):
                        pa, pname = banks[tt * 2 + ci]
                        b.mm(pa, hdnb[:, kd * 512 + tt * 128:
                                      kd * 512 + tt * 128 + 128],
                             wdn[:, ci * 512:(ci + 1) * 512],
                             start=(kd == 0), stop=(kd == NI - 1),
                             inc=(ci == 1),
                             reads=["ytm", wnm], writes=[pname],
                             tag=f"dm{half}_{cp}_{kd}_{tt}")
            for tt in range(4):
                ti = half * 4 + tt
                for ci in range(2):
                    c = cp * 2 + ci
                    pa, pname = banks[tt * 2 + ci]
                    xr = t["xrot"][:, (c % 2) * 1024:(c % 2) * 1024 + 512]
                    xrn = f"xrot{c % 2}"
                    b.dma("Y", xr,
                          x1_dram[ti * 128:(ti + 1) * 128,
                                  c * 512:(c + 1) * 512], f"dx{c % 2}",
                          writes=[xrn], reads=["x1_out"],
                          tag=f"dcx{ti}_{c}")
                    sb2 = t["phiq1"][:, (c % 2) * 512:(c % 2) * 512 + 512]
                    sbn = f"phik1{c % 2}"
                    b.dma("Y", sb2,
                          b.dram["s_scr"][ti * 128:(ti + 1) * 128,
                                          c * 512:(c + 1) * 512],
                          f"dsi{c % 2}", reads=["s_scr"], writes=[sbn],
                          tag=f"dc_s{ti}_{c}")
                    b.v(lambda e, c=c, pa=pa: e.tensor_tensor(
                        t["scrA"][:, 0:512], pa,
                        t["kv_gb"][:, c * 512:(c + 1) * 512], ALU.mult),
                        reads=[pname, "kv_gb"], writes=["scrA"],
                        tag=f"dc_m{ti}_{c}")
                    b.s(lambda e: e.activation(
                        t["scrA"][:, 512:1024], t["scrA"][:, 0:512], AF.Silu),
                        reads=["scrA"], writes=["scrA"], tag=f"dc_si{ti}_{c}")
                    b.v(lambda e, sb2=sb2: e.tensor_tensor(
                        t["scrA"][:, 0:512], t["scrA"][:, 512:1024], sb2,
                        ALU.mult),
                        reads=["scrA", sbn], writes=["scrA"],
                        tag=f"dc_g{ti}_{c}")
                    b.v(lambda e, xr=xr: e.tensor_add(xr, xr,
                                                      t["scrA"][:, 0:512]),
                        reads=["scrA", xrn], writes=[xrn],
                        tag=f"dc_a{ti}_{c}")
                    b.dma("Y", out_dram[ti * 128:(ti + 1) * 128,
                                        c * 512:(c + 1) * 512], xr, "dout",
                          reads=[xrn], writes=["out"], tag=f"dco{ti}_{c}")


def build_stageD():
    nc = bass.Bass("TRN2", target_bir_lowering=False, debug=False)
    b = Builder(nc)
    x1 = b.din("x1", (TLOC, D))
    d1 = declare_l1(b)
    out = b.dout("out", (TLOC, D), F32)
    b.dram["s_scr"] = b.dint("s_scr", (TLOC, D), BF16)
    d = {"identb": b.din("identb", (128, 128), BF16),
         "ones_row": b.din("ones_row", (1, D), BF16)}
    t = alloc_common(b)
    alloc_ffn(b, t)
    t["competeb_sb"] = b.sbuf("competeb_sb", (16, 1), F32)
    t["inprojb_sb"] = b.sbuf("inprojb_sb", (128, 48), F32)
    t["prew_sb"] = t["inprojb_sb"][:, 16:32]
    t["totc"] = t["kv_loc"][:, 0:16]
    t["carry_sb"] = t["kv_loc"][:, 0:16]
    t["hs_sb"] = t["c_var"][:, 0:1]
    t["gs_sb"] = t["kv_loc"][:, 16:60]
    t["us_sb"] = t["kv_loc"][:, 60:104]
    ps = alloc_psum(b)
    for st in ("dx", "dmisc", "dmisc2"):
        b.p.dma_stream(st, mode="slot")
    b.p.dma_stream("dout", mode="barrier")
    for st in ("dws", "dpw", "dpwb", "dcb", "dib", "dif", "drow", "ds2s",
               "dsA", "dlA", "dsz", "dsT", "dmg", "dsU", "dum", "dcptF",
               "dso", "dsi", "dsC", "dlC", "dhs", "ddsb", "dgs", "dus",
               "dx0", "dx1", "dsi0", "dsi1",
               "wk0", "wk1", "wv0", "wv1", "wq0", "wq1", "wo0", "wo1"):
        b.p.dma_stream(st, mode="slot")
    load_consts(b, d, t)
    emit_ln(b, d, t, x1, t["ytm"], tag="ln2")
    emit_tp_plain(b, d, t, ps, tag="tpp")
    emit_l1_attn(b, d1, t, ps)
    emit_l1_ffn(b, d1, t, ps, out, x1)
    b.p.op("Y", lambda e: e.nop(), reads=["out"], writes=(), inc=False,
           tag="fin")
    return finalize(b, [])


def build_fused():
    nc = bass.Bass("TRN2", target_bir_lowering=False, debug=False)
    b = Builder(nc)
    d = declare_l0(b)
    d["inprojb_row"] = b.din("inprojb_row", (1, 3 * D))
    d["postw_bcast"] = b.din("postw_bcast", (128, D))
    d["identf"] = b.din("identf_in", (128, 128))
    d1 = declare_l1(b)
    out = b.dout("out", (TLOC, D), F32)
    d["s_scr"] = b.dint("s_scr", (TLOC, D), BF16)
    x1s = b.dint("x1_scr", (TLOC, D), F32)
    t = alloc_common(b)
    alloc_ffn(b, t)
    t["competeb_sb"] = b.sbuf("competeb_sb", (16, 1), F32)
    t["inprojb_sb"] = b.sbuf("inprojb_sb", (128, 48), F32)
    t["prew_sb"] = t["inprojb_sb"][:, 16:32]
    t["totc"] = t["kv_loc"][:, 0:16]
    t["carry_sb"] = t["kv_loc"][:, 0:16]
    t["hs_sb"] = t["c_var"][:, 0:1]
    t["gs_sb"] = t["kv_loc"][:, 16:60]
    t["us_sb"] = t["kv_loc"][:, 60:104]
    ps = alloc_psum(b)
    for st in ("dx", "dmisc", "dmisc2"):
        b.p.dma_stream(st, mode="slot")
    b.p.dma_stream("dout", mode="barrier")
    b.p.dma_stream("dx1", mode="slot")
    for st in ("dws", "dpw", "dpwb", "dcb", "dib", "dif", "drow", "ds2s",
               "dsA", "dlA", "dsz", "dsT", "dmg", "dsU", "dum", "dcptF",
               "dso", "dsi", "dsC", "dlC", "dhs", "ddsb", "dgs", "dus",
               "dx0", "dx1", "dsi0", "dsi1"):
        b.p.dma_stream(st, mode="slot")
    load_consts(b, d, t)
    emit_ln(b, d, t, d["x"], t["ytm"], tag="ln1")
    emit_cpt_prep(b, d, t)
    emit_tp_logits(b, d, t, ps, 0)
    emit_l0_attn_A(b, d, t, ps)
    emit_l0_ffn_prep(b, d, t)
    emit_ffn_topk(b, d, t, ps, 0)
    emit_l0_attn_BC(b, d, t, ps)
    emit_ffn_u(b, d, t, ps, 0)
    emit_ffn_q(b, d, t, ps, 0)
    emit_ffn_written(b, d, t, ps, 0)
    emit_ffn_scores(b, d, t, ps, 0)
    emit_ffn_xb(b, d, t, ps, 0)
    emit_tp_logits(b, d, t, ps, 1)
    emit_l0_ffn_iter(b, d, t, ps, 1)
    emit_l0_combine(b, d, t, x1s, fuse_ln=True)
    emit_tp_plain(b, d, t, ps, tag="tpp")
    emit_l1_attn(b, d1, t, ps)
    emit_l1_ffn(b, d1, t, ps, out, x1s)
    b.p.op("Y", lambda e: e.nop(), reads=["out"], writes=(), inc=False,
           tag="fin")
    return finalize(b, [])





_NC_CACHE = {}


def _get_nc(name, fn):
    if name not in _NC_CACHE:
        _NC_CACHE[name] = fn()
    return _NC_CACHE[name]


def kernel(**inputs):
    from concourse.bass_utils import run_bass_kernel_spmd
    ins = inputs
    _ = (ins.get("L0_attn_cos"), ins.get("L0_attn_sin"),
         ins.get("L0_ffn_ig_w"), ins.get("L0_ffn_ig_b"))
    P = host_prep_l0(ins)
    P["inprojb_row"] = np.ascontiguousarray(ins["L0_ffn_in_proj_b"][None, :])
    P["postw_bcast"] = np.ascontiguousarray(
        np.broadcast_to(ins["L0_ffn_norm_post_w"][None, :], (128, 2048)))
    P["identf_in"] = np.eye(128, dtype=np.float32)
    P.update(host_prep_l1(ins))
    shards = shard_inputs(np.asarray(ins["L0_attn_x"], np.float32))
    names = ("WqT", "WkT4", "WvT4", "WoT4", "inprojT", "inprojQ", "writeT",
             "outpT4", "competeT", "prew_col", "postw_col", "postw_row",
             "inprojb_col", "competeb_col", "writeb_row", "outpb_row",
             "workspace", "identb", "ones_row", "inprojb_row", "postw_bcast",
             "identf_in", "Wq1T", "Wk1T", "Wv1T", "Wo1T4", "gateT", "upT",
             "downT", "l1b_col", "bo_row", "gates_col", "ups_col", "ds_bcast")
    in_maps = []
    for c in range(8):
        m = {k: P[k] for k in names}
        m["x"] = shards[c]
        m["halfsel"] = np.full((128, 1), float(c % 2), np.float32)
        in_maps.append(m)
    nc = _get_nc("fused", build_fused)
    r = run_bass_kernel_spmd(nc, in_maps, core_ids=list(range(8)))
    out = np.zeros((4, 2048, 2048), np.float32)
    for c in range(8):
        b_, h = c // 2, c % 2
        out[b_, h * 1024:(h + 1) * 1024, :] = r.results[c]["out"]
    return out


# version 21 (final)
# speedup vs baseline: 1.0812x; 1.0571x over previous
import numpy as np
import concourse.bass as bass
import concourse.mybir as mybir

F32 = mybir.dt.float32
BF16 = mybir.dt.bfloat16
AF = mybir.ActivationFunctionType
ALU = mybir.AluOpType
AX = mybir.AxisListType

TLOC = 1024
NT = TLOC // 128
D = 2048
ND = D // 128
PAIRS = [[0, 1], [2, 3], [4, 5], [6, 7]]

PEND = object()
BARRIER = object()


class Op:
    __slots__ = ("eng", "fn", "waits", "inc", "tag")

    def __init__(self, eng, fn, waits, inc, tag):
        self.eng, self.fn, self.waits, self.inc, self.tag = eng, fn, waits, inc, tag


class Prog:
    def __init__(self, nc):
        self.nc = nc
        self.ops = []
        self.sem_counts = {}
        self.sem_step = {}
        self.sem_mode = {}
        self.observed = {e: {} for e in ("T", "V", "S", "G", "Y")}
        self.W = {}
        self.R = {}
        self.sem_names = []
        self._engine_sem = {"T": "sT", "V": "sV", "S": "sS", "G": "sG", "Y": "sY"}
        self._snap = {}
        self._pendT = []
        for s in ("sT", "sV", "sS", "sG", "sY"):
            self._declare_sem(s, 1)

    def _declare_sem(self, name, step, mode="barrier"):
        if name not in self.sem_counts:
            self.sem_counts[name] = 0
            self.sem_step[name] = step
            self.sem_names.append(name)
            self._snap[name] = {}
            self.sem_mode[name] = mode

    def dma_stream(self, name, mode="barrier"):
        self._declare_sem(name, 16, mode)

    def _need(self, eng, dep, waits, tag=""):
        sem, val = dep
        if val is PEND:
            if eng == "T":
                return
            raise RuntimeError(
                f"op {tag}: engine {eng} depends on un-inc'd T chain; "
                f"give the chain's last matmul inc=True")
        if val is BARRIER:
            val = self.sem_counts[sem]
        if eng == "T" and sem == "sT":
            return
        if self.observed[eng].get(sem, 0) >= val:
            return
        snap = self._snap[sem].get(val)
        if snap:
            for s2, v2 in snap.items():
                if self.observed[eng].get(s2, 0) < v2:
                    self.observed[eng][s2] = v2
        self.observed[eng][sem] = val
        waits.append((sem, val))

    def _buf_names(self, bufs):
        out = []
        for b in bufs:
            if b is None:
                continue
            if isinstance(b, str):
                out.append(b)
            else:
                t = getattr(b, "tensor", b)
                out.append(t.name if hasattr(t, "name") else str(t))
        return out

    def op(self, eng, fn, reads=(), writes=(), inc=True, sem=None, tag="",
           war=True):
        waits = []
        rnames = self._buf_names(reads)
        wnames = self._buf_names(writes)
        for n in rnames:
            if n in self.W:
                self._need(eng, self.W[n], waits, tag=tag + "/r:" + n)
        for n in wnames:
            if n in self.W:
                self._need(eng, self.W[n], waits, tag=tag + "/w:" + n)
            if war:
                for r in self.R.get(n, ()):
                    self._need(eng, r, waits, tag=tag + "/war:" + n)
        sem_name = sem if sem is not None else self._engine_sem[eng]
        assert sem_name in self.sem_counts, f"undeclared stream {sem_name}"
        is_dma_stream = sem is not None
        if inc:
            self.sem_counts[sem_name] += self.sem_step[sem_name]
            val = self.sem_counts[sem_name]
            inc_rec = (sem_name, self.sem_step[sem_name])
            snap = dict(self.observed[eng])
            snap[sem_name] = val
            self._snap[sem_name][val] = snap
            if is_dma_stream and self.sem_mode[sem_name] == "barrier":
                done = (sem_name, BARRIER)
            else:
                done = (sem_name, val)
            if eng == "T" and sem_name == "sT":
                for cell, key in self._pendT:
                    cur = cell[key]
                    if isinstance(cur, tuple) and cur[1] is PEND:
                        cell[key] = done
                self._pendT = []
        else:
            assert eng in ("T", "Y") and sem is None, "inc=False only for T chains / final nop"
            inc_rec = None
            done = (sem_name, PEND)
        for n in wnames:
            self.W[n] = done
            self.R[n] = []
            if done[1] is PEND:
                self._pendT.append((self.W, n))
        for n in rnames:
            lst = self.R.setdefault(n, [])
            lst.append(done)
            if done[1] is PEND:
                self._pendT.append((lst, len(lst) - 1))
        self.ops.append(Op(eng, fn, waits, inc_rec, tag))
        return done

    def emit(self, block, semobjs):
        streams = {e: [] for e in ("T", "V", "S", "G", "Y")}
        for o in self.ops:
            streams[o.eng].append(o)

        def run(engine_obj, ops):
            for o in ops:
                for sem, val in o.waits:
                    engine_obj.wait_ge(semobjs[sem], val)
                ins = o.fn(engine_obj)
                if o.inc is not None:
                    assert ins is not None, f"op {o.tag} returned no instruction"
                    ins.then_inc(semobjs[o.inc[0]], o.inc[1])

        if streams["T"]:
            @block.tensor
            def _(t):
                run(t, streams["T"])
        if streams["V"]:
            @block.vector
            def _(v):
                run(v, streams["V"])
        if streams["S"]:
            @block.scalar
            def _(s):
                run(s, streams["S"])
        if streams["G"]:
            @block.gpsimd
            def _(g):
                run(g, streams["G"])
        if streams["Y"]:
            @block.sync
            def _(y):
                run(y, streams["Y"])


class Builder:
    def __init__(self, nc):
        self.nc = nc
        self.p = Prog(nc)
        self.sb = {}
        self.psd = {}
        self.dram = {}
        self.input_names = []

    def sbuf(self, name, shape, dtype):
        h = self.nc.alloc_sbuf_tensor(name, list(shape), dtype)
        self.sb[name] = h
        return h

    def psum(self, name, shape, dtype=F32):
        h = self.nc.alloc_psum_tensor(name, list(shape), dtype)
        self.psd[name] = h
        return h

    def din(self, name, shape, dtype=F32):
        h = self.nc.dram_tensor(name, list(shape), dtype, kind="ExternalInput")
        self.dram[name] = h
        self.input_names.append(name)
        return h

    def dout(self, name, shape, dtype=F32):
        h = self.nc.dram_tensor(name, list(shape), dtype, kind="ExternalOutput")
        self.dram[name] = h
        return h

    def dint(self, name, shape, dtype=F32):
        h = self.nc.dram_tensor(name, list(shape), dtype)
        self.dram[name] = h
        return h

    def dma(self, eng, out, in_, stream, reads=(), writes=(), tag="dma", **kw):
        return self.p.op(
            eng, lambda e: e.dma_start(out=out, in_=in_, **kw),
            reads=reads, writes=writes, sem=stream, tag=tag)

    def mm(self, out, lhsT, rhs, start, stop, reads, writes, inc=None,
           tag="mm", tile_position=None, war=True):
        if inc is None:
            inc = stop
        return self.p.op(
            "T",
            lambda e: e.matmul(out, lhsT, rhs, start=start, stop=stop,
                               tile_position=tile_position),
            reads=reads, writes=writes, inc=inc, tag=tag, war=war)

    def tr(self, out, in_, ident, reads, writes, inc=True, tag="tr"):
        return self.p.op(
            "T", lambda e: e.transpose(out, in_, ident),
            reads=reads, writes=writes, inc=inc, tag=tag)

    def v(self, fn, reads, writes, inc=True, tag="v"):
        return self.p.op("V", fn, reads=reads, writes=writes, inc=inc, tag=tag)

    def s(self, fn, reads, writes, inc=True, tag="s"):
        return self.p.op("S", fn, reads=reads, writes=writes, inc=inc, tag=tag)

    def g(self, fn, reads, writes, inc=True, tag="g", sem=None):
        return self.p.op("G", fn, reads=reads, writes=writes, inc=inc, tag=tag,
                         sem=sem)



def tile_cols(W, cw):
    Din, O = W.shape
    a = W.reshape(Din // 128, 128, O // cw, cw)
    return np.ascontiguousarray(a.transpose(2, 1, 0, 3))


def tile_cols_halfk(W, cw):
    Din, O = W.shape
    a = W.reshape(2, Din // 256, 128, O // cw, cw)
    return np.ascontiguousarray(a.transpose(3, 0, 2, 1, 4))


def col_layout(w):
    return np.ascontiguousarray(w.reshape(-1, 128).T)


def host_prep_l0(ins):
    import ml_dtypes
    P = {}
    P["WqT"] = tile_cols(np.ascontiguousarray(ins["L0_attn_Wq"].T), 128)
    P["WkT4"] = tile_cols_halfk(np.ascontiguousarray(ins["L0_attn_Wk"].T), 512)
    P["WvT4"] = tile_cols_halfk(np.ascontiguousarray(ins["L0_attn_Wv"].T), 512)
    P["WoT4"] = tile_cols_halfk(np.ascontiguousarray(
        ins["L0_attn_Wo"].T), 512).astype(ml_dtypes.bfloat16)
    P["inprojT"] = np.ascontiguousarray(ins["L0_ffn_in_proj_w"].T)
    P["inprojQ"] = tile_cols(P["inprojT"][:, 0:2048], 128)
    P["outpT4"] = tile_cols_halfk(np.ascontiguousarray(
        ins["L0_ffn_out_proj_w"].T), 512).astype(ml_dtypes.bfloat16)
    P["writeT"] = np.ascontiguousarray(
        ins["L0_ffn_write_w"].T).astype(ml_dtypes.bfloat16)
    P["outpT"] = np.ascontiguousarray(ins["L0_ffn_out_proj_w"].T)
    P["competeT"] = np.ascontiguousarray(ins["L0_ffn_compete_w"].T)
    P["prew_col"] = col_layout(ins["L0_ffn_norm_pre_w"])
    P["postw_col"] = col_layout(ins["L0_ffn_norm_post_w"])
    P["postw_row"] = np.ascontiguousarray(ins["L0_ffn_norm_post_w"][None, :])
    P["inprojb_col"] = col_layout(ins["L0_ffn_in_proj_b"])
    P["competeb_col"] = np.ascontiguousarray(ins["L0_ffn_compete_b"][:, None])
    P["writeb_row"] = np.ascontiguousarray(ins["L0_ffn_write_b"][None, :])
    P["outpb_row"] = np.ascontiguousarray(ins["L0_ffn_out_proj_b"][None, :])
    P["workspace"] = np.ascontiguousarray(ins["L0_ffn_workspace"])
    P["identb"] = np.eye(128, dtype=ml_dtypes.bfloat16)
    P["ones_row"] = np.ones((1, 2048), dtype=ml_dtypes.bfloat16)
    return P


def shard_inputs(x_full):
    shards = []
    for c in range(8):
        b, h = c // 2, c % 2
        shards.append(np.ascontiguousarray(x_full[b, h * 1024:(h + 1) * 1024, :]))
    return shards



def declare_l0(b: Builder):
    d = {}
    d["x"] = b.din("x", (TLOC, D))
    for n, sh in [("WqT", (16, 128, 16, 128)), ("WkT4", (4, 2, 128, 8, 512)),
                  ("WvT4", (4, 2, 128, 8, 512)),
                  ("inprojT", (D, 3 * D)), ("inprojQ", (16, 128, 16, 128)),
                  ("competeT", (D, 16))]:
        d[n] = b.din(n, sh)
    d["writeT"] = b.din("writeT", (D, D), BF16)
    d["WoT4"] = b.din("WoT4", (4, 2, 128, 8, 512), BF16)
    d["outpT4"] = b.din("outpT4", (4, 2, 128, 8, 512), BF16)
    for n, sh in [("prew_col", (128, 16)), ("postw_col", (128, 16)),
                  ("postw_row", (1, D)), ("inprojb_col", (128, 48)),
                  ("competeb_col", (16, 1)), ("writeb_row", (1, D)),
                  ("outpb_row", (1, D)), ("workspace", (16, D))]:
        d[n] = b.din(n, sh)
    d["identb"] = b.din("identb", (128, 128), BF16)
    d["ones_row"] = b.din("ones_row", (1, D), BF16)
    d["ccA_i"] = b.dint("ccA_i", (128, 2064))
    d["ccA_o"] = b.dint("ccA_o", (128, 2064))
    d["ccT_i"] = b.dint("ccT_i", (16, 4))
    d["ccT_o"] = b.dint("ccT_o", (32, 4))
    d["ccU_i"] = b.dint("ccU_i", (16, D), BF16)
    d["ccU_o"] = b.dint("ccU_o", (16, D), BF16)
    d["ccZ_i"] = b.dint("ccZ_i", (16, 1))
    d["ccZ_o"] = b.dint("ccZ_o", (16, 1))
    d["zscr"] = b.dint("zscr", (16, TLOC))
    return d


def alloc_common(b: Builder):
    t = {}
    t["ytm"] = b.sbuf("ytm", (128, NT * D), F32)
    t["yT"] = b.sbuf("yT", (128, ND * TLOC), BF16)
    t["big1"] = b.sbuf("big1", (128, ND * TLOC), BF16)
    t["wA0"] = b.sbuf("wA0", (128, 2048), BF16)
    t["wA1"] = b.sbuf("wA1", (128, 2048), BF16)
    t["wB0"] = b.sbuf("wB0", (128, 2048), BF16)
    t["wB1"] = b.sbuf("wB1", (128, 2048), BF16)
    t["wC0"] = b.sbuf("wC0", (128, 4096), BF16)
    t["wC1"] = b.sbuf("wC1", (128, 4096), BF16)
    t["phik0"] = b.sbuf("phik0", (128, NT * 128), BF16)
    t["phik1"] = b.sbuf("phik1", (128, NT * 128), BF16)
    t["vext0"] = b.sbuf("vext0", (128, NT * 129), BF16)
    t["vext1"] = b.sbuf("vext1", (128, NT * 129), BF16)
    t["phiq0"] = t["phik0"]
    t["phiq1"] = t["phik1"]
    t["zb"] = t["vext0"]
    t["zmask"] = b.sbuf("zmask", (128, 256), BF16)
    t["kv_gb"] = b.sbuf("kv_gb", (128, 2064), BF16)
    t["kv_loc"] = b.sbuf("kv_loc", (128, 2064), F32)
    t["z_sb"] = b.sbuf("z_sb", (16, TLOC), F32)
    t["z_row"] = b.sbuf("z_row", (1, TLOC), BF16)
    t["xrot"] = b.sbuf("xrot", (128, D), F32)
    t["scrA"] = b.sbuf("scrA", (128, D), F32)
    for n in ("c_ssq", "c_nm", "c_var", "c_rstd", "c_tmp"):
        t[n] = b.sbuf(n, (128, 16), F32)
    t["c_eps5"] = b.sbuf("c_eps5", (128, 1), F32)
    t["c_eps6"] = b.sbuf("c_eps6", (128, 1), F32)
    t["identb"] = b.sbuf("identb_sb", (128, 128), BF16)
    t["ones"] = b.sbuf("ones_sb", (1, 128), BF16)
    return t


def load_consts(b, d, t):
    b.dma("Y", t["identb"][:, :], d["identb"][:, :], "dmisc", writes=["identb_sb"])
    b.dma("G", t["ones"][:, :], d["ones_row"][0:1, 0:128], "dmisc2", writes=["ones_sb"])
    b.v(lambda e: e.memset(t["c_eps5"][:, :], 1e-5), reads=[], writes=["c_eps5"])
    b.v(lambda e: e.memset(t["c_eps6"][:, :], 1e-6), reads=[], writes=["c_eps6"])


def emit_ln(b, d, t, src_dram, dst_ytm, eps=1e-5, tag="ln", src_dep=()):
    for ti in range(NT):
        xr = t["xrot"][:, 0:D]
        xname = "xrot"
        b.dma("Y", xr, src_dram[ti * 128:(ti + 1) * 128, :], "dx",
              writes=[xname], reads=list(src_dep), tag=f"{tag}_ld{ti}")
        sq = t["scrA"]
        b.s(lambda e, xr=xr, sq=sq, ti=ti: e.activation(
            sq[:, :], xr, AF.Square, accum_out=t["c_ssq"][:, ti:ti + 1]),
            reads=[xname], writes=["scrA", "c_ssq"], tag=f"{tag}_sq{ti}")
        b.v(lambda e, xr=xr, ti=ti: e.reduce_sum(
            t["c_nm"][:, ti:ti + 1], xr, AX.X),
            reads=[xname], writes=["c_nm"], tag=f"{tag}_sum{ti}")
        b.v(lambda e, ti=ti: e.tensor_scalar(
            t["c_nm"][:, ti:ti + 1], t["c_nm"][:, ti:ti + 1], -1.0 / D, None,
            ALU.mult), reads=["c_nm"], writes=["c_nm"], tag=f"{tag}_nm{ti}")
        b.v(lambda e, ti=ti: e.tensor_mul(
            t["c_tmp"][:, ti:ti + 1], t["c_nm"][:, ti:ti + 1],
            t["c_nm"][:, ti:ti + 1]),
            reads=["c_nm"], writes=["c_tmp"], tag=f"{tag}_msq{ti}")
        b.v(lambda e, ti=ti: e.tensor_scalar(
            t["c_var"][:, ti:ti + 1], t["c_ssq"][:, ti:ti + 1], 1.0 / D,
            t["c_tmp"][:, ti:ti + 1], ALU.mult, ALU.subtract),
            reads=["c_ssq", "c_tmp"], writes=["c_var"], tag=f"{tag}_var{ti}")
        b.s(lambda e, ti=ti: e.activation(
            t["c_var"][:, ti:ti + 1], t["c_var"][:, ti:ti + 1], AF.Sqrt,
            bias=t["c_eps5"][:, 0:1]), reads=["c_var", "c_eps5"],
            writes=["c_var"], tag=f"{tag}_sd{ti}")
        b.v(lambda e, ti=ti: e.reciprocal(
            t["c_rstd"][:, ti:ti + 1], t["c_var"][:, ti:ti + 1]),
            reads=["c_var"], writes=["c_rstd"], tag=f"{tag}_rs{ti}")
        dst = dst_ytm[:, ti * D:(ti + 1) * D]
        b.v(lambda e, xr=xr, dst=dst, ti=ti: e.tensor_scalar(
            dst, xr, t["c_nm"][:, ti:ti + 1], t["c_rstd"][:, ti:ti + 1],
            ALU.add, ALU.mult),
            reads=[xname, "c_nm", "c_rstd"], writes=["ytm"],
            tag=f"{tag}_ap{ti}")


def emit_transpose_tm_fm(b, t, src, dst, psW, tag="tp"):
    for dj in range(ND):
        for ti in range(NT):
            b.tr(psW[:, ti * 128:(ti + 1) * 128],
                 src[:, ti * D + dj * 128: ti * D + dj * 128 + 128],
                 t["identb"][:, :],
                 reads=[src, "identb_sb"], writes=["psW"], inc=(ti == NT - 1),
                 tag=f"{tag}_t{dj}_{ti}")
        b.v(lambda e, dj=dj: e.tensor_copy(
            dst[:, dj * TLOC:(dj + 1) * TLOC], psW[:, :]),
            reads=["psW"], writes=[dst], tag=f"{tag}_ev{dj}")


def emit_phi_evac(b, t, ps, psname, dst, dstname, ncols, bias_col=None,
                  tag="phi"):
    scr1 = t["scrA"][:, :ncols]
    scr2 = t["scrA"][:, 512:512 + ncols]
    if bias_col is None:
        b.s(lambda e: e.activation(scr1, ps, AF.Relu, scale=-1.0),
            reads=[psname], writes=["scrA"], tag=f"{tag}_r")
    else:
        b.s(lambda e: e.activation(scr1, ps, AF.Relu, scale=-1.0,
                                   bias=bias_col[1]),
            reads=[psname, bias_col[2]], writes=["scrA"], tag=f"{tag}_r")
    b.s(lambda e: e.activation(scr2, scr1, AF.Exp, scale=-1.0),
        reads=["scrA"], writes=["scrA"], tag=f"{tag}_e")
    if bias_col is None:
        b.v(lambda e: e.scalar_tensor_tensor(dst, ps, 0.0, scr2, ALU.max,
                                             ALU.add),
            reads=[psname, "scrA"], writes=[dstname], tag=f"{tag}_c")
    else:
        b.v(lambda e: e.tensor_scalar(scr1, ps, bias_col[0], 0.0, ALU.add,
                                      ALU.max),
            reads=[psname, bias_col[2]], writes=["scrA"], tag=f"{tag}_m")
        b.v(lambda e: e.tensor_add(dst, scr1, scr2),
            reads=["scrA"], writes=[dstname], tag=f"{tag}_c")




BANKS8 = None


def psum_banks8(ps):
    return [
        (ps["A0"][:, 0:512], "psA0"), (ps["A1"][:, 0:512], "psA1"),
        (ps["A2"][:, 0:512], "psA2"), (ps["A3"][:, 0:512], "psA3"),
        (ps["B"][:, 0:512], "psB"), (ps["B"][:, 512:1024], "psB"),
        (ps["K"][:, 0:512], "psK"),
        (ps["W"][:, :].bitcast(F32)[:, 0:512], "psW"),
    ]


def emit_colblock_out(b, t, ps, wdram4, lhsT_fn, lhs_deps, bias_dram_row,
                      evac_fn, tag):
    banks = psum_banks8(ps)
    for cb in range(4):
        if bias_dram_row is not None and cb % 2 == 0:
            b.dma("G", t["z_row"][0:1, :],
                  bias_dram_row[0:1, cb * 512:cb * 512 + 1024], "drow",
                  reads=[], writes=["z_row"], tag=f"{tag}_br{cb}")
        for kh in range(2):
            sl = (2 * cb + kh) % 2
            b.dma("Y", bass.AP(t[f"wC{sl}"], 0, [[4096, 128], [512, 8], [1, 512]]),
                  wdram4[cb, kh], f"hc{sl}", writes=[f"wC{sl}"],
                  tag=f"{tag}_w{cb}_{kh}")
            for ti in range(NT):
                pa, pname = banks[ti]
                for kd in range(8):
                    last = (bias_dram_row is None and kh == 1 and kd == 7)
                    b.mm(pa, lhsT_fn(kh * 8 + kd, ti),
                         t[f"wC{sl}"][:, kd * 512:(kd + 1) * 512],
                         start=(kh == 0 and kd == 0), stop=last,
                         inc=(kd == 7),
                         reads=list(lhs_deps) + [f"wC{sl}"], writes=[pname],
                         tag=f"{tag}_m{cb}_{kh}_{ti}")
        for ti in range(NT):
            pa, pname = banks[ti]
            if bias_dram_row is not None:
                b.mm(pa, t["ones"][0:1, 0:128],
                     t["z_row"][0:1, (cb % 2) * 512:(cb % 2) * 512 + 512],
                     start=False, stop=True,
                     reads=["ones_sb", "z_row"], writes=[pname],
                     tag=f"{tag}_b{cb}_{ti}")
            evac_fn(cb, ti, pa, pname)


def emit_l0_attn(b, d, t, ps):
    emit_l0_attn_A(b, d, t, ps)
    emit_l0_attn_BC(b, d, t, ps)


def emit_l0_attn_A(b, d, t, ps):
    p = b.p
    for st in ("wk0", "wk1", "wv0", "wv1", "wq0", "wq1", "wo0", "wo1",
               "hw0", "hw1", "hc0", "hc1"):
        p.dma_stream(st, mode="slot")
    p.dma_stream("dcc", mode="barrier")
    for sl in range(2):
        vext = t[f"vext{sl}"]
        b.v(lambda e, vext=vext: e.memset(
            bass.AP(vext, 128, [[vext.shape[1], 128], [129, NT]]), 1.0),
            reads=[], writes=[f"vext{sl}"], tag=f"vones{sl}")

    pk = t["big1"]
    vg_off = 4096
    for ti in range(NT):
        b.v(lambda e, ti=ti: e.memset(
            bass.AP(t["big1"], vg_off + ti * 516 + 128,
                    [[ND * TLOC, 128], [129, 4]]), 1.0),
            reads=[], writes=["big1"], tag=f"vone{ti}")
    for g in range(4):
        for part, wnm in ((0, "WkT4"), (1, "WvT4")):
            for kh in range(2):
                sl = kh
                wc = t[f"wC{sl}"]
                b.dma("G", bass.AP(wc, 0, [[4096, 128], [512, 8], [1, 512]]),
                      d[wnm][g, kh], f"wo{sl}", writes=[f"wC{sl}"],
                      tag=f"ldkv{g}_{part}_{kh}")
            for ti in range(NT):
                pa = ps[f"A{ti % 4}"]
                for kh in range(2):
                    for kd in range(8):
                        b.mm(pa[:, 0:512],
                             t["yT"][:, (kh * 8 + kd) * TLOC + ti * 128:
                                     (kh * 8 + kd) * TLOC + ti * 128 + 128],
                             t[f"wC{kh}"][:, kd * 512:(kd + 1) * 512],
                             start=(kh == 0 and kd == 0),
                             stop=(kh == 1 and kd == 7),
                             reads=["yT", f"wC{kh}"], writes=[f"psA{ti % 4}"],
                             tag=f"kv{g}_{part}_{ti}")
                if part == 0:
                    emit_phi_evac(b, t, pa[:, 0:512], f"psA{ti % 4}",
                                  pk[:, ti * 512:(ti + 1) * 512], "big1", 512,
                                  tag=f"phk{g}_{ti}")
                else:
                    b.v(lambda e, ti=ti, pa=pa: e.tensor_copy(
                        bass.AP(t["big1"], vg_off + ti * 516,
                                [[ND * TLOC, 128], [129, 4], [1, 128]]),
                        bass.AP(pa, 0, [[512, 128], [128, 4], [1, 128]])),
                        reads=[f"psA{ti % 4}"], writes=["big1"],
                        tag=f"vev{g}_{ti}")
        kvbanks = [(ps["K"][:, 0:129], "psK"), (ps["B"][:, 0:129], "psB"),
                   (ps["B"][:, 512:641], "psB")]
        for hl in range(4):
            h = g * 4 + hl
            pa, pan = kvbanks[hl % 3]
            for ti in range(NT):
                b.mm(pa,
                     pk[:, ti * 512 + hl * 128: ti * 512 + hl * 128 + 128],
                     bass.AP(t["big1"], vg_off + ti * 516 + hl * 129,
                             [[ND * TLOC, 128], [1, 129]]),
                     start=(ti == 0), stop=(ti == NT - 1),
                     reads=["big1"], writes=[pan],
                     tag=f"kvm{h}_{ti}")
            b.v(lambda e, h=h, pa=pa: e.tensor_copy(
                t["kv_loc"][:, h * 129:(h + 1) * 129], pa),
                reads=[pan], writes=["kv_loc"], tag=f"kvev{h}")

    b.dma("Y", d["ccA_i"][:, :], t["kv_loc"][:, :], "dsA", reads=["kv_loc"],
          writes=["ccA_i"], tag="kv_out")
    p._declare_sem("cc0", 1)
    b.g(lambda e: e.collective_compute(
        "AllReduce", ALU.add, replica_groups=PAIRS,
        ins=[d["ccA_i"].ap().opt()], outs=[d["ccA_o"].ap().opt()]),
        reads=["ccA_i"], writes=["ccA_o"], sem="cc0", tag="cc_kv")


def emit_l0_attn_BC(b, d, t, ps):
    b.dma("G", t["kv_gb"][:, :], d["ccA_o"][:, :], "dlA", reads=["ccA_o"],
          writes=["kv_gb"], tag="kv_in")
    b.g(lambda e: e.memset(t["zmask"][:, :], 0.0), reads=[], writes=["zmask"],
        tag="zm0")
    for h in range(16):
        b.v(lambda e, h=h: e.tensor_copy(
            t["zmask"][:, h * 16 + h: h * 16 + h + 1],
            t["kv_gb"][:, h * 129 + 128: h * 129 + 129]),
            reads=["kv_gb", "zmask"], writes=["zmask"], tag=f"zm{h}")

    for h in range(16):
        sl = h % 2
        wq = t[f"wA{sl}"]
        phiq = t[f"phiq{sl}"]
        b.dma("G", bass.AP(wq, 0, [[2048, 128], [128, 16], [1, 128]]),
              d["WqT"][h], f"wq{sl}",
              writes=[f"wA{sl}"], tag=f"ldwq{h}")
        for c in range(2):
            for kd in range(ND):
                b.mm(ps["A0"][:, :512],
                     wq[:, kd * 128:(kd + 1) * 128],
                     t["yT"][:, kd * TLOC + c * 512: kd * TLOC + c * 512 + 512],
                     start=(kd == 0), stop=(kd == ND - 1),
                     reads=["yT", f"wA{sl}"], writes=["psA0"],
                     tag=f"qproj{h}_{c}")
            emit_phi_evac(b, t, ps["A0"][:, :512], "psA0",
                          phiq[:, c * 512:(c + 1) * 512], f"phik{sl}", 512,
                          tag=f"phiq{h}_{c}")
        for c in range(2):
            b.mm(ps["B"][0:16, c * 512:(c + 1) * 512],
                 t["zmask"][:, h * 16:(h + 1) * 16],
                 phiq[:, c * 512:(c + 1) * 512],
                 start=(h == 0), stop=(h == 15),
                 reads=["zmask", f"phik{sl}"], writes=["psB"],
                 tag=f"zmm{h}_{c}")
        for c in range(2):
            for _ in range(1):
                b.mm(ps["A1"][:, :512], t["kv_gb"][:, h * 129: h * 129 + 128],
                     phiq[:, c * 512:(c + 1) * 512], start=True, stop=True,
                     reads=["kv_gb", f"phik{sl}"], writes=["psA1"],
                     tag=f"pre{h}_{c}")
            b.v(lambda e, h=h, c=c: e.tensor_copy(
                t["big1"][:, h * TLOC + c * 512: h * TLOC + (c + 1) * 512],
                ps["A1"][:, :512]),
                reads=["psA1"], writes=["big1"], tag=f"preev{h}_{c}")
    b.v(lambda e: e.tensor_scalar(t["z_sb"][:, :], ps["B"][0:16, :], 1e-6,
                                  None, ALU.add),
        reads=["psB"], writes=["z_sb"], tag="zadd")
    b.v(lambda e: e.reciprocal(t["z_sb"][:, :], t["z_sb"][:, :]),
        reads=["z_sb"], writes=["z_sb"], tag="zrec")
    b.dma("Y", d["zscr"][:, :], t["z_sb"][:, :], "dsz", reads=["z_sb"],
          writes=["zscr"], tag="zrow_o")
    for h in range(16):
        b.dma("G", t["z_row"][0:1, :], d["zscr"][h:h + 1, :], "dcc",
              reads=["zscr"], writes=["z_row"], tag=f"zrow_i{h}")
        for c in range(2):
            b.mm(ps["B"][:, c * 512:(c + 1) * 512], t["ones"][0:1, 0:128],
                 t["z_row"][0:1, c * 512:(c + 1) * 512],
                 start=True, stop=True,
                 reads=["ones_sb", "z_row"], writes=["psB"], tag=f"zb{h}_{c}")
            b.v(lambda e, h=h, c=c: e.tensor_copy(
                t["zb"][:, c * 512:(c + 1) * 512], ps["B"][:, c * 512:(c + 1) * 512]),
                reads=["psB"], writes=["vext0"], tag=f"zbev{h}_{c}")
        b.v(lambda e, h=h: e.tensor_mul(
            t["big1"][:, h * TLOC:(h + 1) * TLOC],
            t["big1"][:, h * TLOC:(h + 1) * TLOC], t["zb"][:, 0:TLOC]),
            reads=["big1", "vext0"], writes=["big1"], tag=f"zap{h}")

    def lhs_c(kd, ti):
        return t["big1"][:, kd * TLOC + ti * 128: kd * TLOC + ti * 128 + 128]

    def evac_c(cb, ti, pa, pname):
        sb = t["phiq0"][:, (ti % 2) * 512:(ti % 2) * 512 + 512]
        b.v(lambda e, pa=pa, sb=sb: e.tensor_copy(sb, pa),
            reads=[pname], writes=["phik0"], tag=f"sev{cb}_{ti}")
        b.dma("Y", d["s_scr"][ti * 128:(ti + 1) * 128,
                              cb * 512:(cb + 1) * 512], sb, "dso",
              reads=["phik0"], writes=["s_scr"], tag=f"sst{cb}_{ti}")

    emit_colblock_out(b, t, ps, d["WoT4"], lhs_c, ["big1"], None, evac_c, "sC")


def alloc_psum(b: Builder):
    ps = {}
    ps["A0"] = b.psum("psA0", (128, 512), F32)
    ps["A1"] = b.psum("psA1", (128, 512), F32)
    ps["A2"] = b.psum("psA2", (128, 512), F32)
    ps["A3"] = b.psum("psA3", (128, 512), F32)
    ps["B"] = b.psum("psB", (128, 1024), F32)
    ps["K"] = b.psum("psK", (128, 512), F32)
    ps["W"] = b.psum("psW", (128, 1024), BF16)
    return ps


def finalize(b: Builder, out_waits):
    nc = b.nc
    from contextlib import ExitStack
    es = ExitStack()
    semobjs = {}
    with es:
        for name in b.p.sem_names:
            semobjs[name] = es.enter_context(nc.semaphore(name))
        block = es.enter_context(nc.Block())
        b.p.op("Y", lambda e: e.nop(), reads=list(out_waits), writes=(),
               inc=False, tag="final_wait") if False else None
        b.p.emit(block, semobjs)
    return nc


def build_stageB():
    nc = bass.Bass("TRN2", target_bir_lowering=False, debug=False)
    b = Builder(nc)
    d = declare_l0(b)
    so = b.dout("s_out", (TLOC, D), BF16)
    t = alloc_common(b)
    ps = alloc_psum(b)
    for st in ("dx", "dmisc", "dmisc2", "dout"):
        b.p.dma_stream(st)
    load_consts(b, d, t)
    emit_ln(b, d, t, d["x"], t["ytm"], tag="ln1")
    emit_transpose_tm_fm(b, t, t["ytm"], t["yT"], ps["W"], tag="tp1")
    emit_l0_attn(b, d, t, ps)
    for ti in range(NT):
        b.dma("Y", so[ti * 128:(ti + 1) * 128, :],
              t["sbuf_s"][:, ti * D:(ti + 1) * D], "dout",
              reads=["sbuf_s"], writes=["s_out"], tag=f"so{ti}")
    b.p.op("Y", lambda e: e.nop(), reads=["s_out"], writes=(), inc=False,
           tag="fin")
    return finalize(b, [])


def alloc_ffn(b: Builder, t):
    t["sm16a"] = b.sbuf("sm16a", (16, 1024), BF16)
    t["identf"] = b.sbuf("identf", (128, 128), F32)
    t["mcols"] = b.sbuf("mcols", (16, 16), F32)
    t["mg8"] = b.sbuf("mg8", (16, 8), F32)
    t["cptT"] = t["zmask"]
    t["ws"] = None
    t["pwb"] = None
    t["u_sm"] = None
    t["tkw"] = None
    t["logit_sm"] = None
    t["eT"] = None
    t["uT"] = b.sbuf("uT", (128, 256), BF16)
    t["cpth"] = b.sbuf("cpth", (128, 256), BF16)
    t["cptl"] = b.sbuf("cptl", (128, 256), BF16)
    t["cptF"] = None
    t["yloT"] = None
    t["wsT"] = t["uT"]


def emit_l0_ffn_iter(b, d, t, ps, it, dbg=None):
    emit_ffn_topk(b, d, t, ps, it, dbg)
    emit_ffn_u(b, d, t, ps, it, dbg)
    emit_ffn_q(b, d, t, ps, it)
    emit_ffn_written(b, d, t, ps, it, dbg)
    emit_ffn_scores(b, d, t, ps, it, dbg)
    emit_ffn_xb(b, d, t, ps, it, dbg)


def _ffn_env(b, d, t, ps, it):
    class E: pass
    env = E()
    env.ws = t["kv_loc"][0:16, 0:2048]
    env.u_sm = t["wC1"][0:16, 2048:4096]
    env.tkw = t["z_sb"][0:16, 0:1024]
    env.logit_sm = t["xrot"][0:16, 0:1024]
    env.e_bf = t["sm16a"]
    env.eT = t["phik0"][:, 0:128]
    env.att_s = t["vext1"]
    env.qT = t["phik1"]
    env.sA = t["scrA"]
    return env


def emit_ffn_topk(b, d, t, ps, it, dbg=None):
    p = b.p
    env = _ffn_env(b, d, t, ps, it)
    ws, u_sm, tkw, logit_sm = env.ws, env.u_sm, env.tkw, env.logit_sm
    e_bf, eT, att_s, qT, sA = env.e_bf, env.eT, env.att_s, env.qT, env.sA
    tks = t["xrot"][:, 1024:2048]

    if dbg:
        b.dma("Y", dbg["lg"].ap(), logit_sm[:, :], "dout", reads=["xrot"],
              writes=["dbg_lg"], tag="d_lg")
    b.v(lambda e: e.tensor_copy(tkw, logit_sm[:, :]),
        reads=["xrot"], writes=["z_sb"], tag=f"f{it}tk0")
    for i in range(4):
        b.v(lambda e, i=i: e.reduce_max(
            t["mcols"][:, i:i + 1], tkw, AX.X),
            reads=["z_sb"], writes=["mcols"], tag=f"f{it}tkm{i}")
        if i < 3:
            b.v(lambda e, i=i: e.tensor_scalar(
                tks[0:16, 0:1024], tkw, t["mcols"][:, i:i + 1], None,
                ALU.is_ge), reads=["z_sb", "mcols"], writes=["xrot"],
                tag=f"f{it}tke{i}")
            b.v(lambda e: e.scalar_tensor_tensor(
                tkw, tks[0:16, 0:1024], -1e30, tkw, ALU.mult, ALU.add),
                reads=["xrot", "z_sb"], writes=["z_sb"], tag=f"f{it}tkr{i}")
    b.dma("Y", d["ccT_i"][:, :], t["mcols"][:, 0:4], "dsT", reads=["mcols"],
          writes=["ccT_i"], tag=f"f{it}tko")
    p._declare_sem(f"ccT{it}", 1)
    b.g(lambda e: e.collective_compute(
        "AllGather", ALU.bypass, replica_groups=PAIRS,
        ins=[d["ccT_i"].ap().opt()], outs=[d["ccT_o"].ap().opt()]),
        reads=["ccT_i"], writes=["ccT_o"], sem=f"ccT{it}", tag=f"f{it}ccT")
    b.dma("G", bass.AP(t["mg8"], 0, [[8, 16], [4, 2], [1, 4]]),
          bass.AP(d["ccT_o"], 0, [[4, 16], [64, 2], [1, 4]]),
          "dmg", reads=["ccT_o"], writes=["mg8"], tag=f"f{it}tki")
    b.v(lambda e: e.tensor_copy(tks[0:16, 0:8], t["mg8"][:, :]),
        reads=["mg8"], writes=["xrot"], tag=f"f{it}mg0")
    for i in range(4):
        b.v(lambda e, i=i: e.reduce_max(
            t["mcols"][:, 8 + i: 9 + i], tks[0:16, 0:8], AX.X),
            reads=["xrot"], writes=["mcols"], tag=f"f{it}mgm{i}")
        if i < 3:
            b.v(lambda e, i=i: e.tensor_scalar(
                tks[0:16, 8:16], tks[0:16, 0:8], t["mcols"][:, 8 + i:9 + i],
                None, ALU.is_ge), reads=["xrot", "mcols"], writes=["xrot"],
                tag=f"f{it}mge{i}")
            b.v(lambda e: e.scalar_tensor_tensor(
                tks[0:16, 0:8], tks[0:16, 8:16], -1e30, tks[0:16, 0:8],
                ALU.mult, ALU.add),
                reads=["xrot"], writes=["xrot"], tag=f"f{it}mgr{i}")


def emit_ffn_u(b, d, t, ps, it, dbg=None):
    p = b.p
    env = _ffn_env(b, d, t, ps, it)
    ws, u_sm, tkw, logit_sm = env.ws, env.u_sm, env.tkw, env.logit_sm
    e_bf, eT, att_s, qT, sA = env.e_bf, env.eT, env.att_s, env.qT, env.sA

    b.v(lambda e: e.tensor_scalar(
        sA[0:16, 0:1024], logit_sm[:, :], t["mcols"][:, 8:9], None,
        ALU.subtract), reads=["xrot", "mcols"], writes=["scrA"],
        tag=f"f{it}es")
    b.s(lambda e: e.activation(tkw, sA[0:16, 0:1024], AF.Exp),
        reads=["scrA"], writes=["z_sb"], tag=f"f{it}ee")
    b.v(lambda e: e.tensor_scalar(
        sA[0:16, 0:1024], logit_sm[:, :], t["mcols"][:, 11:12], None,
        ALU.is_ge), reads=["xrot", "mcols"], writes=["scrA"], tag=f"f{it}em")
    b.v(lambda e: e.tensor_mul(tkw, tkw, sA[0:16, 0:1024]),
        reads=["z_sb", "scrA"], writes=["z_sb"], tag=f"f{it}emm")
    b.v(lambda e: e.tensor_copy(e_bf[:, :], tkw),
        reads=["z_sb"], writes=["sm16a"], tag=f"f{it}ec")
    b.v(lambda e: e.tensor_scalar(
        t["mcols"][:, 0:4], t["mcols"][:, 8:12], t["mcols"][:, 8:9], None,
        ALU.subtract), reads=["mcols"], writes=["mcols"], tag=f"f{it}z0")
    b.s(lambda e: e.activation(t["mcols"][:, 0:4], t["mcols"][:, 0:4],
                               AF.Exp),
        reads=["mcols"], writes=["mcols"], tag=f"f{it}z1")
    b.v(lambda e: e.reduce_sum(t["mcols"][:, 13:14], t["mcols"][:, 0:4],
                               AX.X),
        reads=["mcols"], writes=["mcols"], tag=f"f{it}z2")
    for ti in range(NT):
        b.tr(ps["W"][:, ti * 16:(ti + 1) * 16],
             e_bf[:, ti * 128:(ti + 1) * 128], t["identb"][0:16, 0:16],
             reads=["sm16a", "identb_sb"], writes=["psW"], inc=(ti == NT - 1),
             tag=f"f{it}etr{ti}")
    b.v(lambda e: e.tensor_copy(eT, ps["W"][:, 0:128]),
        reads=["psW"], writes=["phik0"], tag=f"f{it}etc")
    for c in range(4):
        for ti in range(NT):
            ybs = t["phik1"][:, (ti % 2) * 512:(ti % 2) * 512 + 512]
            b.v(lambda e, ybs=ybs, ti=ti, c=c: e.tensor_copy(
                ybs, t["ytm"][:, ti * D + c * 512: ti * D + (c + 1) * 512]),
                reads=["ytm"], writes=["phik1"], tag=f"f{it}ubc{c}_{ti}")
            b.mm(ps[f"A{c}"][0:16, 0:512], eT[:, ti * 16:(ti + 1) * 16],
                 ybs, start=(ti == 0), stop=(ti == NT - 1), inc=True,
                 reads=["phik0", "phik1"], writes=[f"psA{c}"],
                 tag=f"f{it}up{c}_{ti}")
        b.v(lambda e, c=c: e.tensor_copy(
            u_sm[:, c * 512:(c + 1) * 512], ps[f"A{c}"][0:16, 0:512]),
            reads=[f"psA{c}"], writes=["wC1"], tag=f"f{it}upe{c}")


def emit_ffn_written(b, d, t, ps, it, dbg=None):
    p = b.p
    env = _ffn_env(b, d, t, ps, it)
    ws, u_sm, tkw, logit_sm = env.ws, env.u_sm, env.tkw, env.logit_sm
    e_bf, eT, att_s, qT, sA = env.e_bf, env.eT, env.att_s, env.qT, env.sA
    if it == 0:
        b.g(lambda e: e.memset(t["vext0"][:, 0:512], 0.0), reads=[],
            writes=["vext0"], tag="kbd0")
        b.g(lambda e: e.memset(t["kv_gb"][0:32, 0:2048], 0.0), reads=[],
            writes=["kv_gb"], tag="vbd0")

    b.dma("Y", d["ccU_i"][:, :], u_sm, "dsU", reads=["wC1"], writes=["ccU_i"],
          tag=f"f{it}uo")
    p._declare_sem(f"ccU{it}", 1)
    b.g(lambda e: e.collective_compute(
        "AllReduce", ALU.add, replica_groups=PAIRS,
        ins=[d["ccU_i"].ap().opt()], outs=[d["ccU_o"].ap().opt()]),
        reads=["ccU_i"], writes=["ccU_o"], sem=f"ccU{it}",
        tag=f"f{it}ccU")
    b.dma("G", u_sm, d["ccU_o"][:, :], "dum", reads=["ccU_o"], writes=["wC1"],
          tag=f"f{it}ui")
    b.v(lambda e: e.reciprocal(t["mcols"][:, 14:15], t["mcols"][:, 13:14]),
        reads=["mcols"], writes=["mcols"], tag=f"f{it}zr")
    b.v(lambda e: e.tensor_scalar(
        u_sm, u_sm, t["mcols"][:, 14:15], None, ALU.mult),
        reads=["wC1", "mcols"], writes=["wC1"], tag=f"f{it}us")
    if dbg:
        b.dma("Y", dbg["mc"].ap(), t["mcols"][:, :], "dout", reads=["mcols"],
              writes=["dbg_mc"], tag="d_mc")
        b.dma("Y", dbg["u"].ap(), u_sm, "dout", reads=["wC1"],
              writes=["dbg_u"], tag="d_u")
    for half in range(2):
        for j in range(8):
            kd = half * 8 + j
            b.tr(ps["W"][:, j * 16:(j + 1) * 16],
                 u_sm[:, kd * 128:(kd + 1) * 128], t["identb"][0:16, 0:16],
                 reads=["wC1", "identb_sb"], writes=["psW"], inc=(j == 7),
                 tag=f"f{it}utr{kd}")
        b.v(lambda e, half=half: e.tensor_copy(
            t["uT"][:, half * 128:(half + 1) * 128], ps["W"][:, 0:128]),
            reads=["psW"], writes=["uT"], tag=f"f{it}utc{half}")
    for kd in range(ND):
        sl = kd % 2
        b.dma("Y", t[f"wB{sl}"][:, :],
              d["writeT"][kd * 128:(kd + 1) * 128, :], f"hw{sl}",
              writes=[f"wB{sl}"], tag=f"f{it}lww{kd}")
        for c in range(4):
            b.mm(ps[f"A{c}"][0:16, 0:512],
                 t["uT"][:, (kd % 8) * 16 + (kd // 8) * 128:
                         (kd % 8) * 16 + (kd // 8) * 128 + 16],
                 t[f"wB{sl}"][:, c * 512:(c + 1) * 512],
                 start=(kd == 0), stop=False, inc=(c == 3),
                 reads=["uT", f"wB{sl}"], writes=[f"psA{c}"],
                 tag=f"f{it}wr{kd}_{c}")
    for c in range(4):
        half = c // 2
        if c % 2 == 0:
            b.dma("G", t["z_row"][0:1, :],
                  d["writeb_row"][0:1, half * 1024:(half + 1) * 1024], "dcc",
                  reads=[], writes=["z_row"], tag=f"f{it}lwb{half}")
        b.mm(ps[f"A{c}"][0:16, 0:512], t["ones"][0:1, 0:16],
             t["z_row"][0:1, (c % 2) * 512:(c % 2) * 512 + 512],
             start=False, stop=True,
             reads=["ones_sb", "z_row"], writes=[f"psA{c}"],
             tag=f"f{it}wb{c}")
    for c in range(4):
        b.v(lambda e, c=c: e.tensor_tensor(
            ws[:, c * 512:(c + 1) * 512], ps[f"A{c}"][0:16, 0:512],
            ws[:, c * 512:(c + 1) * 512], ALU.add),
            reads=[f"psA{c}", "kv_loc"], writes=["kv_loc"],
            tag=f"f{it}wsa{c}")
    b.s(lambda e: e.activation(ps["B"][0:16, 0:1024], ws[:, 0:1024],
                               AF.Square, accum_out=t["mcols"][:, 15:16]),
        reads=["kv_loc"], writes=["psB", "mcols"], tag=f"f{it}wsq0")
    b.s(lambda e: e.activation(ps["B"][0:16, 0:1024], ws[:, 1024:2048],
                               AF.Square, accum_out=t["mcols"][:, 7:8]),
        reads=["kv_loc"], writes=["psB", "mcols"], tag=f"f{it}wsq1")
    b.v(lambda e: e.tensor_scalar(
        t["mcols"][:, 15:16], t["mcols"][:, 15:16], 1.0,
        t["mcols"][:, 7:8], ALU.mult, ALU.add),
        reads=["mcols"], writes=["mcols"], tag=f"f{it}wsq2")
    b.v(lambda e: e.tensor_scalar(
        t["mcols"][:, 15:16], t["mcols"][:, 15:16], 1.0 / D, None, ALU.mult),
        reads=["mcols"], writes=["mcols"], tag=f"f{it}wsq3")
    b.s(lambda e: e.activation(t["mcols"][:, 15:16], t["mcols"][:, 15:16],
                               AF.Sqrt, bias=t["c_eps6"][0:16, 0:1]),
        reads=["mcols", "c_eps6"], writes=["mcols"], tag=f"f{it}wsq4")
    b.v(lambda e: e.reciprocal(t["mcols"][:, 15:16], t["mcols"][:, 15:16]),
        reads=["mcols"], writes=["mcols"], tag=f"f{it}wsq5")
    for hf in range(2):
        b.dma("G", t["vext1"][:, 0:1024],
              d["postw_bcast"][:, hf * 1024:(hf + 1) * 1024], "dpwb",
              writes=["vext1"], tag=f"f{it}pw{hf}")
        b.v(lambda e, hf=hf: e.scalar_tensor_tensor(
            ws[:, hf * 1024:(hf + 1) * 1024],
            ws[:, hf * 1024:(hf + 1) * 1024], t["mcols"][:, 15:16],
            t["vext1"][0:16, 0:1024], ALU.mult, ALU.mult),
            reads=["kv_loc", "mcols", "vext1"], writes=["kv_loc"],
            tag=f"f{it}wsn{hf}")
    if dbg:
        b.dma("Y", dbg["ws"].ap(), ws, "dout", reads=["kv_loc"],
              writes=["dbg_ws"], tag="d_ws")
    for half in range(2):
        b.v(lambda e, half=half: e.tensor_copy(
            t["sm16a"][:, :], ws[:, half * 1024:(half + 1) * 1024]),
            reads=["kv_loc"], writes=["sm16a"], tag=f"f{it}wsbc{half}")
        for j in range(8):
            b.tr(ps["W"][:, j * 16:(j + 1) * 16],
                 t["sm16a"][:, j * 128:(j + 1) * 128], t["identb"][0:16, 0:16],
                 reads=["sm16a", "identb_sb"], writes=["psW"], inc=(j == 7),
                 tag=f"f{it}wst{half}_{j}")
        b.v(lambda e, half=half: e.tensor_copy(
            t["wsT"][:, half * 128:(half + 1) * 128], ps["W"][:, 0:128]),
            reads=["psW"], writes=["wsT"], tag=f"f{it}wstc{half}")
    banksKV = [
        (ps["A0"], "psA0"), (ps["A1"], "psA1"),
        (ps["A2"], "psA2"), (ps["A3"], "psA3"),
        (ps["B"][:, 0:512], "psB"), (ps["B"][:, 512:1024], "psB"),
        (ps["K"][0:128, 0:512], "psK"), (ps["K"], "psK"),
    ]
    for kd in range(ND):
        sl = kd % 2
        b.dma("G", t[f"wC{sl}"][:, :],
              d["inprojT"][kd * 128:(kd + 1) * 128, 2048:6144],
              f"wo{sl}", writes=[f"wC{sl}"], tag=f"f{it}lkv{kd}")
        for part in range(2):
            for c in range(4):
                idx = part * 4 + c
                pa, pan = banksKV[idx]
                po = pa[32:48, 0:512] if idx == 7 else pa[0:16, 0:512]
                b.mm(po,
                     t["wsT"][:, (kd // 8) * 128 + (kd % 8) * 16:
                              (kd // 8) * 128 + (kd % 8) * 16 + 16],
                     t[f"wC{sl}"][:, part * 2048 + c * 512:
                                  part * 2048 + (c + 1) * 512],
                     start=(kd == 0), stop=False, inc=(c == 3),
                     reads=["wsT", f"wC{sl}"], writes=[pan],
                     tag=f"f{it}kv{part}_{kd}_{c}")
    for part, ofs in enumerate([2048, 4096]):
        for c in range(4):
            half = c // 2
            idx = part * 4 + c
            pa, pan = banksKV[idx]
            po = pa[32:48, 0:512] if idx == 7 else pa[0:16, 0:512]
            if c % 2 == 0:
                b.dma("G", t["z_row"][0:1, :],
                      d["inprojb_row"][0:1, ofs + half * 1024:
                                       ofs + half * 1024 + 1024], "drow",
                      reads=[], writes=["z_row"], tag=f"f{it}lkb{part}_{half}")
            b.mm(po, t["ones"][0:1, 0:16],
                 t["z_row"][0:1, (c % 2) * 512:(c % 2) * 512 + 512],
                 start=False, stop=True,
                 reads=["ones_sb", "z_row"], writes=[pan],
                 tag=f"f{it}kvb{part}_{c}")
        for half in range(2):
            for cc in range(2):
                c = half * 2 + cc
                idx = part * 4 + c
                pa, pan = banksKV[idx]
                po = pa[32:48, 0:512] if idx == 7 else pa[0:16, 0:512]
                b.v(lambda e, po=po, cc=cc: e.tensor_copy(
                    t["sm16a"][:, cc * 512:(cc + 1) * 512], po),
                    reads=[pan], writes=["sm16a"],
                    tag=f"f{it}kve{part}_{c}")
            if part == 0:
                for j in range(8):
                    b.tr(ps["W"][:, j * 16:(j + 1) * 16],
                         t["sm16a"][:, j * 128:(j + 1) * 128],
                         t["identb"][0:16, 0:16],
                         reads=["sm16a", "identb_sb"], writes=["psW"],
                         inc=(j == 7), tag=f"f{it}kvt{half}_{j}")
                b.v(lambda e, half=half: e.tensor_copy(
                    bass.AP(t["vext0"], half * 256,
                            [[1032, 64], [32, 8], [1, 16]]),
                    bass.AP(ps["W"], 0, [[1024, 64], [16, 8], [1, 16]])),
                    reads=["psW"], writes=["vext0"], tag=f"f{it}kbdA{half}")
                b.v(lambda e, half=half: e.tensor_copy(
                    bass.AP(t["vext0"], 64 * 1032 + half * 256 + 16,
                            [[1032, 64], [32, 8], [1, 16]]),
                    bass.AP(ps["W"], 64 * 1024,
                            [[1024, 64], [16, 8], [1, 16]])),
                    reads=["psW"], writes=["vext0"], tag=f"f{it}kbdB{half}")
            else:
                b.v(lambda e, half=half: e.tensor_copy(
                    bass.AP(t["kv_gb"], half * 1024,
                            [[2064, 16], [128, 8], [1, 64]]),
                    bass.AP(t["sm16a"], 0, [[1024, 16], [128, 8], [1, 64]])),
                    reads=["sm16a"], writes=["kv_gb"], tag=f"f{it}vbdA{half}")
                b.dma("G",
                      bass.AP(t["kv_gb"], 16 * 2064 + half * 1024 + 64,
                              [[2064, 16], [128, 8], [1, 64]]),
                      bass.AP(t["sm16a"], 64, [[1024, 16], [128, 8], [1, 64]]),
                      "ds2s", reads=["sm16a"], writes=["kv_gb"],
                      tag=f"f{it}vbdB{half}")

    if dbg:
        b.dma("Y", dbg["kk"].ap(), t["vext0"][:, 0:512], "dout",
              reads=["vext0"], writes=["dbg_kk"], tag="d_kk")
        b.dma("Y", dbg["vv"].ap(), t["kv_gb"][0:32, 0:2048], "dout",
              reads=["kv_gb"], writes=["dbg_vv"], tag="d_vv")


def emit_ffn_slotattn(b, d, t, ps, it, dbg=None):
    emit_ffn_q(b, d, t, ps, it)
    emit_ffn_scores(b, d, t, ps, it, dbg)


def emit_ffn_q(b, d, t, ps, it):
    p = b.p
    env = _ffn_env(b, d, t, ps, it)
    for j in range(ND):
        sl = j % 2
        wq = t[f"wA{sl}"]
        b.dma("G", bass.AP(wq, 0, [[2048, 128], [128, 16], [1, 128]]),
              d["inprojQ"][j],
              f"wk{sl}", writes=[f"wA{sl}"], tag=f"f{it}lq{j}")
        pb, pbn = (ps["B"][:, 0:512], "psB") if j % 2 == 0 else                   (ps["B"][:, 512:1024], "psB")
        for c in range(2):
            pa, pan = psum_banks8(ps)[(2 * j + c) % 4]
            for kd in range(ND):
                b.mm(pa, wq[:, kd * 128:(kd + 1) * 128],
                     t["yT"][:, kd * TLOC + c * 512: kd * TLOC + c * 512 + 512],
                     start=(kd == 0), stop=(kd == ND - 1),
                     reads=["yT", f"wA{sl}"], writes=[pan],
                     tag=f"f{it}q{j}_{c}")
            b.v(lambda e, j=j, c=c, pa=pa: e.tensor_scalar(
                t["big1"][:, j * TLOC + c * 512: j * TLOC + (c + 1) * 512],
                pa, t["inprojb_sb"][:, j:j + 1], None, ALU.add),
                reads=[pan, "inprojb_sb"], writes=["big1"],
                tag=f"f{it}qe{j}_{c}")


def emit_ffn_scores(b, d, t, ps, it, dbg=None):
    p = b.p
    env = _ffn_env(b, d, t, ps, it)
    att_s, sA = env.att_s, env.sA
    if dbg:
        b.dma("Y", dbg["kk"].ap(), t["vext0"][:, 0:512], "dout",
              reads=["vext0"], writes=["dbg_kk"], tag="d_kk")
        b.dma("Y", dbg["vv"].ap(), t["kv_gb"][0:32, 0:2048], "dout",
              reads=["kv_gb"], writes=["dbg_vv"], tag="d_vv")
    for j in range(ND):
        qT = t["big1"][:, j * TLOC:(j + 1) * TLOC]
        for ti in range(NT):
            b.mm(ps["A2"][:, ti * 32: ti * 32 + 32],
                 qT[:, ti * 128:(ti + 1) * 128],
                 t["vext0"][:, j * 32:(j + 1) * 32],
                 start=True, stop=True, inc=(ti == NT - 1),
                 reads=["big1", "vext0"], writes=["psA2"],
                 tag=f"f{it}sc{j}_{ti}")
        b.s(lambda e: e.activation(sA[:, 0:256], ps["A2"][:, 0:256],
                                   AF.Exp, scale=0.125),
            reads=["psA2"], writes=["scrA"], tag=f"f{it}se{j}")
        b.v(lambda e: e.reduce_sum(
            sA[:, 256:272], bass.AP(t["scrA"], 0, [[2048, 128], [16, 16],
                                                   [1, 16]]), AX.X),
            reads=["scrA"], writes=["scrA"], tag=f"f{it}sd{j}")
        b.v(lambda e: e.reciprocal(sA[:, 272:288], sA[:, 256:272]),
            reads=["scrA"], writes=["scrA"], tag=f"f{it}sr{j}")
        b.v(lambda e: e.tensor_tensor(
            bass.AP(t["vext1"], 0, [[1032, 128], [16, 16], [1, 16]]),
            bass.AP(t["scrA"], 0, [[2048, 128], [16, 16], [1, 16]]),
            bass.AP(t["scrA"], 272, [[2048, 128], [1, 16], [0, 16]]),
            ALU.mult),
            reads=["scrA"], writes=["vext1"], tag=f"f{it}sa{j}")
        for g4 in range(2):
            for tl in range(4):
                ti = g4 * 4 + tl
                b.tr(ps["W"][0:32, tl * 128:(tl + 1) * 128],
                     att_s[:, ti * 32:(ti + 1) * 32], t["identb"][:, :],
                     reads=["vext1", "identb_sb"], writes=["psW"],
                     inc=(tl == 3), tag=f"f{it}st{j}_{ti}")
            b.v(lambda e: e.tensor_copy(
                att_s[0:32, 512:1024], ps["W"][0:32, 0:512]),
                reads=["psW"], writes=["vext1"], tag=f"f{it}ste{j}_{g4}")
            b.mm(ps["A3"][:, 0:512],
                 t["kv_gb"][0:32, j * 128:(j + 1) * 128],
                 att_s[0:32, 512:1024],
                 start=True, stop=True,
                 reads=["kv_gb", "vext1"], writes=["psA3"],
                 tag=f"f{it}cx{j}_{g4}")
            b.v(lambda e, j=j, g4=g4: e.tensor_copy(
                t["big1"][:, j * TLOC + g4 * 512: j * TLOC + (g4 + 1) * 512],
                ps["A3"][:, 0:512]),
                reads=["psA3"], writes=["big1"], tag=f"f{it}cxe{j}_{g4}")
    if dbg:
        b.dma("Y", dbg["ctx"].ap(), t["big1"][:, :], "dout", reads=["big1"],
              writes=["dbg_ctx"], tag="d_ctx")


def emit_ffn_xb(b, d, t, ps, it, dbg=None):
    p = b.p
    env = _ffn_env(b, d, t, ps, it)
    ws, u_sm, tkw, logit_sm = env.ws, env.u_sm, env.tkw, env.logit_sm
    e_bf, eT, att_s, qT, sA = env.e_bf, env.eT, env.att_s, env.qT, env.sA

    def lhs_x(kd, ti):
        return t["big1"][:, kd * TLOC + ti * 128: kd * TLOC + ti * 128 + 128]

    def evac_x(cb, ti, pa, pname):
        b.v(lambda e, cb=cb, ti=ti, pa=pa: e.tensor_tensor(
            t["ytm"][:, ti * D + cb * 512: ti * D + (cb + 1) * 512], pa,
            t["ytm"][:, ti * D + cb * 512: ti * D + (cb + 1) * 512], ALU.add),
            reads=[pname, "ytm"], writes=["ytm"], tag=f"f{it}ya{cb}_{ti}")
        b.s(lambda e, cb=cb, ti=ti: e.activation(
            ps["A3"][:, 0:512] if False else t["scrA"][:, 1024:1536],
            t["ytm"][:, ti * D + cb * 512: ti * D + (cb + 1) * 512],
            AF.Square, accum_out=t["c_tmp"][:, ti:ti + 1]),
            reads=["ytm"], writes=["scrA", "c_tmp"],
            tag=f"f{it}ysq{cb}_{ti}")
        if cb == 0:
            b.v(lambda e, ti=ti: e.tensor_copy(
                t["c_ssq"][:, ti:ti + 1], t["c_tmp"][:, ti:ti + 1]),
                reads=["c_tmp"], writes=["c_ssq"], tag=f"f{it}yac{cb}_{ti}")
        else:
            b.v(lambda e, ti=ti: e.tensor_add(
                t["c_ssq"][:, ti:ti + 1], t["c_ssq"][:, ti:ti + 1],
                t["c_tmp"][:, ti:ti + 1]),
                reads=["c_tmp", "c_ssq"], writes=["c_ssq"],
                tag=f"f{it}yac{cb}_{ti}")

    emit_colblock_out(b, t, ps, d["outpT4"], lhs_x, ["big1"],
                      d["outpb_row"], evac_x, f"xb{it}")
    for ti in range(NT):
        b.v(lambda e, ti=ti: e.tensor_scalar(
            t["c_ssq"][:, ti:ti + 1], t["c_ssq"][:, ti:ti + 1], 1.0 / D, None,
            ALU.mult), reads=["c_ssq"], writes=["c_ssq"], tag=f"f{it}yn0{ti}")
        b.s(lambda e, ti=ti: e.activation(
            t["c_ssq"][:, ti:ti + 1], t["c_ssq"][:, ti:ti + 1], AF.Sqrt,
            bias=t["c_eps6"][:, 0:1]),
            reads=["c_ssq", "c_eps6"], writes=["c_ssq"], tag=f"f{it}yn1{ti}")
        b.v(lambda e, ti=ti: e.reciprocal(
            t["c_rstd"][:, ti:ti + 1], t["c_ssq"][:, ti:ti + 1]),
            reads=["c_ssq"], writes=["c_rstd"], tag=f"f{it}yn2{ti}")
        for hf in range(2):
            if ti == 0:
                b.dma("G", t["vext1"][:, 0:1024] if hf == 0 else
                      t["phik1"][:, 0:1024],
                      d["postw_bcast"][:, hf * 1024:(hf + 1) * 1024], "dpwb",
                      writes=["vext1" if hf == 0 else "phik1"],
                      tag=f"f{it}ypw{hf}")
            buf = t["vext1"][:, 0:1024] if hf == 0 else t["phik1"][:, 0:1024]
            bname = "vext1" if hf == 0 else "phik1"
            b.v(lambda e, ti=ti, hf=hf, buf=buf: e.scalar_tensor_tensor(
                t["ytm"][:, ti * D + hf * 1024: ti * D + (hf + 1) * 1024],
                t["ytm"][:, ti * D + hf * 1024: ti * D + (hf + 1) * 1024],
                t["c_rstd"][:, ti:ti + 1], buf, ALU.mult, ALU.mult),
                reads=["ytm", "c_rstd", bname], writes=["ytm"],
                tag=f"f{it}yap{ti}_{hf}")


def emit_l0_ffn_prep(b, d, t):
    b.dma("Y", t["kv_loc"][0:16, 0:2048], d["workspace"][:, :], "dws",
          writes=["kv_loc"], tag="ldws")
    b.dma("Y", t["inprojb_sb"][:, :], d["inprojb_col"][:, :], "dib",
          writes=["inprojb_sb"], tag="ldib")



def emit_l0_combine(b, d, t, x1_out, fuse_ln=False, ps_g=None):
    for ti in range(NT):
        xr = t["xrot"][:, 0:D] if ti % 2 == 0 else t["kv_loc"][:, 0:2048]
        xrn = "xrot" if ti % 2 == 0 else "kv_loc"
        st = f"dx{ti % 2}"
        b.dma("Y", xr, d["x"][ti * 128:(ti + 1) * 128, :], st,
              writes=[xrn], tag=f"cb_ld{ti}")
        for c in range(2):
            sl = c * 1024
            sbuf = t["phiq1"][:, 0:1024] if c == 0 else t["phiq0"][:, 0:1024]
            sbn = "phik1" if c == 0 else "phik0"
            scr = t["scrA"][:, 0:1024] if c == 0 else t["scrA"][:, 1024:2048]
            b.dma("Y", sbuf, d["s_scr"][ti * 128:(ti + 1) * 128,
                                        sl:sl + 1024], f"dsi{c}",
                  reads=["s_scr"], writes=[sbn], tag=f"cb_ls{ti}_{c}")
            b.s(lambda e, ti=ti, sl=sl, scr=scr: e.activation(
                scr, t["ytm"][:, ti * D + sl: ti * D + sl + 1024], AF.Silu),
                reads=["ytm"], writes=["scrA"], tag=f"cb_si{ti}_{c}")
            b.v(lambda e, sbuf=sbuf, scr=scr: e.tensor_tensor(
                scr, scr, sbuf, ALU.mult),
                reads=["scrA", sbn], writes=["scrA"],
                tag=f"cb_m{ti}_{c}")
            b.v(lambda e, xr=xr, sl=sl, scr=scr: e.tensor_add(
                xr[:, sl:sl + 1024], xr[:, sl:sl + 1024], scr),
                reads=["scrA", xrn], writes=[xrn], tag=f"cb_a{ti}_{c}")
        b.dma("Y", x1_out[ti * 128:(ti + 1) * 128, :], xr,
              "dout", reads=[xrn], writes=["x1_out"], tag=f"cb_st{ti}")
        if fuse_ln:
            emit_ln_tile(b, t, ti, tag="lnf", xr=xr, xrn=xrn)


def emit_ln_tile(b, t, ti, eps=1e-5, tag="ln", xr=None, xrn=None):
    if xr is None:
        xr = t["xrot"][:, 0:D]
        xrn = "xrot"
    xname = xrn
    b.s(lambda e, xr=xr, ti=ti: e.activation(
        t["scrA"][:, 0:1024], xr[:, 0:1024], AF.Square,
        accum_out=t["c_ssq"][:, ti:ti + 1]),
        reads=[xname], writes=["scrA", "c_ssq"], tag=f"{tag}_sqa{ti}")
    b.s(lambda e, xr=xr, ti=ti: e.activation(
        t["scrA"][:, 0:1024], xr[:, 1024:2048], AF.Square,
        accum_out=t["c_tmp"][:, ti:ti + 1]),
        reads=[xname, "scrA"], writes=["scrA", "c_tmp"], tag=f"{tag}_sqb{ti}")
    b.v(lambda e, ti=ti: e.tensor_scalar(
        t["c_ssq"][:, ti:ti + 1], t["c_ssq"][:, ti:ti + 1], 1.0,
        t["c_tmp"][:, ti:ti + 1], ALU.mult, ALU.add),
        reads=["c_ssq", "c_tmp"], writes=["c_ssq"], tag=f"{tag}_ss{ti}")
    b.v(lambda e, xr=xr, ti=ti: e.reduce_sum(
        t["c_nm"][:, ti:ti + 1], xr, AX.X),
        reads=[xname], writes=["c_nm"], tag=f"{tag}_sum{ti}")
    b.v(lambda e, ti=ti: e.tensor_scalar(
        t["c_nm"][:, ti:ti + 1], t["c_nm"][:, ti:ti + 1], -1.0 / D, None,
        ALU.mult), reads=["c_nm"], writes=["c_nm"], tag=f"{tag}_nm{ti}")
    b.v(lambda e, ti=ti: e.tensor_mul(
        t["c_tmp"][:, ti:ti + 1], t["c_nm"][:, ti:ti + 1],
        t["c_nm"][:, ti:ti + 1]),
        reads=["c_nm"], writes=["c_tmp"], tag=f"{tag}_msq{ti}")
    b.v(lambda e, ti=ti: e.tensor_scalar(
        t["c_var"][:, ti:ti + 1], t["c_ssq"][:, ti:ti + 1], 1.0 / D,
        t["c_tmp"][:, ti:ti + 1], ALU.mult, ALU.subtract),
        reads=["c_ssq", "c_tmp"], writes=["c_var"], tag=f"{tag}_var{ti}")
    b.s(lambda e, ti=ti: e.activation(
        t["c_var"][:, ti:ti + 1], t["c_var"][:, ti:ti + 1], AF.Sqrt,
        bias=t["c_eps5"][:, 0:1]), reads=["c_var", "c_eps5"],
        writes=["c_var"], tag=f"{tag}_sd{ti}")
    b.v(lambda e, ti=ti: e.reciprocal(
        t["c_rstd"][:, ti:ti + 1], t["c_var"][:, ti:ti + 1]),
        reads=["c_var"], writes=["c_rstd"], tag=f"{tag}_rs{ti}")
    b.v(lambda e, xr=xr, ti=ti: e.tensor_scalar(
        t["ytm"][:, ti * D:(ti + 1) * D], xr, t["c_nm"][:, ti:ti + 1],
        t["c_rstd"][:, ti:ti + 1], ALU.add, ALU.mult),
        reads=[xname, "c_nm", "c_rstd"], writes=["ytm"], tag=f"{tag}_ap{ti}")


def build_stageC():
    nc = bass.Bass("TRN2", target_bir_lowering=False, debug=False)
    b = Builder(nc)
    d = declare_l0(b)
    d["inprojb_row"] = b.din("inprojb_row", (1, 3 * D))
    d["postw_bcast"] = b.din("postw_bcast", (128, D))
    d["identf"] = b.din("identf_in", (128, 128))
    x1o = b.dout("x1_out", (TLOC, D), F32)
    d["s_scr"] = b.dint("s_scr", (TLOC, D), BF16)
    t = alloc_common(b)
    alloc_ffn(b, t)
    t["competeb_sb"] = b.sbuf("competeb_sb", (16, 1), F32)
    t["inprojb_sb"] = b.sbuf("inprojb_sb", (128, 48), F32)
    t["prew_sb"] = t["inprojb_sb"][:, 16:32]
    ps = alloc_psum(b)
    for st in ("dx", "dmisc", "dmisc2", "dout"):
        b.p.dma_stream(st, mode="slot" if st != "dout" else "barrier")
    for st in ("dws", "dcpt", "dpw", "dpwb", "dcb", "dib", "dif", "drow",
               "ds2s", "dsA", "dlA", "dsz", "dsT", "dmg", "dsU", "dum", "dcptF", "dso", "dsi"):
        b.p.dma_stream(st, mode="slot")
    load_consts(b, d, t)
    emit_ln(b, d, t, d["x"], t["ytm"], tag="ln1")
    emit_cpt_prep(b, d, t)
    emit_tp_logits(b, d, t, ps, 0)
    emit_l0_attn(b, d, t, ps)
    emit_l0_ffn_prep(b, d, t)
    emit_l0_ffn_iter(b, d, t, ps, 0)
    emit_tp_logits(b, d, t, ps, 1)
    emit_l0_ffn_iter(b, d, t, ps, 1)
    emit_l0_combine(b, d, t, x1o)
    b.p.op("Y", lambda e: e.nop(), reads=["x1_out"], writes=(), inc=False,
           tag="fin")
    return finalize(b, [])


def build_stageC_dbg():
    nc = bass.Bass("TRN2", target_bir_lowering=False, debug=False)
    b = Builder(nc)
    d = declare_l0(b)
    d["inprojb_row"] = b.din("inprojb_row", (1, 3 * D))
    d["postw_bcast"] = b.din("postw_bcast", (128, D))
    d["identf"] = b.din("identf_in", (128, 128))
    x1o = b.dout("x1_out", (TLOC, D), F32)
    dbg = {
        "lg": b.dout("dbg_lg", (16, TLOC), F32),
        "mc": b.dout("dbg_mc", (16, 16), F32),
        "u": b.dout("dbg_u", (16, D), BF16),
        "ws": b.dout("dbg_ws", (16, D), F32),
        "kk": b.dout("dbg_kk", (128, 512), BF16),
        "vv": b.dout("dbg_vv", (32, D), BF16),
        "ctx": b.dout("dbg_ctx", (128, ND * TLOC), BF16),
        "att": b.dout("dbg_att", (128, 64), BF16),
        "q": b.dout("dbg_q", (128, TLOC), BF16),
    }
    dbg_y = b.dout("dbg_y", (TLOC, D), F32)
    d["s_scr"] = b.dint("s_scr", (TLOC, D), BF16)
    t = alloc_common(b)
    alloc_ffn(b, t)
    t["competeb_sb"] = b.sbuf("competeb_sb", (16, 1), F32)
    t["inprojb_sb"] = b.sbuf("inprojb_sb", (128, 48), F32)
    t["prew_sb"] = t["inprojb_sb"][:, 16:32]
    ps = alloc_psum(b)
    for st in ("dx", "dmisc", "dmisc2"):
        b.p.dma_stream(st, mode="slot")
    b.p.dma_stream("dout", mode="barrier")
    for st in ("dws", "dcpt", "dpw", "dpwb", "dcb", "dib", "dif", "drow",
               "ds2s", "dsA", "dlA", "dsz", "dsT", "dmg", "dsU", "dum", "dcptF", "dso", "dsi"):
        b.p.dma_stream(st, mode="slot")
    load_consts(b, d, t)
    emit_ln(b, d, t, d["x"], t["ytm"], tag="ln1")
    emit_cpt_prep(b, d, t)
    emit_tp_logits(b, d, t, ps, 0)
    emit_l0_attn(b, d, t, ps)
    emit_l0_ffn_prep(b, d, t)
    emit_l0_ffn_iter(b, d, t, ps, 0, dbg=dbg)
    for ti in range(NT):
        b.dma("Y", dbg_y[ti * 128:(ti + 1) * 128, :],
              t["ytm"][:, ti * D:(ti + 1) * D], "dout", reads=["ytm"],
              writes=["dbg_y"], tag=f"dy{ti}")
    emit_tp_logits(b, d, t, ps, 1)
    emit_l0_ffn_iter(b, d, t, ps, 1)
    emit_l0_combine(b, d, t, x1o)
    b.p.op("Y", lambda e: e.nop(),
           reads=["x1_out", "dbg_lg", "dbg_mc", "dbg_u", "dbg_ws", "dbg_y",
                  "dbg_kk", "dbg_vv", "dbg_ctx", "dbg_att", "dbg_q"],
           writes=(), inc=False, tag="fin")
    return finalize(b, [])


def emit_cpt_prep(b, d, t):
    cptF = t["xrot"][:, 1024:1280]
    b.dma("Y", t["identf"][:, :], d["identf"][:, :], "dif",
          writes=["identf"], tag="ldif")
    b.dma("Y", t["competeb_sb"][:, :], d["competeb_col"][:, :], "dcb",
          writes=["competeb_sb"], tag="ldcb")
    b.dma("Y", bass.AP(t["xrot"], 1024, [[2048, 128], [16, 16], [1, 16]]),
          d["competeT"].ap().rearrange("(k p) c -> p k c", p=128), "dcptF",
          writes=["xrot"], tag="ldcptF")
    b.dma("Y", t["prew_sb"], d["prew_col"][:, :], "dpw",
          writes=["inprojb_sb"], tag="ldprew")
    for kd in range(ND):
        b.v(lambda e, kd=kd, cptF=cptF: e.tensor_scalar(
            cptF[:, kd * 16:(kd + 1) * 16], cptF[:, kd * 16:(kd + 1) * 16],
            t["inprojb_sb"][:, 16 + kd:17 + kd], None, ALU.mult),
            reads=["xrot", "inprojb_sb"], writes=["xrot"], tag=f"cpts{kd}")
    b.v(lambda e, cptF=cptF: e.tensor_copy(t["cpth"][:, :], cptF),
        reads=["xrot"], writes=["cpth"], tag="cpth")
    b.v(lambda e, cptF=cptF: e.tensor_tensor(t["cptl"][:, :], cptF,
                                             t["cpth"][:, :], ALU.subtract),
        reads=["xrot", "cpth"], writes=["cptl"], tag="cptl")


def emit_tp_logits(b, d, t, ps, it):
    logit_sm = t["xrot"][0:16, 0:1024]
    sA = t["scrA"]
    for ti in range(NT):
        for c in range(2):
            hi = t["phik1"][:, 0:1024]
            lo = t["vext1"][:, 0:1024]
            b.s(lambda e, ti=ti, c=c, hi=hi: e.copy(
                hi, t["ytm"][:, ti * D + c * 1024: ti * D + (c + 1) * 1024]),
                reads=["ytm"], writes=["phik1"], tag=f"tl{it}h{ti}_{c}")
            b.v(lambda e, ti=ti, c=c, hi=hi, lo=lo: e.tensor_tensor(
                lo, t["ytm"][:, ti * D + c * 1024: ti * D + (c + 1) * 1024],
                hi, ALU.subtract),
                reads=["ytm", "phik1"], writes=["vext1"], tag=f"tl{it}l{ti}_{c}")
            for j in range(8):
                b.tr(ps["W"][:, j * 128:(j + 1) * 128],
                     hi[:, j * 128:(j + 1) * 128], t["identb"][:, :],
                     reads=["phik1", "identb_sb"], writes=["psW"],
                     inc=(j == 7), tag=f"tl{it}th{ti}_{c}_{j}")
            b.v(lambda e, ti=ti, c=c: e.tensor_copy(
                bass.AP(t["yT"], c * 8 * TLOC + ti * 128,
                        [[ND * TLOC, 128], [TLOC, 8], [1, 128]]),
                ps["W"][:, :]),
                reads=["psW"], writes=["yT"], tag=f"tl{it}eh{ti}_{c}")
            for j in range(8):
                b.tr(ps["W"][:, j * 128:(j + 1) * 128],
                     lo[:, j * 128:(j + 1) * 128], t["identb"][:, :],
                     reads=["vext1", "identb_sb"], writes=["psW"],
                     inc=(j == 7), tag=f"tl{it}tl{ti}_{c}_{j}")
            b.v(lambda e, ti=ti, c=c: e.tensor_copy(
                t["wC0"][:, c * 1024:(c + 1) * 1024], ps["W"][:, :]),
                reads=["psW"], writes=["wC0"], tag=f"tl{it}el{ti}_{c}")
        for kd in range(ND):
            yh = t["yT"][:, kd * TLOC + ti * 128: kd * TLOC + ti * 128 + 128]
            yl = t["wC0"][:, kd * 128:(kd + 1) * 128]
            b.mm(ps["A2"][:, 0:16], yh, t["cpth"][:, kd * 16:(kd + 1) * 16],
                 start=(kd == 0), stop=False, inc=False,
                 reads=["yT", "cpth"], writes=["psA2"], tag=f"tl{it}m1_{ti}")
            b.mm(ps["A2"][:, 0:16], yh, t["cptl"][:, kd * 16:(kd + 1) * 16],
                 start=False, stop=False, inc=False,
                 reads=["yT", "cptl"], writes=["psA2"], tag=f"tl{it}m2_{ti}")
            b.mm(ps["A2"][:, 0:16], yl, t["cpth"][:, kd * 16:(kd + 1) * 16],
                 start=False, stop=(kd == ND - 1),
                 reads=["wC0", "cpth"], writes=["psA2"], tag=f"tl{it}m3_{ti}")
        b.s(lambda e, ti=ti: e.activation(
            ps["B"][:, 0:1024], t["ytm"][:, ti * D: ti * D + 1024], AF.Square,
            accum_out=t["c_tmp"][:, ti:ti + 1]),
            reads=["ytm"], writes=["psB", "c_tmp"], tag=f"tl{it}ra{ti}")
        b.s(lambda e, ti=ti: e.activation(
            ps["B"][:, 0:1024], t["ytm"][:, ti * D + 1024: ti * D + 2048],
            AF.Square, accum_out=t["c_var"][:, ti:ti + 1]),
            reads=["ytm"], writes=["psB", "c_var"], tag=f"tl{it}rb{ti}")
        b.v(lambda e, ti=ti: e.tensor_scalar(
            t["c_ssq"][:, ti:ti + 1], t["c_tmp"][:, ti:ti + 1], 1.0,
            t["c_var"][:, ti:ti + 1], ALU.mult, ALU.add),
            reads=["c_tmp", "c_var"], writes=["c_ssq"], tag=f"tl{it}rc{ti}")
        b.v(lambda e, ti=ti: e.tensor_scalar(
            t["c_ssq"][:, ti:ti + 1], t["c_ssq"][:, ti:ti + 1], 1.0 / D, None,
            ALU.mult), reads=["c_ssq"], writes=["c_ssq"], tag=f"tl{it}rd{ti}")
        b.s(lambda e, ti=ti: e.activation(
            t["c_ssq"][:, ti:ti + 1], t["c_ssq"][:, ti:ti + 1], AF.Sqrt,
            bias=t["c_eps6"][:, 0:1]),
            reads=["c_ssq", "c_eps6"], writes=["c_ssq"], tag=f"tl{it}re{ti}")
        b.v(lambda e, ti=ti: e.reciprocal(
            t["c_rstd"][:, ti:ti + 1], t["c_ssq"][:, ti:ti + 1]),
            reads=["c_ssq"], writes=["c_rstd"], tag=f"tl{it}rf{ti}")
        b.v(lambda e, ti=ti: e.tensor_scalar(
            sA[:, 0:16], ps["A2"][:, 0:16], t["c_rstd"][:, ti:ti + 1], None,
            ALU.mult), reads=["psA2", "c_rstd"], writes=["scrA"],
            tag=f"tl{it}sc{ti}")
        b.tr(ps["K"][0:16, 0:128], sA[:, 0:16], t["identf"][:, :],
             reads=["scrA", "identf"], writes=["psK"], tag=f"tl{it}tr{ti}")
        b.v(lambda e, ti=ti: e.tensor_scalar(
            logit_sm[:, ti * 128:(ti + 1) * 128], ps["K"][0:16, 0:128],
            t["competeb_sb"][:, 0:1], None, ALU.add),
            reads=["psK", "competeb_sb"], writes=["xrot"], tag=f"tl{it}ev{ti}")


def host_prep_l1(ins):
    import ml_dtypes
    P = {}
    P["Wq1T"] = tile_cols(np.ascontiguousarray(ins["L1_attn_Wq"].T), 128)
    P["Wk1T"] = tile_cols(np.ascontiguousarray(ins["L1_attn_Wk"].T), 128)
    P["Wv1T"] = tile_cols(np.ascontiguousarray(ins["L1_attn_Wv"].T), 128)
    P["Wo1T4"] = tile_cols_halfk(np.ascontiguousarray(
        ins["L1_attn_Wo"].T), 512).astype(ml_dtypes.bfloat16)
    P["gateT"] = tile_cols(np.ascontiguousarray(
        ins["L1_ffn_gate_w"].T), 256).astype(ml_dtypes.bfloat16)
    P["upT"] = tile_cols(np.ascontiguousarray(
        ins["L1_ffn_up_w"].T), 256).astype(ml_dtypes.bfloat16)
    dT = np.ascontiguousarray(ins["L1_ffn_down_w"].T)
    P["downT"] = np.ascontiguousarray(
        dT.reshape(44, 128, 2, 1024).transpose(2, 0, 1, 3)).astype(
            ml_dtypes.bfloat16)
    l1b = np.concatenate([ins["L1_attn_bq"], ins["L1_attn_bk"],
                          ins["L1_attn_bv"]])
    P["l1b_col"] = col_layout(l1b)
    P["bo_row"] = np.ascontiguousarray(ins["L1_attn_bo"][None, :])
    P["gates_col"] = col_layout(ins["L1_ffn_gate_s"])
    P["ups_col"] = col_layout(ins["L1_ffn_up_s"])
    P["ds_bcast"] = np.ascontiguousarray(
        np.broadcast_to(ins["L1_ffn_down_s"][None, :], (128, 2048)))
    return P


def declare_l1(b: Builder):
    d = {}
    for n, sh in [("Wq1T", (16, 128, 16, 128)), ("Wk1T", (16, 128, 16, 128)),
                  ("Wv1T", (16, 128, 16, 128)),
                  ("l1b_col", (128, 48)),
                  ("bo_row", (1, D)), ("gates_col", (128, 44)),
                  ("ups_col", (128, 44)), ("ds_bcast", (128, D))]:
        d[n] = b.din(n, sh)
    d["downT"] = b.din("downT", (2, 44, 128, 1024), BF16)
    d["gateT"] = b.din("gateT", (22, 128, 16, 256), BF16)
    d["upT"] = b.din("upT", (22, 128, 16, 256), BF16)
    d["Wo1T4"] = b.din("Wo1T4", (4, 2, 128, 8, 512), BF16)
    d["halfsel"] = b.din("halfsel", (128, 1))
    d["ccC_i"] = b.dint("ccC_i", (128, 16))
    d["ccC_o"] = b.dint("ccC_o", (256, 16))
    return d


def emit_tp_plain(b, d, t, ps, tag="tpp"):
    for ti in range(NT):
        for c in range(2):
            hi = t["phik1"][:, 0:1024]
            b.v(lambda e, ti=ti, c=c, hi=hi: e.tensor_copy(
                hi, t["ytm"][:, ti * D + c * 1024: ti * D + (c + 1) * 1024]),
                reads=["ytm"], writes=["phik1"], tag=f"{tag}h{ti}_{c}")
            for j in range(8):
                b.tr(ps["W"][:, j * 128:(j + 1) * 128],
                     hi[:, j * 128:(j + 1) * 128], t["identb"][:, :],
                     reads=["phik1", "identb_sb"], writes=["psW"],
                     inc=(j == 7), tag=f"{tag}t{ti}_{c}_{j}")
            b.v(lambda e, ti=ti, c=c: e.tensor_copy(
                bass.AP(t["yT"], c * 8 * TLOC + ti * 128,
                        [[ND * TLOC, 128], [TLOC, 8], [1, 128]]),
                ps["W"][:, :]),
                reads=["psW"], writes=["yT"], tag=f"{tag}e{ti}_{c}")


def emit_l1_attn(b, d1, t, ps):
    p = b.p
    b.dma("Y", t["inprojb_sb"][:, :], d1["l1b_col"][:, :], "dib",
          writes=["inprojb_sb"], tag="l1b")
    b.v(lambda e: e.tensor_scalar(t["c_ssq"][:, :], t["inprojb_sb"][:, 0:16],
                                  -1.0, None, ALU.mult),
        reads=["inprojb_sb"], writes=["c_ssq"], tag="l1nbq")
    b.v(lambda e: e.tensor_scalar(t["c_tmp"][:, :], t["inprojb_sb"][:, 16:32],
                                  -1.0, None, ALU.mult),
        reads=["inprojb_sb"], writes=["c_tmp"], tag="l1nbk")
    phik = t["xrot"][:, 0:1024]
    vbuf = t["xrot"][:, 1024:2048]
    for j in range(ND):
        sl = j % 2
        wk, wv = t[f"wA{sl}"], t[f"wB{sl}"]
        b.dma("G", bass.AP(wk, 0, [[2048, 128], [128, 16], [1, 128]]),
              d1["Wk1T"][j], f"wk{sl}",
              writes=[f"wA{sl}"], tag=f"l1k{j}")
        b.dma("G", bass.AP(wv, 0, [[2048, 128], [128, 16], [1, 128]]),
              d1["Wv1T"][j], f"wv{sl}",
              writes=[f"wB{sl}"], tag=f"l1v{j}")
        for c in range(2):
            for kd in range(ND):
                b.mm(ps["B"][:, c * 512:(c + 1) * 512],
                     wk[:, kd * 128:(kd + 1) * 128],
                     t["yT"][:, kd * TLOC + c * 512: kd * TLOC + c * 512 + 512],
                     start=(kd == 0), stop=(kd == ND - 1),
                     reads=["yT", f"wA{sl}"], writes=["psB"],
                     tag=f"l1kp{j}_{c}")
            emit_phi_evac(b, t, ps["B"][:, c * 512:(c + 1) * 512], "psB",
                          phik[:, c * 512:(c + 1) * 512], "xrot", 512,
                          bias_col=(t["inprojb_sb"][:, 16 + j:17 + j],
                                    t["c_tmp"][:, j:j + 1], "c_tmp"),
                          tag=f"l1pk{j}_{c}")
        for c in range(2):
            for kd in range(ND):
                b.mm(ps["A2"][:, 0:512],
                     wv[:, kd * 128:(kd + 1) * 128],
                     t["yT"][:, kd * TLOC + c * 512: kd * TLOC + c * 512 + 512],
                     start=(kd == 0), stop=(kd == ND - 1),
                     reads=["yT", f"wB{sl}"], writes=["psA2"],
                     tag=f"l1vp{j}_{c}")
            b.v(lambda e, j=j, c=c: e.tensor_scalar(
                vbuf[:, c * 512:(c + 1) * 512], ps["A2"][:, 0:512],
                t["inprojb_sb"][:, 32 + j:33 + j], None, ALU.add),
                reads=["psA2", "inprojb_sb"], writes=["xrot"],
                tag=f"l1ve{j}_{c}")
        b.v(lambda e, j=j: e.tensor_mul(
            t["big1"][:, j * TLOC:(j + 1) * TLOC], phik, vbuf),
            reads=["xrot"], writes=["big1"], tag=f"l1u{j}")
        b.v(lambda e, j=j: e.reduce_sum(
            t["totc"][:, j:j + 1], t["big1"][:, j * TLOC:(j + 1) * TLOC],
            AX.X), reads=["big1"], writes=["kv_loc"], tag=f"l1t{j}")
    b.dma("Y", d1["ccC_i"][:, :], t["totc"], "dsC", reads=["kv_loc"],
          writes=["ccC_i"], tag="l1to")
    p._declare_sem("ccC", 1)
    b.g(lambda e: e.collective_compute(
        "AllGather", ALU.bypass, replica_groups=PAIRS,
        ins=[d1["ccC_i"].ap().opt()], outs=[d1["ccC_o"].ap().opt()]),
        reads=["ccC_i"], writes=["ccC_o"], sem="ccC", tag="l1cc")
    b.dma("Y", t["carry_sb"], d1["ccC_o"][0:128, :], "dlC",
          reads=["ccC_o"], writes=["kv_loc"], tag="l1ci")
    b.dma("Y", t["hs_sb"], d1["halfsel"][:, :], "dhs", writes=["c_var"],
          tag="l1hs2")
    b.v(lambda e: e.tensor_scalar(t["carry_sb"], t["carry_sb"], t["hs_sb"],
                                  None, ALU.mult),
        reads=["kv_loc", "c_var"], writes=["kv_loc"], tag="l1cs")
    for j in range(ND):
        sl = j % 2
        wq = t[f"wA{sl}"]
        b.dma("G", bass.AP(wq, 0, [[2048, 128], [128, 16], [1, 128]]),
              d1["Wq1T"][j], f"wk{sl}",
              writes=[f"wA{sl}"], tag=f"l1q{j}")
        for c in range(2):
            for kd in range(ND):
                b.mm(ps["B"][:, c * 512:(c + 1) * 512],
                     wq[:, kd * 128:(kd + 1) * 128],
                     t["yT"][:, kd * TLOC + c * 512: kd * TLOC + c * 512 + 512],
                     start=(kd == 0), stop=(kd == ND - 1),
                     reads=["yT", f"wA{sl}"], writes=["psB"],
                     tag=f"l1qp{j}_{c}")
            emit_phi_evac(b, t, ps["B"][:, c * 512:(c + 1) * 512], "psB",
                          phik[:, c * 512:(c + 1) * 512], "xrot", 512,
                          bias_col=(t["inprojb_sb"][:, j:j + 1],
                                    t["c_ssq"][:, j:j + 1], "c_ssq"),
                          tag=f"l1pq{j}_{c}")
        b.v(lambda e, j=j: e.tensor_tensor_scan(
            t["scrA"][:, 0:1024], t["big1"][:, j * TLOC:(j + 1) * TLOC],
            t["big1"][:, j * TLOC:(j + 1) * TLOC],
            t["carry_sb"][:, j:j + 1] if False else t["carry_sb"][:, j:j + 1],
            ALU.add, ALU.bypass),
            reads=["big1", "kv_loc"], writes=["scrA"], tag=f"l1s{j}")
        b.v(lambda e, j=j: e.scalar_tensor_tensor(
            t["big1"][:, j * TLOC:(j + 1) * TLOC], phik,
            0.08838834764831845, t["scrA"][:, 0:1024], ALU.mult, ALU.mult),
            reads=["xrot", "scrA"], writes=["big1"], tag=f"l1c{j}")
    def lhs_s2(kd, ti):
        return t["big1"][:, kd * TLOC + ti * 128: kd * TLOC + ti * 128 + 128]

    def evac_s2(cb, ti, pa, pname):
        sb = t["phiq0"][:, (ti % 2) * 512:(ti % 2) * 512 + 512]
        b.v(lambda e, pa=pa, sb=sb: e.tensor_copy(sb, pa),
            reads=[pname], writes=["phik0"], tag=f"l1se{cb}_{ti}")
        b.dma("Y", b.dram["s_scr"][ti * 128:(ti + 1) * 128,
                                   cb * 512:(cb + 1) * 512], sb, "dso",
              reads=["phik0"], writes=["s_scr"], tag=f"l1ss{cb}_{ti}")

    emit_colblock_out(b, t, ps, d1["Wo1T4"], lhs_s2, ["big1"],
                      d1["bo_row"], evac_s2, "s2")


def emit_l1_ffn(b, d1, t, ps, out_dram, x1_dram):
    NI = 44
    for st in ("hv0", "hv1", "hv2", "hv3", "ho0", "ho1", "hk0", "hk1"):
        b.p.dma_stream(st, mode="slot")
    hdnb = t["ytm"][:, :].bitcast(BF16)
    b.dma("G", t["kv_gb"][:, 0:2048], d1["ds_bcast"][:, :], "ddsb",
          writes=["kv_gb"], tag="ldds")
    b.dma("Y", t["gs_sb"], d1["gates_col"][:, :], "dgs", writes=["kv_loc"],
          tag="ldgs")
    b.dma("Y", t["us_sb"], d1["ups_col"][:, :], "dus", writes=["kv_loc"],
          tag="ldus")
    banks = psum_banks8(ps)
    up_slots = (t["big1"][:, 8192:12288], t["big1"][:, 12288:16384])
    for half in range(2):
        toff = half * 512
        for i2 in range(22):
            sl = i2 % 2
            wg = t[f"wC{sl}"]
            wu = up_slots[sl]
            b.dma("Y", bass.AP(wg, 0, [[4096, 128], [256, 16], [1, 256]]),
                  d1["gateT"][i2], f"ho{sl}",
                  writes=[f"wC{sl}"], tag=f"g{half}_{i2}")
            b.dma("Y", bass.AP(t["big1"], 8192 + sl * 4096,
                               [[ND * TLOC, 128], [256, 16], [1, 256]]),
                  d1["upT"][i2], f"hk{sl}",
                  writes=["big1"], tag=f"u{half}_{i2}")
            for io in range(2):
                i = i2 * 2 + io
                pg, pgn = banks[i % 4]
                pu, pun = banks[4 + i % 4]
                for kd in range(ND):
                    b.mm(pg, wg[:, kd * 256 + io * 128: kd * 256 + io * 128 + 128],
                         t["yT"][:, kd * TLOC + toff: kd * TLOC + toff + 512],
                         start=(kd == 0), stop=(kd == ND - 1),
                         reads=["yT", f"wC{sl}"], writes=[pgn],
                         tag=f"gm{half}_{i}")
                b.s(lambda e, i=i, io=io, pg=pg: e.activation(
                    t["scrA"][:, io * 512:io * 512 + 512], pg, AF.Silu,
                    scale=t["gs_sb"][:, i:i + 1]),
                    reads=[pgn, "kv_loc"], writes=["scrA"],
                    tag=f"gsi{half}_{i}")
                for kd in range(ND):
                    b.mm(pu, bass.AP(t["big1"],
                                     8192 + sl * 4096 + kd * 256 + io * 128,
                                     [[ND * TLOC, 128], [1, 128]]),
                         t["yT"][:, kd * TLOC + toff: kd * TLOC + toff + 512],
                         start=(kd == 0), stop=(kd == ND - 1),
                         reads=["yT", "big1"], writes=[pun],
                         tag=f"um{half}_{i}")
                b.v(lambda e, i=i, io=io, pu=pu: e.scalar_tensor_tensor(
                    hdnb[:, i * 512:(i + 1) * 512], pu, t["us_sb"][:, i:i + 1],
                    t["scrA"][:, io * 512:io * 512 + 512],
                    ALU.mult, ALU.mult),
                    reads=[pun, "kv_loc", "scrA"], writes=["ytm"],
                    tag=f"hd{half}_{i}")
        for cp in range(2):
            for kd in range(NI):
                sl = kd % 4
                wnm = ("wB0", "wB1", "wA0", "wA1")[sl]
                wdn = t[wnm]
                b.dma("Y", wdn[:, 0:1024], d1["downT"][cp, kd], f"hv{sl}",
                      writes=[wnm], tag=f"dn{half}_{cp}_{kd}")
                for tt in range(4):
                    for ci in range(2):
                        pa, pname = banks[tt * 2 + ci]
                        b.mm(pa, hdnb[:, kd * 512 + tt * 128:
                                      kd * 512 + tt * 128 + 128],
                             wdn[:, ci * 512:(ci + 1) * 512],
                             start=(kd == 0), stop=(kd == NI - 1),
                             inc=(ci == 1),
                             reads=["ytm", wnm], writes=[pname],
                             tag=f"dm{half}_{cp}_{kd}_{tt}")
            for tt in range(4):
                ti = half * 4 + tt
                for ci in range(2):
                    c = cp * 2 + ci
                    pa, pname = banks[tt * 2 + ci]
                    xr = t["xrot"][:, (c % 2) * 1024:(c % 2) * 1024 + 512]
                    xrn = f"xrot{c % 2}"
                    b.dma("Y", xr,
                          x1_dram[ti * 128:(ti + 1) * 128,
                                  c * 512:(c + 1) * 512], f"dx{c % 2}",
                          writes=[xrn], reads=["x1_out"],
                          tag=f"dcx{ti}_{c}")
                    sb2 = t["phiq1"][:, (c % 2) * 512:(c % 2) * 512 + 512]
                    sbn = f"phik1{c % 2}"
                    b.dma("Y", sb2,
                          b.dram["s_scr"][ti * 128:(ti + 1) * 128,
                                          c * 512:(c + 1) * 512],
                          f"dsi{c % 2}", reads=["s_scr"], writes=[sbn],
                          tag=f"dc_s{ti}_{c}")
                    b.v(lambda e, c=c, pa=pa: e.tensor_tensor(
                        t["scrA"][:, 0:512], pa,
                        t["kv_gb"][:, c * 512:(c + 1) * 512], ALU.mult),
                        reads=[pname, "kv_gb"], writes=["scrA"],
                        tag=f"dc_m{ti}_{c}")
                    b.s(lambda e: e.activation(
                        t["scrA"][:, 512:1024], t["scrA"][:, 0:512], AF.Silu),
                        reads=["scrA"], writes=["scrA"], tag=f"dc_si{ti}_{c}")
                    b.v(lambda e, sb2=sb2: e.tensor_tensor(
                        t["scrA"][:, 0:512], t["scrA"][:, 512:1024], sb2,
                        ALU.mult),
                        reads=["scrA", sbn], writes=["scrA"],
                        tag=f"dc_g{ti}_{c}")
                    b.v(lambda e, xr=xr: e.tensor_add(xr, xr,
                                                      t["scrA"][:, 0:512]),
                        reads=["scrA", xrn], writes=[xrn],
                        tag=f"dc_a{ti}_{c}")
                    b.dma("Y", out_dram[ti * 128:(ti + 1) * 128,
                                        c * 512:(c + 1) * 512], xr, "dout",
                          reads=[xrn], writes=["out"], tag=f"dco{ti}_{c}")


def build_stageD():
    nc = bass.Bass("TRN2", target_bir_lowering=False, debug=False)
    b = Builder(nc)
    x1 = b.din("x1", (TLOC, D))
    d1 = declare_l1(b)
    out = b.dout("out", (TLOC, D), F32)
    b.dram["s_scr"] = b.dint("s_scr", (TLOC, D), BF16)
    d = {"identb": b.din("identb", (128, 128), BF16),
         "ones_row": b.din("ones_row", (1, D), BF16)}
    t = alloc_common(b)
    alloc_ffn(b, t)
    t["competeb_sb"] = b.sbuf("competeb_sb", (16, 1), F32)
    t["inprojb_sb"] = b.sbuf("inprojb_sb", (128, 48), F32)
    t["prew_sb"] = t["inprojb_sb"][:, 16:32]
    t["totc"] = t["kv_loc"][:, 0:16]
    t["carry_sb"] = t["kv_loc"][:, 0:16]
    t["hs_sb"] = t["c_var"][:, 0:1]
    t["gs_sb"] = t["kv_loc"][:, 16:60]
    t["us_sb"] = t["kv_loc"][:, 60:104]
    ps = alloc_psum(b)
    for st in ("dx", "dmisc", "dmisc2"):
        b.p.dma_stream(st, mode="slot")
    b.p.dma_stream("dout", mode="barrier")
    for st in ("dws", "dpw", "dpwb", "dcb", "dib", "dif", "drow", "ds2s",
               "dsA", "dlA", "dsz", "dsT", "dmg", "dsU", "dum", "dcptF",
               "dso", "dsi", "dsC", "dlC", "dhs", "ddsb", "dgs", "dus",
               "dx0", "dx1", "dsi0", "dsi1",
               "wk0", "wk1", "wv0", "wv1", "wq0", "wq1", "wo0", "wo1"):
        b.p.dma_stream(st, mode="slot")
    load_consts(b, d, t)
    emit_ln(b, d, t, x1, t["ytm"], tag="ln2")
    emit_tp_plain(b, d, t, ps, tag="tpp")
    emit_l1_attn(b, d1, t, ps)
    emit_l1_ffn(b, d1, t, ps, out, x1)
    b.p.op("Y", lambda e: e.nop(), reads=["out"], writes=(), inc=False,
           tag="fin")
    return finalize(b, [])


def build_fused():
    nc = bass.Bass("TRN2", target_bir_lowering=False, debug=False)
    b = Builder(nc)
    d = declare_l0(b)
    d["inprojb_row"] = b.din("inprojb_row", (1, 3 * D))
    d["postw_bcast"] = b.din("postw_bcast", (128, D))
    d["identf"] = b.din("identf_in", (128, 128))
    d1 = declare_l1(b)
    out = b.dout("out", (TLOC, D), F32)
    d["s_scr"] = b.dint("s_scr", (TLOC, D), BF16)
    x1s = b.dint("x1_scr", (TLOC, D), F32)
    t = alloc_common(b)
    alloc_ffn(b, t)
    t["competeb_sb"] = b.sbuf("competeb_sb", (16, 1), F32)
    t["inprojb_sb"] = b.sbuf("inprojb_sb", (128, 48), F32)
    t["prew_sb"] = t["inprojb_sb"][:, 16:32]
    t["totc"] = t["kv_loc"][:, 0:16]
    t["carry_sb"] = t["kv_loc"][:, 0:16]
    t["hs_sb"] = t["c_var"][:, 0:1]
    t["gs_sb"] = t["kv_loc"][:, 16:60]
    t["us_sb"] = t["kv_loc"][:, 60:104]
    ps = alloc_psum(b)
    for st in ("dx", "dmisc", "dmisc2"):
        b.p.dma_stream(st, mode="slot")
    b.p.dma_stream("dout", mode="barrier")
    b.p.dma_stream("dx1", mode="slot")
    for st in ("dws", "dpw", "dpwb", "dcb", "dib", "dif", "drow", "ds2s",
               "dsA", "dlA", "dsz", "dsT", "dmg", "dsU", "dum", "dcptF",
               "dso", "dsi", "dsC", "dlC", "dhs", "ddsb", "dgs", "dus",
               "dx0", "dx1", "dsi0", "dsi1"):
        b.p.dma_stream(st, mode="slot")
    load_consts(b, d, t)
    emit_ln(b, d, t, d["x"], t["ytm"], tag="ln1")
    emit_cpt_prep(b, d, t)
    emit_tp_logits(b, d, t, ps, 0)
    emit_l0_attn_A(b, d, t, ps)
    emit_l0_ffn_prep(b, d, t)
    emit_ffn_topk(b, d, t, ps, 0)
    emit_l0_attn_BC(b, d, t, ps)
    emit_ffn_u(b, d, t, ps, 0)
    emit_ffn_q(b, d, t, ps, 0)
    emit_ffn_written(b, d, t, ps, 0)
    emit_ffn_scores(b, d, t, ps, 0)
    emit_ffn_xb(b, d, t, ps, 0)
    emit_tp_logits(b, d, t, ps, 1)
    emit_l0_ffn_iter(b, d, t, ps, 1)
    emit_l0_combine(b, d, t, x1s, fuse_ln=True)
    emit_tp_plain(b, d, t, ps, tag="tpp")
    emit_l1_attn(b, d1, t, ps)
    emit_l1_ffn(b, d1, t, ps, out, x1s)
    b.p.op("Y", lambda e: e.nop(), reads=["out"], writes=(), inc=False,
           tag="fin")
    return finalize(b, [])





_NC_CACHE = {}


def _get_nc(name, fn):
    if name not in _NC_CACHE:
        _NC_CACHE[name] = fn()
    return _NC_CACHE[name]


def kernel(**inputs):
    from concourse.bass_utils import run_bass_kernel_spmd
    ins = inputs
    _ = (ins.get("L0_attn_cos"), ins.get("L0_attn_sin"),
         ins.get("L0_ffn_ig_w"), ins.get("L0_ffn_ig_b"))
    P = host_prep_l0(ins)
    P["inprojb_row"] = np.ascontiguousarray(ins["L0_ffn_in_proj_b"][None, :])
    P["postw_bcast"] = np.ascontiguousarray(
        np.broadcast_to(ins["L0_ffn_norm_post_w"][None, :], (128, 2048)))
    P["identf_in"] = np.eye(128, dtype=np.float32)
    P.update(host_prep_l1(ins))
    shards = shard_inputs(np.asarray(ins["L0_attn_x"], np.float32))
    names = ("WqT", "WkT4", "WvT4", "WoT4", "inprojT", "inprojQ", "writeT",
             "outpT4", "competeT", "prew_col", "postw_col", "postw_row",
             "inprojb_col", "competeb_col", "writeb_row", "outpb_row",
             "workspace", "identb", "ones_row", "inprojb_row", "postw_bcast",
             "identf_in", "Wq1T", "Wk1T", "Wv1T", "Wo1T4", "gateT", "upT",
             "downT", "l1b_col", "bo_row", "gates_col", "ups_col", "ds_bcast")
    in_maps = []
    for c in range(8):
        m = {k: P[k] for k in names}
        m["x"] = shards[c]
        m["halfsel"] = np.full((128, 1), float(c % 2), np.float32)
        in_maps.append(m)
    nc = _get_nc("fused", build_fused)
    r = run_bass_kernel_spmd(nc, in_maps, core_ids=list(range(8)))
    out = np.zeros((4, 2048, 2048), np.float32)
    for c in range(8):
        b_, h = c // 2, c % 2
        out[b_, h * 1024:(h + 1) * 1024, :] = r.results[c]["out"]
    return out
